# version 14
# baseline (speedup 1.0000x reference)
import numpy as np
import ml_dtypes
import concourse.bass as bass
import concourse.mybir as mybir
from concourse.bass_utils import run_bass_kernel_spmd

F32 = mybir.dt.float32
BF16 = mybir.dt.bfloat16
AF = mybir.ActivationFunctionType
ALU = mybir.AluOpType
AX = mybir.AxisListType
BF = ml_dtypes.bfloat16

ENGINES = ("tensor", "vector", "scalar", "gpsimd", "sync")


class _Op:
    __slots__ = ("engine", "fn", "deps", "idx", "dma_sem", "value", "marked", "label")

    def __init__(self, engine, fn, deps, idx, dma_sem=None):
        self.engine = engine
        self.fn = fn
        self.deps = deps
        self.idx = idx
        self.dma_sem = dma_sem
        self.value = None
        self.marked = dma_sem is not None


class DmaSem:
    __slots__ = ("name", "sem", "count", "group")

    def __init__(self, name, group=False):
        self.name = name
        self.sem = None
        self.count = 0
        self.group = group


class Sched:
    def __init__(self, nc):
        self.nc = nc
        self.ops = {e: [] for e in ENGINES}
        self.all_ops = []
        self.sems = {}
        self.dma_sems = []

    def new_dma_sem(self, name, group=False):
        d = DmaSem(name, group)
        self.dma_sems.append(d)
        return d

    def setup_sems(self, ctx):
        for e in ENGINES:
            self.sems[e] = ctx.enter_context(self.nc.semaphore(f"chain_{e}"))
        for d in self.dma_sems:
            d.sem = ctx.enter_context(self.nc.semaphore(f"dma_{d.name}"))

    def add(self, engine, fn, deps=()):
        deps = tuple(d for d in deps if d is not None)
        op = _Op(engine, fn, deps, len(self.ops[engine]))
        self.ops[engine].append(op)
        self.all_ops.append(op)
        return op

    def dma(self, engine, out, in_, dsem, deps=()):
        deps = tuple(d for d in deps if d is not None)

        def fn(eng):
            return eng.dma_start(out=out, in_=in_)

        op = _Op(engine, fn, deps, len(self.ops[engine]), dma_sem=dsem)
        dsem.count += 16
        op.value = dsem.count
        self.ops[engine].append(op)
        self.all_ops.append(op)
        return op

    def barrier_token(self, engine):
        ops = self.ops[engine]
        return ops[-1] if ops else None

    def all_tokens(self):
        return [self.barrier_token(e) for e in ENGINES if self.ops[e]]

    def emit(self):
        for op in self.all_ops:
            for d in op.deps:
                d.marked = True
        counts = {e: 0 for e in ENGINES}
        for e in ENGINES:
            for op in self.ops[e]:
                if op.marked and op.dma_sem is None:
                    counts[e] += 1
                    op.value = counts[e]
        self.final_counts = dict(counts)
        self.inst_map = {}
        nc, sems, ops = self.nc, self.sems, self.ops
        with nc.Block() as block:
            def mk(e):
                def body(eng):
                    seen = {}
                    for op in ops[e]:
                        for d in op.deps:
                            if d.value is None:
                                raise RuntimeError(f"unmarked dep on {e} #{op.idx}")
                            if d.dma_sem is not None:
                                key, sem = d.dma_sem, d.dma_sem.sem
                                val = d.dma_sem.count if d.dma_sem.group else d.value
                            else:
                                key, sem, val = d.engine, sems[d.engine], d.value
                            if val > seen.get(key, 0):
                                eng.wait_ge(sem, val)
                                seen[key] = val
                        inst = op.fn(eng)
                        try:
                            self.inst_map[inst.ins.name] = (e, op.idx, getattr(op, 'label', None))
                        except Exception:
                            pass
                        if op.dma_sem is not None:
                            inst.then_inc(op.dma_sem.sem, 16)
                        elif op.marked:
                            inst.then_inc(sems[e], 1)
                    return None
                return body
            for e in ENGINES:
                getattr(block, e)(mk(e))


HID = 1024
NCH = 8
HALO = 1152
OW = 4096
TEXT = HALO + OW
NT_EXT = TEXT // 128
NT_REST = 4096 // 128
HEADS_A = 16
DA = 64
HEADS_F = 8
DF = 64
WIN = 512
INNER = 512
HPAD = 2816
H2 = 2 * HPAD
MEGA = 1024

L_CFG = [
    dict(blk0_col=128, nblk=10, mlp_col0=512, mlp_ntok=4736),
    dict(blk0_col=1152, nblk=8, mlp_col0=1152, mlp_ntok=4096),
]

NEG = -1.0e30
DBG_TAP = False
P2STOP = 99


def _perm_head_major():
    return np.array([d * 16 + h for h in range(16) for d in range(64)], dtype=np.int64)


def _rotary_tables():
    D = 64
    half = np.arange(0, D, 2, dtype=np.float64)
    inv_freq = 1.0 / 10000.0 ** (half / D)
    t = np.arange(8192, dtype=np.float64)
    freqs = t[:, None] * inv_freq[None, :]
    freqs = np.concatenate([freqs, freqs], axis=1)
    base = (half + 0.4 * D) / (1.4 * D)
    power = (t - 4096.0) / float(WIN // 2)
    scale = base[None, :] ** power[:, None]
    scale = np.concatenate([scale, scale], axis=1)
    cos = np.cos(freqs)
    sin = np.sin(freqs)
    cs = cos * scale
    csk = cos / scale
    sgn = np.concatenate([-np.ones(32), np.ones(32)])[None, :]
    ssn = sin * scale * sgn
    sskn = sin / scale * sgn
    return (cs.astype(np.float32), ssn.astype(np.float32),
            csk.astype(np.float32), sskn.astype(np.float32))


def _local_masks():
    qq = np.arange(128)[:, None]
    kk = np.arange(128)[None, :]
    mq_lo = (kk < qq).astype(np.float32)
    mq_hi = (kk > qq).astype(np.float32)
    return mq_lo, mq_hi, mq_lo.T.copy(), mq_hi.T.copy()


def make_core_inputs(inputs, core):
    b, h = core // 2, core % 2
    s = 4096 * h
    perm = _perm_head_major()
    x = np.asarray(inputs["L0_attn_x"])
    xT = np.ascontiguousarray(x[b].T)

    x_extT = np.zeros((HID, TEXT), np.float32)
    lo = s - HALO
    src_lo = max(lo, 0)
    x_extT[:, src_lo - lo:] = xT[:, src_lo:s + OW]

    x_restT = np.zeros((HID, 4096), np.float32)
    if h == 0:
        x_restT[:, :] = xT[:, 4096:8192]
        rest_valid = 4096
    else:
        x_restT[:, :2944] = xT[:, 0:2944]
        rest_valid = 2944

    ext_valid = np.zeros(TEXT, np.float32)
    ext_valid[HALO - min(HALO, s):] = 1.0
    if h == 1:
        ext_valid[:] = 1.0
    mask_ext_col = ext_valid.reshape(NT_EXT, 128).T.copy()
    rv = np.zeros(4096, np.float32)
    rv[:rest_valid] = 1.0
    mask_rest_col = rv.reshape(NT_REST, 128).T.copy()

    mrow = np.where(ext_valid > 0, 0.0, NEG).astype(np.float32)
    maskrow_col = mrow.reshape(NT_EXT, 128).T.copy()

    d = {
        "x_extT": x_extT, "x_restT": x_restT,
        "mask_ext_col": mask_ext_col, "mask_rest_col": mask_rest_col,
        "maskrow_col": maskrow_col,
    }

    d["Wh"] = np.ascontiguousarray(np.asarray(inputs["L0_attn_Wh"])[:, perm])
    d["Wq"] = np.ascontiguousarray(np.asarray(inputs["L0_attn_Wq"])[:, perm])
    d["Wk"] = np.ascontiguousarray(np.asarray(inputs["L0_attn_Wk"])[:, perm])
    d["Wv"] = np.ascontiguousarray(np.asarray(inputs["L0_attn_Wv"])[:, perm])
    d["Wo"] = np.ascontiguousarray(np.asarray(inputs["L0_attn_Wo"])[perm, :])
    d["bh_row"] = np.asarray(inputs["L0_attn_bh"])[perm][None, :].copy()
    d["beta_row"] = np.asarray(inputs["L0_attn_beta"])[perm][None, :].copy()
    d["bk_row"] = np.asarray(inputs["L0_attn_bk"])[perm][None, :].copy()
    d["bv_row"] = np.asarray(inputs["L0_attn_bv"])[perm][None, :].copy()
    d["bq_col"] = np.ascontiguousarray(
        np.asarray(inputs["L0_attn_bq"])[perm].reshape(NCH, 128).T)
    d["bo_col"] = np.ascontiguousarray(
        np.asarray(inputs["L0_attn_bo"]).reshape(NCH, 128).T)
    d["lnw_rep"] = np.ascontiguousarray(
        np.broadcast_to(np.asarray(inputs["L0_attn_ln_w"])[perm], (128, HID)))
    d["lnb_rep"] = np.ascontiguousarray(
        np.broadcast_to(np.asarray(inputs["L0_attn_ln_b"])[perm], (128, HID)))

    w1 = np.asarray(inputs["L0_ffn_w1"])
    b1 = np.asarray(inputs["L0_ffn_b1"])
    w2 = np.asarray(inputs["L0_ffn_w2"])
    for l in range(2):
        d[f"wqkv{l}"] = np.ascontiguousarray(np.asarray(inputs["L0_ffn_wqkv"])[l])
        d[f"wo{l}"] = np.ascontiguousarray(np.asarray(inputs["L0_ffn_wo"])[l])
        w1p = np.zeros((HID, H2), np.float32)
        w1p[:, 0:2730] = w1[l][:, 0:2730]
        w1p[:, HPAD:HPAD + 2730] = w1[l][:, 2730:5460]
        d[f"w1{l}"] = w1p
        b1p = np.zeros((1, H2), np.float32)
        b1p[0, 0:2730] = b1[l][0:2730]
        b1p[0, HPAD:HPAD + 2730] = b1[l][2730:5460]
        d[f"b1row{l}"] = b1p
        w2p = np.zeros((HPAD, HID), np.float32)
        w2p[0:2730, :] = w2[l]
        d[f"w2{l}"] = w2p
        d[f"lg_col{l}"] = np.ascontiguousarray(
            np.asarray(inputs["L0_ffn_ln_g"])[l].reshape(NCH, 128).T)
        d[f"lb_col{l}"] = np.ascontiguousarray(
            np.asarray(inputs["L0_ffn_ln_b"])[l].reshape(NCH, 128).T)
        d[f"b2_col{l}"] = np.ascontiguousarray(
            np.asarray(inputs["L0_ffn_b2"])[l].reshape(NCH, 128).T)
        d[f"qs_rep{l}"] = np.ascontiguousarray(np.broadcast_to(
            np.tile(np.asarray(inputs["L0_ffn_q_scale"])[l], HEADS_F), (128, INNER)))
        d[f"ks_rep{l}"] = np.ascontiguousarray(np.broadcast_to(
            np.tile(np.asarray(inputs["L0_ffn_k_scale"])[l], HEADS_F), (128, INNER)))

    cs, ssn, csk, sskn = _rotary_tables()
    pos = np.clip(np.arange(s - HALO, s + OW), 0, 8191)
    d["tab"] = np.ascontiguousarray(
        np.stack([cs[pos], ssn[pos], csk[pos], sskn[pos]], axis=1)
    ).astype(np.float32)
    mq_lo, mq_hi, mk_lo, mk_hi = _local_masks()
    d["mq_lo"], d["mq_hi"], d["mk_lo"], d["mk_hi"] = mq_lo, mq_hi, mk_lo, mk_hi
    d["ident"] = np.eye(128, dtype=np.float32)
    o2 = np.zeros((2, MEGA), np.float32)
    o2[1, :] = 1.0
    d["ones2row"] = o2
    d["negI"] = (NEG * np.eye(128)).astype(np.float32)
    hs = np.zeros((8, 8, 128), np.float32)
    for hh in range(8):
        hs[hh, hh, :] = 1.0
    d["hsel"] = hs
    e4 = np.zeros((1, 16), np.float32)
    for c in range(4):
        e4[0, 4 * c + c] = 1.0
    d["e4"] = e4
    d["halo01"] = np.full((128, 1), 1.0 if s > 0 else 0.0, np.float32)
    return d



from contextlib import ExitStack


class KB:

    def __init__(self, nc, ctx):
        self.nc = nc
        self.ctx = ctx
        self.s = Sched(nc)
        self.inp = {}
        self.sb = {}

    def din(self, name, shape):
        ap = self.nc.dram_tensor(name, list(shape), F32, kind="ExternalInput").ap()
        self.inp[name] = ap
        return ap

    def sbuf(self, name, shape, dtype=BF16, ctx=None):
        return (ctx or self.ctx).enter_context(
            self.nc.sbuf_tensor(name, list(shape), dtype))

    def psum(self, name, shape, dtype=F32, ctx=None):
        return (ctx or self.ctx).enter_context(
            self.nc.psum_tensor(name, list(shape), dtype))

    def gate(self):
        return [t for t in (self.s.barrier_token(e) for e in ENGINES) if t is not None]

    def build_io(self):
        d = self.din
        d("x_extT", (HID, TEXT))
        d("x_restT", (HID, 4096))
        d("mask_ext_col", (128, NT_EXT))
        d("mask_rest_col", (128, NT_REST))
        d("maskrow_col", (128, NT_EXT))
        for w in ("Wh", "Wq", "Wk", "Wv", "Wo"):
            d(w, (HID, HID))
        for r in ("bh_row", "beta_row", "bk_row", "bv_row"):
            d(r, (1, HID))
        d("bq_col", (128, NCH))
        d("bo_col", (128, NCH))
        d("lnw_rep", (128, HID))
        d("lnb_rep", (128, HID))
        for l in range(2):
            d(f"wqkv{l}", (HID, 3 * INNER))
            d(f"wo{l}", (INNER, HID))
            d(f"w1{l}", (HID, H2))
            d(f"b1row{l}", (1, H2))
            d(f"w2{l}", (HPAD, HID))
            d(f"lg_col{l}", (128, NCH))
            d(f"lb_col{l}", (128, NCH))
            d(f"b2_col{l}", (128, NCH))
            d(f"qs_rep{l}", (128, INNER))
            d(f"ks_rep{l}", (128, INNER))
        d("tab", (TEXT, 4, DF))
        d("ones2row", (2, MEGA))
        d("halo01", (128, 1))
        d("e4", (1, 16))
        d("hsel", (8, 8, 128))
        for m in ("mq_lo", "mq_hi", "mk_lo", "mk_hi", "ident", "negI"):
            d(m, (128, 128))
        self.outT = self.nc.dram_tensor("outT", [HID, OW], F32, kind="ExternalOutput").ap()
        self.dbg_qf = self.nc.dram_tensor("dbg_qf", [128, 4, 512], F32, kind="ExternalOutput").ap()
        self.dbg_kf = self.nc.dram_tensor("dbg_kf", [128, 4, 512], F32, kind="ExternalOutput").ap()
        self.dbg_vt = self.nc.dram_tensor("dbg_vt", [128, 4, 520], F32, kind="ExternalOutput").ap()
        self.dbg_m = self.nc.dram_tensor("dbg_m", [8, TEXT], F32, kind="ExternalOutput").ap()
        self.dbg_o = self.nc.dram_tensor("dbg_o", [128, 4, 512], F32, kind="ExternalOutput").ap()
        self.dbg_kf2 = self.nc.dram_tensor("dbg_kf2", [128, 4, 512], F32, kind="ExternalOutput").ap()
        self.dbg_vt2 = self.nc.dram_tensor("dbg_vt2", [128, 4, 520], F32, kind="ExternalOutput").ap()
        self.x1b = self.nc.dram_tensor("x1b", [128, NCH, OW], BF16, kind="Internal").ap()
        self.w1bf = [self.nc.dram_tensor(f"w1bf{l}", [HID, H2], BF16, kind="Internal").ap()
                     for l in range(2)]
        self.w2bf = [self.nc.dram_tensor(f"w2bf{l}", [HPAD, HID], BF16, kind="Internal").ap()
                     for l in range(2)]

    def build_consts(self):
        s, sb, inp = self.s, self.sb, self.inp
        for n, shp, dt in [
            ("ones_row", (1, 640), BF16), ("ones_col", (128, 1), BF16),
            ("identF", (128, 128), F32),
            ("negI", (128, 128), BF16),
            ("mk_lo", (128, 128), BF16), ("mk_hi", (128, 128), BF16),
            ("eps", (128, 1), F32), ("maskrow", (128, NT_EXT), F32),
            ("zrow", (1, 512), BF16),
            ("X", (128, NCH, TEXT), BF16),
            ("lg_col0", (128, NCH), BF16), ("lb_col0", (128, NCH), BF16),
            ("lg_col1", (128, NCH), BF16), ("lb_col1", (128, NCH), BF16),
            ("b2_col0", (128, NCH), F32), ("b2_col1", (128, NCH), F32),
            ("halo01", (128, 1), F32), ("e4", (1, 16), F32),
        ]:
            sb[n] = self.sbuf(n + "_g", shp, dt)

        dco = s.new_dma_sem("consts", group=True)
        g = "gpsimd"
        for m in ("negI", "mk_lo", "mk_hi"):
            s.dma(g, sb[m][:, :], inp[m][:, :], dco)
        s.dma(g, sb["identF"][:, :], inp["ident"][:, :], dco)
        s.dma(g, sb["maskrow"][:, :], inp["maskrow_col"][:, :], dco)
        s.dma(g, sb["halo01"][:, :], inp["halo01"][:, :], dco)
        s.dma(g, sb["e4"][:, :], inp["e4"][:, :], dco)
        for l in range(2):
            s.dma(g, sb[f"lg_col{l}"][:, :], inp[f"lg_col{l}"][:, :], dco)
            s.dma(g, sb[f"lb_col{l}"][:, :], inp[f"lb_col{l}"][:, :], dco)
            s.dma(g, sb[f"b2_col{l}"][:, :], inp[f"b2_col{l}"][:, :], dco)
        t_cd = s.ops[g][-1]
        s.add(g, lambda gg: gg.memset(sb["ones_row"][0:1, 0:640], 1.0), [t_cd])
        s.add(g, lambda gg: gg.memset(sb["ones_col"][:, :], 1.0))
        s.add(g, lambda gg: gg.memset(sb["zrow"][:, :], 0.0))
        t_ms = s.add(g, lambda gg: gg.memset(sb["eps"][:, :], 1e-5))
        self.d_consts = dco
        self.t_consts = t_ms

        dX = s.new_dma_sem("xload", group=True)
        xin = inp["x_extT"].rearrange("(c p) t -> p c t", p=128)
        tx = None
        for i in range(4):
            tx = s.dma(g, sb["X"][:, 2 * i:2 * i + 2, :], xin[:, 2 * i:2 * i + 2, :], dX)
        self.t_X = tx

        dwc = s.new_dma_sem("wcast", group=True)
        tw = None
        for l in range(2):
            for i in range(4):
                tw = s.dma(g, self.w1bf[l][:, i * (H2 // 4):(i + 1) * (H2 // 4)],
                           inp[f"w1{l}"][:, i * (H2 // 4):(i + 1) * (H2 // 4)], dwc)
            for i in range(4):
                tw = s.dma(g, self.w2bf[l][i * (HPAD // 4):(i + 1) * (HPAD // 4), :],
                           inp[f"w2{l}"][i * (HPAD // 4):(i + 1) * (HPAD // 4), :], dwc)
        self.t_wcast = tw

    def build_p1(self):
        s, nc, sb, inp = self.s, self.nc, self.sb, self.inp
        sb["kv_sb"] = self.sbuf("kv_sb", (128, 1040), BF16, self.es_kv)
        sb["kv2"] = self.sbuf("kv2", (128, 1040), BF16, self.es_kv)
        with ExitStack() as es:
            Wk = self.sbuf("WkSb", (128, NCH, HID), BF16, es)
            Wv = self.sbuf("WvSb", (128, NCH, HID), BF16, es)
            xr = [self.sbuf(f"xr{i}", (128, NCH, 128), BF16, es) for i in range(2)]
            ktile = self.sbuf("ktile", (128, HID), BF16, es)
            brow1 = self.sbuf("brow1", (1, 2 * HID), BF16, es)
            vtile = self.sbuf("vtile", (128, 16 * 65), BF16, es)
            sb["mask_ext"] = self.sbuf("mask_ext_s", (128, NT_EXT), F32, es)
            sb["mask_rest"] = self.sbuf("mask_rest_s", (128, NT_REST), F32, es)
            kps = [self.psum(f"kps{i}", (128, 512), F32, es) for i in range(2)]
            vps = [self.psum(f"vps{i}", (128, 512), F32, es) for i in range(2)]
            kvp = [self.psum(f"kvp{i}", (64, 512), F32, es) for i in range(3)]

            dW = s.new_dma_sem("p1w", group=True)
            s.dma("gpsimd", sb["mask_ext"][:, :], inp["mask_ext_col"][:, :], dW)
            s.dma("gpsimd", sb["mask_rest"][:, :], inp["mask_rest_col"][:, :], dW)
            s.dma("gpsimd", brow1[0:1, 0:HID], inp["bk_row"][:, :], dW)
            s.dma("gpsimd", brow1[0:1, HID:2 * HID], inp["bv_row"][:, :], dW)
            wk_in = inp["Wk"].rearrange("(c p) o -> p c o", p=128)
            wv_in = inp["Wv"].rearrange("(c p) o -> p c o", p=128)
            tw = None
            for i in range(4):
                s.dma("gpsimd", Wk[:, 2 * i:2 * i + 2, :], wk_in[:, 2 * i:2 * i + 2, :], dW)
                tw = s.dma("gpsimd", Wv[:, 2 * i:2 * i + 2, :], wv_in[:, 2 * i:2 * i + 2, :], dW)
            vones = s.add("vector", lambda v: v.memset(
                bass.AP(vtile, 64, [[16 * 65, 128], [65, 16]]), 1.0), [])

            xrest_in = inp["x_restT"].rearrange("(c p) t -> p c t", p=128)
            dxr = [s.new_dma_sem(f"xr{i}") for i in range(2)]

            t_kevac = t_vevac = t_kmask = t_vcopy = t_kvmm = None
            t_xr_free = [None, None]
            started = {}
            n_total = NT_EXT + NT_REST

            for it in range(n_total):
                is_ext = it < NT_EXT
                ti = it if is_ext else it - NT_EXT
                if is_ext:
                    def xsl(c, ti=ti):
                        return sb["X"][:, c, 128 * ti:128 * (ti + 1)]
                    xdep = self.t_X
                    mcol = sb["mask_ext"][:, ti:ti + 1]
                else:
                    slot = ti % 2
                    ld = s.dma("gpsimd", xr[slot][:, :, :],
                               xrest_in[:, :, 128 * ti:128 * (ti + 1)], dxr[slot],
                               deps=[t_xr_free[slot]])

                    def xsl(c, slot=slot):
                        return xr[slot][:, c, :]
                    xdep = ld
                    mcol = sb["mask_rest"][:, ti:ti + 1]

                mm = None
                for half in range(2):
                    for c in range(NCH):
                        deps = [xdep, tw, t_kevac, t_kmask, self.t_consts] if (half == 0 and c == 0) else []
                        mm = s.add("tensor", lambda t, c=c, half=half, xs=xsl, st=(c == 0): t.matmul(
                            kps[half][:, :], xs(c), Wk[:, c, 512 * half:512 * (half + 1)],
                            start=st, stop=False), deps)
                    mm = s.add("tensor", lambda t, half=half: t.matmul(
                        kps[half][:, :], sb["ones_row"][0:1, 0:128],
                        brow1[0:1, 512 * half:512 * (half + 1)],
                        start=False, stop=True), [tw, self.t_consts] if it == 0 else [])
                t_kmm = mm
                for half in range(2):
                    for c in range(NCH):
                        deps = [t_vevac] if (half == 0 and c == 0) else []
                        mm = s.add("tensor", lambda t, c=c, half=half, xs=xsl, st=(c == 0): t.matmul(
                            vps[half][:, :], xs(c), Wv[:, c, 512 * half:512 * (half + 1)],
                            start=st, stop=False), deps)
                    mm = s.add("tensor", lambda t, half=half: t.matmul(
                        vps[half][:, :], sb["ones_row"][0:1, 0:128],
                        brow1[0:1, HID + 512 * half:HID + 512 * (half + 1)],
                        start=False, stop=True), [])
                t_vmm = mm
                if not is_ext:
                    t_xr_free[ti % 2] = t_vmm

                t1 = s.add("scalar", lambda sc: sc.activation(
                    ktile[:, 0:512], kps[0][:, :], AF.Sigmoid), [t_kmm, t_kvmm])
                t_kevac = s.add("scalar", lambda sc: sc.activation(
                    ktile[:, 512:1024], kps[1][:, :], AF.Sigmoid), [t1])
                t_kmask = s.add("vector", lambda v, mcol=mcol: v.tensor_scalar(
                    ktile[:, :], ktile[:, :], mcol, None, ALU.mult),
                    [t_kevac, self.t_consts])
                t1 = s.add("vector", lambda v: v.tensor_copy(
                    bass.AP(vtile, 0, [[16 * 65, 128], [65, 8], [1, 64]]),
                    vps[0][:, :].rearrange("p (h m) -> p h m", m=64)),
                    [t_vmm, t_kvmm, vones])
                t_vcopy = s.add("vector", lambda v: v.tensor_copy(
                    bass.AP(vtile, 8 * 65, [[16 * 65, 128], [65, 8], [1, 64]]),
                    vps[1][:, :].rearrange("p (h m) -> p h m", m=64)), [t1])
                t_vevac = t_vcopy

                for hd in range(16):
                    piece = 0 if hd < 7 else (1 if hd < 14 else 2)
                    cb = hd - (0, 7, 14)[piece]
                    st = not started.get(piece, False)
                    started[piece] = True
                    last = (it == n_total - 1) and (hd in (6, 13, 15))
                    mm = s.add("tensor", lambda t, hd=hd, piece=piece, cb=cb, st=st, last=last: t.matmul(
                        kvp[piece][:, 65 * cb:65 * cb + 65],
                        ktile[:, 64 * hd:64 * (hd + 1)],
                        vtile[:, 65 * hd:65 * (hd + 1)],
                        start=st, stop=last),
                        [t_kmask, t_vcopy] if hd == 0 else [])
                t_kvmm = mm

            mzk1 = s.add("vector", lambda v: v.memset(sb["kv_sb"][64:128, :], 0.0))
            mzk2 = s.add("vector", lambda v: v.memset(sb["kv2"][0:64, :], 0.0), [mzk1])
            e1 = s.add("vector", lambda v: v.tensor_copy(
                sb["kv_sb"][0:64, 0:455], kvp[0][:, 0:455]), [t_kvmm])
            e2 = s.add("vector", lambda v: v.tensor_copy(
                sb["kv_sb"][0:64, 455:910], kvp[1][:, 0:455]), [e1])
            e3 = s.add("vector", lambda v: v.tensor_copy(
                sb["kv_sb"][0:64, 910:1040], kvp[2][:, 0:130]), [e2])
            self.t_kv = s.dma("sync", sb["kv2"][64:128, :], sb["kv_sb"][0:64, :],
                              s.new_dma_sem("kvdup", group=True),
                              deps=[e3, mzk2])

    def build_p2(self):
        s, nc, sb, inp = self.s, self.nc, self.sb, self.inp
        gate = self.gate()
        with ExitStack() as es:
            Wq = self.sbuf("WqSb", (128, NCH, HID), BF16, es)
            Wh = self.sbuf("WhSb", (128, NCH, HID), BF16, es)
            Wo = self.sbuf("WoSb", (128, NCH, HID), BF16, es)
            lnw_rep = self.sbuf("lnw_rep_s", (128, HID), BF16, es)
            lnb_rep = self.sbuf("lnb_rep_s", (128, HID), BF16, es)
            brow2 = self.sbuf("brow2", (1, 2 * HID), BF16, es)
            qf = self.sbuf("qf", (128, NCH, 512), BF16, es)
            gatedT = self.sbuf("gatedT", (128, NCH, 512), BF16, es)
            ao = self.sbuf("ao", (128, NCH, 512), BF16, es)
            htok = self.sbuf("htok", (128, HID), BF16, es)
            attn = self.sbuf("attn", (128, HID), BF16, es)
            lnout = self.sbuf("lnout", (128, HID), BF16, es)
            gated = self.sbuf("gated", (128, HID), BF16, es)
            bn = self.sbuf("bn", (128, 16), F32, es)
            sb["ident"] = self.sbuf("ident_s", (128, 128), BF16, es)
            sb["bq_col"] = self.sbuf("bq_col_s", (128, NCH), F32, es)
            sb["bo_col"] = self.sbuf("bo_col_s", (128, NCH), F32, es)
            denb = self.sbuf("denb", (128, 16), F32, es)
            xsq = self.sbuf("xsq", (128, NCH, 512), BF16, es)
            rowA = self.sbuf("rowA", (1, 512), F32, es)
            rowB = self.sbuf("rowB", (1, 512), F32, es)
            rowC = self.sbuf("rowC", (1, 512), F32, es)
            rowBb = self.sbuf("rowBb", (1, 512), BF16, es)
            rowCb = self.sbuf("rowCb", (1, 512), BF16, es)
            P = [self.psum(f"p2b{i}", (128, 512), F32, es) for i in range(7)]
            TP = self.psum("p2tp", (128, 1024), BF16, es)

            dW = s.new_dma_sem("p2w", group=True)
            s.dma("gpsimd", sb["ident"][:, :], inp["ident"][:, :], dW, deps=gate)
            s.dma("gpsimd", sb["bq_col"][:, :], inp["bq_col"][:, :], dW)
            s.dma("gpsimd", sb["bo_col"][:, :], inp["bo_col"][:, :], dW)
            s.dma("gpsimd", lnw_rep[:, :], inp["lnw_rep"][:, :], dW)
            s.dma("gpsimd", lnb_rep[:, :], inp["lnb_rep"][:, :], dW)
            s.dma("gpsimd", brow2[0:1, 0:HID], inp["bh_row"][:, :], dW)
            t_bb = s.dma("gpsimd", brow2[0:1, HID:2 * HID], inp["beta_row"][:, :], dW)
            t_brow = s.add("vector", lambda v: v.tensor_add(
                brow2[0:1, 0:HID], brow2[0:1, 0:HID],
                brow2[0:1, HID:2 * HID]), [t_bb])
            tw = None
            for nm, t in (("Wq", Wq), ("Wh", Wh), ("Wo", Wo)):
                win = inp[nm].rearrange("(c p) o -> p c o", p=128)
                for i in range(2):
                    tw = s.dma("gpsimd", t[:, 4 * i:4 * i + 4, :],
                               win[:, 4 * i:4 * i + 4, :], dW, deps=gate if nm == "Wq" and i == 0 else [])
            t_w = tw
            dx1 = s.new_dma_sem("x1store", group=True)
            self.d_x1b = dx1

            HEAD_PIECE = [(0, 0), (0, 1), (0, 2), (0, 3), (0, 4), (0, 5), (0, 6),
                          (1, 0), (1, 1), (1, 2), (1, 3), (1, 4), (1, 5), (1, 6),
                          (2, 0), (2, 1)]
            NP = (7, 7, 2)
            KV = sb["kv_sb"]

            groups = [(512 * g, 512) for g in range(10)] + [(5120, 128)]
            t_qevac = [None, None]
            t_hevac = None
            t_div = None
            t_tevac = None
            t_aoevac = None
            t_resid = None
            t_stats = None
            t_x1 = None
            t_wo = None
            t_num = None
            t_sq = None
            t_g_prev = None

            for (g0, Ng) in groups:
                ntile = Ng // 128
                for c in range(NCH):
                    pb = c % 2
                    for fin in range(NCH):
                        deps = []
                        if c == 0 and fin == 0:
                            deps = [t_w, self.t_X, t_div, t_aoevac, t_stats] + (gate if g0 == 0 else [])
                        if fin == 0:
                            deps = deps + [t_qevac[pb]]
                        mm = s.add("tensor", lambda t, c=c, fin=fin, pb=pb, g0=g0, Ng=Ng, st=(fin == 0): t.matmul(
                            P[pb][:, 0:Ng], Wq[:, fin, 128 * c:128 * (c + 1)],
                            sb["X"][:, fin, g0:g0 + Ng],
                            start=st, stop=(fin == NCH - 1)), deps)
                    t_qevac[pb] = s.add("scalar", lambda sc, c=c, pb=pb, Ng=Ng: sc.activation(
                        qf[:, c, 0:Ng], P[pb][:, 0:Ng], AF.Sigmoid,
                        bias=sb["bq_col"][:, c:c + 1]), [mm])
                t_qf = t_qevac[(NCH - 1) % 2]

                for ti in range(ntile):
                    tcol = g0 + 128 * ti
                    ext_ti = tcol // 128
                    if P2STOP < 2:
                        continue
                    mm = None
                    for half in range(2):
                        pb = 2 + half
                        for c in range(NCH):
                            deps = []
                            if half == 0 and c == 0:
                                deps = [t_hevac, t_w]
                            mm = s.add("tensor", lambda t, c=c, half=half, pb=pb, tcol=tcol, st=(c == 0): t.matmul(
                                P[pb][:, :], sb["X"][:, c, tcol:tcol + 128],
                                Wh[:, c, 512 * half:512 * (half + 1)],
                                start=st, stop=False), deps)
                        mm = s.add("tensor", lambda t, half=half, pb=pb: t.matmul(
                            P[pb][:, :], sb["ones_row"][0:1, 0:128],
                            brow2[0:1, 512 * half:512 * (half + 1)],
                            start=False, stop=True), [t_brow] if g0 == 0 and ti == 0 else [])
                    t_hmm = mm
                    th1 = s.add("vector", lambda v: v.tensor_copy(
                        htok[:, 0:512], P[2][:, :]), [t_hmm, t_tevac])
                    t_hevac = s.add("vector", lambda v: v.tensor_copy(
                        htok[:, 512:1024], P[3][:, :]), [th1])

                    if P2STOP < 3:
                        continue
                    mm = None
                    for hd in range(16):
                        piece, slot = HEAD_PIECE[hd]
                        deps = []
                        if hd == 0:
                            deps = [t_qf, self.t_kv, t_div, t_stats]
                        KVp = KV if hd % 2 == 0 else sb["kv2"]
                        mm = s.add("tensor", lambda t, hd=hd, piece=piece, slot=slot, ti=ti, KVp=KVp: t.matmul(
                            P[4 + piece][:, 65 * slot:65 * slot + 65],
                            qf[:, hd // 2, 128 * ti:128 * (ti + 1)],
                            KVp[:, 65 * hd:65 * hd + 65],
                            start=(slot == 0), stop=(slot == NP[piece] - 1)), deps)
                    t_num = mm

                    if P2STOP < 4:
                        continue
                    td = None
                    for piece in range(3):
                        nh = NP[piece]
                        base = (0, 7, 14)[piece]
                        td = s.add("scalar", lambda sc, piece=piece, nh=nh, base=base: sc.copy(
                            bass.AP(denb, base, [[16, 128], [1, nh], [1, 1]]),
                            bass.AP(P[4 + piece], 64, [[512, 128], [65, nh], [1, 1]])),
                            [t_num, t_g_prev] if piece == 0 else [])
                        td = s.add("vector", lambda v, nh=nh, base=base: v.reciprocal(
                            denb[:, base:base + nh], denb[:, base:base + nh]), [td])
                        td = s.add("vector", lambda v, piece=piece, nh=nh, base=base: v.tensor_tensor(
                            bass.AP(attn, 64 * base, [[HID, 128], [64, nh], [1, 64]]),
                            bass.AP(P[4 + piece], 0, [[512, 128], [65, nh], [1, 64]]),
                            bass.AP(denb, base, [[16, 128], [1, nh], [0, 64]]),
                            ALU.mult), [td])
                    t_div = td

                    if P2STOP < 5:
                        continue
                    b1 = s.add("vector", lambda v: v.bn_stats(bn[:, 0:6], attn[:, 0:512]), [t_div])
                    b2 = s.add("vector", lambda v: v.bn_stats(bn[:, 6:12], attn[:, 512:1024]), [b1])
                    b3 = s.add("vector", lambda v: v.bn_aggr(bn[:, 12:14], bn[:, 0:12]), [b2])
                    sq = s.add("scalar", lambda sc: sc.activation(
                        bn[:, 14:15], bn[:, 13:14], AF.Sqrt, bias=sb["eps"][:, :]), [b3])
                    t_sq = s.add("vector", lambda v: v.reciprocal(bn[:, 15:16], bn[:, 14:15]), [sq])
                    t_g = s.add("vector", lambda v: v.tensor_scalar(
                        lnout[:, :], attn[:, :], bn[:, 12:13], bn[:, 15:16],
                        ALU.subtract, ALU.mult), [t_sq])
                    t_g = s.add("vector", lambda v: v.scalar_tensor_tensor(
                        lnout[:, :], lnout[:, :], 0.0, lnw_rep[:, :],
                        ALU.bypass, ALU.mult), [t_g])
                    t_g = s.add("vector", lambda v: v.tensor_add(
                        lnout[:, :], lnout[:, :], lnb_rep[:, :]), [t_g])
                    t_g = s.add("vector", lambda v: v.tensor_mul(
                        gated[:, :], lnout[:, :], htok[:, :]), [t_g, t_hevac])
                    t_g_prev = t_g

                    if P2STOP < 6:
                        continue
                    tt = None
                    for c in range(NCH):
                        deps = [t_g, self.t_consts] if c == 0 else []
                        if c == 0:
                            deps = deps + [t_tevac]
                        tt = s.add("tensor", lambda t, c=c: t.transpose(
                            TP[:, 128 * c:128 * (c + 1)], gated[:, 128 * c:128 * (c + 1)],
                            sb["ident"][:, :]), deps)
                    t_tevac = s.add("scalar", lambda sc, ti=ti: sc.copy(
                        gatedT[:, :, 128 * ti:128 * (ti + 1)],
                        TP[:, :].rearrange("p (c x) -> p c x", x=128)), [tt])

                if P2STOP < 7:
                    continue
                mm = None
                for c in range(NCH):
                    pb = c % 2
                    for fin in range(NCH):
                        deps = []
                        if c == 0 and fin == 0:
                            deps = [t_tevac, t_qf]
                        if fin == 0:
                            deps = deps + [t_aoevac]
                        mm = s.add("tensor", lambda t, c=c, fin=fin, pb=pb, Ng=Ng, st=(fin == 0): t.matmul(
                            P[pb][:, 0:Ng], Wo[:, fin, 128 * c:128 * (c + 1)],
                            gatedT[:, fin, 0:Ng],
                            start=st, stop=(fin == NCH - 1)), deps)
                    t_aoevac = s.add("scalar", lambda sc, c=c, pb=pb, Ng=Ng: sc.activation(
                        ao[:, c, 0:Ng], P[pb][:, 0:Ng], AF.Relu,
                        bias=sb["bo_col"][:, c:c + 1]), [mm])

                if P2STOP < 8:
                    continue
                t_resid = s.add("vector", lambda v, g0=g0, Ng=Ng: v.tensor_add(
                    sb["X"][:, :, g0:g0 + Ng],
                    sb["X"][:, :, g0:g0 + Ng], ao[:, :, 0:Ng]), [t_aoevac, t_stats])

                if P2STOP < 9:
                    continue
                t_sqr = s.add("scalar", lambda sc, g0=g0, Ng=Ng: sc.activation(
                    xsq[:, :, 0:Ng], sb["X"][:, :, g0:g0 + Ng], AF.Square), [t_resid])
                mm = None
                for c in range(NCH):
                    mm = s.add("tensor", lambda t, c=c, g0=g0, Ng=Ng, st=(c == 0): t.matmul(
                        P[4][0:1, 0:Ng], sb["ones_col"][:, :], sb["X"][:, c, g0:g0 + Ng],
                        start=st, stop=(c == NCH - 1)),
                        [t_resid, t_num, t_div] if c == 0 else [])
                for c in range(NCH):
                    mm = s.add("tensor", lambda t, c=c, Ng=Ng, st=(c == 0): t.matmul(
                        P[5][0:1, 0:Ng], sb["ones_col"][:, :], xsq[:, c, 0:Ng],
                        start=st, stop=(c == NCH - 1)), [t_sqr] if c == 0 else [])
                t_smm = mm
                r1 = s.add("vector", lambda v, Ng=Ng: v.tensor_scalar(
                    rowA[0:1, 0:Ng], P[4][0:1, 0:Ng], 1.0 / HID, None, ALU.mult), [t_smm])
                r2 = s.add("vector", lambda v, Ng=Ng: v.tensor_scalar(
                    rowB[0:1, 0:Ng], P[5][0:1, 0:Ng], 1.0 / HID, None, ALU.mult), [r1])
                r3 = s.add("vector", lambda v, Ng=Ng: v.tensor_mul(
                    rowC[0:1, 0:Ng], rowA[0:1, 0:Ng], rowA[0:1, 0:Ng]), [r2])
                r4 = s.add("vector", lambda v, Ng=Ng: v.tensor_sub(
                    rowB[0:1, 0:Ng], rowB[0:1, 0:Ng], rowC[0:1, 0:Ng]), [r3])
                r5 = s.add("scalar", lambda sc, Ng=Ng: sc.activation(
                    rowB[0:1, 0:Ng], rowB[0:1, 0:Ng], AF.Sqrt, bias=sb["eps"][0:1, 0:1]), [r4])
                r6 = s.add("vector", lambda v, Ng=Ng: v.reciprocal(
                    rowB[0:1, 0:Ng], rowB[0:1, 0:Ng]), [r5])
                r7 = s.add("vector", lambda v, Ng=Ng: v.tensor_mul(
                    rowC[0:1, 0:Ng], rowA[0:1, 0:Ng], rowB[0:1, 0:Ng]), [r6])
                c1 = s.add("vector", lambda v, Ng=Ng: v.tensor_copy(
                    rowBb[0:1, 0:Ng], rowB[0:1, 0:Ng]), [r7])
                c2 = s.add("vector", lambda v, Ng=Ng: v.tensor_copy(
                    rowCb[0:1, 0:Ng], rowC[0:1, 0:Ng]), [c1])
                m1 = s.add("tensor", lambda t, Ng=Ng: t.matmul(
                    P[4][:, 0:Ng], sb["ones_row"][0:1, 0:128], rowBb[0:1, 0:Ng],
                    start=True, stop=True), [c1, r1])
                m2 = s.add("tensor", lambda t, Ng=Ng: t.matmul(
                    P[5][:, 0:Ng], sb["ones_row"][0:1, 0:128], rowCb[0:1, 0:Ng],
                    start=True, stop=True), [c2, r2])
                x1a = s.add("vector", lambda v, g0=g0, Ng=Ng: v.tensor_mul(
                    sb["X"][:, :, g0:g0 + Ng], sb["X"][:, :, g0:g0 + Ng],
                    bass.AP(P[4], 0, [[512, 128], [0, NCH], [1, Ng]])), [m1])
                t_x1 = x1a
                t_stats = s.add("vector", lambda v, g0=g0, Ng=Ng: v.tensor_sub(
                    sb["X"][:, :, g0:g0 + Ng], sb["X"][:, :, g0:g0 + Ng],
                    bass.AP(P[5], 0, [[512, 128], [0, NCH], [1, Ng]])), [t_x1, m2])

                lo = max(g0, HALO)
                hi = g0 + Ng
                if hi > lo:
                    s.dma("sync", self.x1b[:, :, lo - HALO:hi - HALO],
                          sb["X"][:, :, lo:hi], dx1, deps=[t_stats])
            self.t_p2 = s.add("vector", lambda v: v.tensor_mul(
                sb["X"][:, :, HALO - 1:HALO], sb["X"][:, :, HALO - 1:HALO],
                bass.AP(sb["halo01"], 0, [[1, 128], [0, NCH], [0, 1]])),
                [t_stats, self.t_consts])

    def build_layer(self, l, do_mlp=True):
        s, nc, sb, inp = self.s, self.nc, self.sb, self.inp
        gate = self.gate()
        if l == 0:
            pcol0, n_proj, qcol0, CL = 128, 10, 128, 1
        else:
            pcol0, n_proj, qcol0, CL = 128, 10, 640, 1
        sqd = (qcol0 - pcol0) // 512
        n_O = (TEXT - qcol0) // 512
        n_iter = n_proj + CL + 1
        Bshift = CL + sqd

        with ExitStack() as es:
            wqkv = self.sbuf(f"wqkv_l{l}", (128, NCH, 3 * INNER), BF16, es)
            wo = self.sbuf(f"wo_l{l}", (128, 4, HID), BF16, es)
            qs_rep = self.sbuf(f"qsr{l}", (128, INNER), BF16, es)
            ks_rep = self.sbuf(f"ksr{l}", (128, INNER), BF16, es)
            qf = [self.sbuf(f"qf{l}_{i}", (128, 4, 512), BF16, es) for i in range(3)]
            kf_e = [self.sbuf(f"kfe{l}_{i}", (128, 4, 512), BF16, es) for i in range(3)]
            kf_o = [self.sbuf(f"kfo{l}_{i}", (128, 4, 512), BF16, es) for i in range(3)]
            vt = [self.sbuf(f"vt{l}_{i}", (128, 4, 520), BF16, es) for i in range(3)]
            tabs = self.sbuf(f"tab{l}", (128, 4, 4, DF), BF16, es)
            pt = [self.sbuf(f"pt{l}_{i}", (128, 640), BF16, es) for i in range(2)]
            m8 = self.sbuf(f"m8_{l}", (128, 1), F32, es)
            maskrowH = self.sbuf(f"mrowH{l}", (128, NT_EXT), F32, es)
            O_acc = self.sbuf(f"Oacc{l}", (65, HEADS_F, 512), BF16, es)
            o_sb = [self.sbuf(f"osb{l}_0", (128, 4, 512), BF16, es)]
            tsum = self.sbuf(f"tsum{l}", (65, 512), F32, es)
            rrowb = self.sbuf(f"rrowb{l}", (1, 512), BF16, es)
            rrowf = self.sbuf(f"rrowf{l}", (1, 512), F32, es)
            r4sb = self.sbuf(f"r4sb{l}", (128, 4), F32, es)
            ubuf = self.sbuf(f"ubuf{l}", (128, 512), BF16, es)
            qhat = self.sbuf(f"qhat{l}", (128, 512), BF16, es)
            qrot = self.sbuf(f"qrot{l}", (128, 512), F32, es)
            krot = self.sbuf(f"krot{l}", (128, 512), F32, es)
            xqk = self.sbuf(f"xqk{l}", (128, 512), BF16, es)
            qraw = self.sbuf(f"qraw{l}", (128, 512), BF16, es)
            bnf = self.sbuf(f"bnf{l}", (128, 32), F32, es)
            P = [self.psum(f"L{l}b{i}", (128, 512), F32, es) for i in range(8)]

            dW = s.new_dma_sem(f"L{l}w", group=True)
            win = inp[f"wqkv{l}"].rearrange("(c p) o -> p c o", p=128)
            tw = None
            for i in range(4):
                tw = s.dma("gpsimd", wqkv[:, 2 * i:2 * i + 2, :],
                           win[:, 2 * i:2 * i + 2, :], dW, deps=gate if i == 0 else [])
            woin = inp[f"wo{l}"].rearrange("(c p) o -> p c o", p=128)
            tw = s.dma("gpsimd", wo[:, :, :], woin[:, :, :], dW)
            tw = s.dma("gpsimd", qs_rep[:, :], inp[f"qs_rep{l}"][:, :], dW)
            tw = s.dma("gpsimd", ks_rep[:, :], inp[f"ks_rep{l}"][:, :], dW)
            t_w = tw
            t_w8 = s.add("vector", lambda v: v.tensor_scalar(
                qs_rep[:, :], qs_rep[:, :], 8.0, None, ALU.mult), [t_w])
            t_mr = s.add("vector", lambda v: v.tensor_mul(
                ubuf[:, :], qs_rep[:, :], ks_rep[:, :]), [t_w8])
            t_mr = s.add("vector", lambda v: v.tensor_reduce(
                m8[:, 0:1], ubuf[:, :], AX.X, ALU.max), [t_mr])
            t_mrow = s.add("vector", lambda v: v.tensor_scalar(
                maskrowH[:, :], sb["maskrow"][:, :], m8[:, 0:1], None,
                ALU.subtract), [t_mr, self.t_consts])
            dtab = s.new_dma_sem(f"L{l}tab")

            mz = []
            mz.append(s.add("vector", lambda v: v.memset(kf_e[2][:, :, :], 0.0), gate))
            mz.append(s.add("vector", lambda v: v.memset(kf_o[2][:, :, :], 0.0), []))
            for i in range(2):
                mz.append(s.add("vector", lambda v, i=i: v.memset(
                    kf_e[i][64:128, :, :], 0.0), []))
                mz.append(s.add("vector", lambda v, i=i: v.memset(
                    kf_o[i][0:64, :, :], 0.0), []))
            for i in range(3):
                mz.append(s.add("vector", lambda v, i=i: v.memset(
                    bass.AP(vt[i], 64, [[4 * 520, 128], [520, 4], [65, 8]]), 1.0), []))
            mz.append(s.add("vector", lambda v: v.memset(O_acc[:, :, :], 0.0), []))
            t_init = mz[-1]

            t_tab = None
            t_tabuse = None
            t_vevac = {}
            t_qkevac = {}
            t_qke = {0: [None] * 4, 1: [None] * 4, 2: [None] * 4}
            t_expL = None
            t_expP = [None, None]
            t_ksq_last = None
            self.t_ttq = None
            self.t_ttk = None
            self.t_qe_prev = None
            self.t_ke_prev = None
            self.t_cpr = None
            t_pv = None
            t_oacc = None
            t_tt1 = None
            t_wo = None
            t_wo2 = None
            t_wres = None
            t_uq = None
            t_uk = None
            t_qsqr = None
            t_qrd = None
            t_rep = None
            t_tke_last = None
            t_projmm = None
            t_stmm = None

            def piece_tab(pw):
                nonlocal t_tab
                t_tab = s.dma("gpsimd", tabs[:, :, :, :],
                              inp["tab"][pcol0 + 512 * pw:pcol0 + 512 * pw + 512, :, :]
                              .rearrange("(j p) a d -> p j a d", p=128),
                              dtab, deps=[t_tabuse])

            def piece_v(pw):
                slot = pw % 3
                for j in range(4):
                    tc = pcol0 + 512 * pw + 128 * j
                    mm = None
                    for c in range(NCH):
                        off = -1 if c >= 4 else 0
                        deps = []
                        if c == 0:
                            deps = [t_vevac.get(slot), t_tt1, t_w,
                                    self.t_p2 if l == 0 else
                                    getattr(self, "t_halofix", None),
                                    getattr(self, "_dbg_tok", None)]
                            if pw == 0 and j == 0:
                                deps = deps + gate
                        mm = s.add("tensor", lambda t, c=c, tc=tc, off=off, st=(c == 0): t.matmul(
                            P[6][:, :], sb["X"][:, c, tc + off:tc + off + 128],
                            wqkv[:, c, 1024:1536],
                            start=st, stop=(c == NCH - 1)), deps)
                    t_vevac[slot] = s.add("vector", lambda v, slot=slot, j=j: v.tensor_copy(
                        bass.AP(vt[slot], 520 * j, [[4 * 520, 128], [65, 8], [1, 64]]),
                        P[6][:, :].rearrange("p (h m) -> p h m", m=64)),
                        [mm, t_init, t_pv, getattr(self, "_dbg_tok", None)])

            def piece_q(pw, j):
                nonlocal t_projmm, t_qsqr, t_qrd, t_uq
                tc = pcol0 + 512 * pw + 128 * j
                deps0 = [t_w, self.t_p2 if l == 0 else
                         getattr(self, "t_halofix", None)]
                mm = None
                for c in range(NCH):
                    off = -1 if c >= 4 else 0
                    deps = []
                    if c == 0:
                        deps = deps0 + [t_uq, t_tt1]
                    mm = s.add("tensor", lambda t, c=c, tc=tc, off=off, st=(c == 0): t.matmul(
                        P[2][:, :], sb["X"][:, c, tc + off:tc + off + 128],
                        wqkv[:, c, 0:512],
                        start=st, stop=(c == NCH - 1)), deps)
                t_projmm = mm
                t_qsqr = s.add("scalar", lambda sc: sc.activation(
                    xqk[:, :], P[2][:, :], AF.Square), [mm, t_qrd])
                rd = s.add("vector", lambda v: v.tensor_reduce(
                    bnf[:, 0:8], xqk[:, :].rearrange("p (h d) -> p h d", d=64),
                    AX.X, ALU.add), [t_qsqr])
                t_qrd = rd
                tqr = s.add("vector", lambda v: v.tensor_copy(
                    qraw[:, :], P[2][:, :]), [rd])
                t_uq = tqr

            def piece_k(pw, j):
                nonlocal t_projmm, t_ksq_last, t_qrd, t_uk, t_tabuse
                tc = pcol0 + 512 * pw + 128 * j
                mm = None
                for c in range(NCH):
                    off = -1 if c >= 4 else 0
                    deps = []
                    if c == 0:
                        deps = [t_uk, t_uq, t_tt1]
                    mm = s.add("tensor", lambda t, c=c, tc=tc, off=off, st=(c == 0): t.matmul(
                        P[2][:, :], sb["X"][:, c, tc + off:tc + off + 128],
                        wqkv[:, c, 512:1024],
                        start=st, stop=(c == NCH - 1)), deps)
                t_projmm = mm
                tks = s.add("scalar", lambda sc: sc.activation(
                    xqk[:, :], P[2][:, :], AF.Square), [mm, t_qrd])
                t_ksq_last = tks
                rd = s.add("vector", lambda v: v.tensor_reduce(
                    bnf[:, 8:16], xqk[:, :].rearrange("p (h d) -> p h d", d=64),
                    AX.X, ALU.add), [tks])
                t_qrd = rd
                sq = s.add("scalar", lambda sc: sc.activation(
                    bnf[:, 16:32], bnf[:, 0:16], AF.Sqrt), [rd])
                mx = s.add("vector", lambda v: v.tensor_scalar(
                    bnf[:, 16:32], bnf[:, 16:32], 1e-12, None, ALU.max), [sq])
                rc = s.add("vector", lambda v: v.reciprocal(
                    bnf[:, 16:32], bnf[:, 16:32]), [mx])
                t1 = s.add("vector", lambda v: v.tensor_tensor(
                    ubuf[:, :], qraw[:, :],
                    bass.AP(bnf, 16, [[32, 128], [1, 8], [0, 64]]),
                    ALU.mult), [rc])
                t2 = s.add("vector", lambda v: v.tensor_mul(
                    qhat[:, :], ubuf[:, :], qs_rep[:, :]), [t1, t_w8])
                ta = s.add("vector", lambda v, j=j: v.tensor_tensor(
                    ubuf[:, :], qhat[:, :],
                    bass.AP(tabs, 256 * j + 0 * DF, [[4 * 4 * DF, 128], [0, 8], [1, DF]]),
                    ALU.mult), [t2, t_tab])
                tb = s.add("vector", lambda v, j=j: v.tensor_tensor(
                    qrot[:, :].rearrange("p (h t x) -> p h t x", h=8, t=2),
                    bass.AP(qhat, 32, [[512, 128], [64, 8], [-32, 2], [1, 32]]),
                    bass.AP(tabs, 256 * j + 1 * DF, [[4 * 4 * DF, 128], [0, 8], [32, 2], [1, 32]]),
                    ALU.mult), [ta, self.t_ttq])
                tq = s.add("vector", lambda v: v.tensor_add(
                    qrot[:, :], qrot[:, :], ubuf[:, :]), [tb])
                t1 = s.add("vector", lambda v: v.tensor_tensor(
                    ubuf[:, :], P[2][:, :],
                    bass.AP(bnf, 24, [[32, 128], [1, 8], [0, 64]]),
                    ALU.mult), [tq])
                t_uk = t1
                t2 = s.add("vector", lambda v: v.tensor_mul(
                    qhat[:, :], ubuf[:, :], ks_rep[:, :]), [t1])
                ta = s.add("vector", lambda v, j=j: v.tensor_tensor(
                    ubuf[:, :], qhat[:, :],
                    bass.AP(tabs, 256 * j + 2 * DF, [[4 * 4 * DF, 128], [0, 8], [1, DF]]),
                    ALU.mult), [t2])
                tb = s.add("vector", lambda v, j=j: v.tensor_tensor(
                    krot[:, :].rearrange("p (h t x) -> p h t x", h=8, t=2),
                    bass.AP(qhat, 32, [[512, 128], [64, 8], [-32, 2], [1, 32]]),
                    bass.AP(tabs, 256 * j + 3 * DF, [[4 * 4 * DF, 128], [0, 8], [32, 2], [1, 32]]),
                    ALU.mult), [ta, self.t_ttk])
                tk = s.add("vector", lambda v: v.tensor_add(
                    krot[:, :], krot[:, :], ubuf[:, :]), [tb])
                t_tabuse = tk
                return tq, tk

            def piece_T(pw, j, tq_tok, tk_tok):
                nonlocal t_qkevac, t_tke_last
                slot = pw % 3
                tt = None
                for cc in range(4):
                    deps = [tq_tok, t_w]
                    if cc == 0:
                        deps = deps + [t_qkevac.get(slot), self.t_qe_prev]
                    tt = s.add("tensor", lambda t, cc=cc: t.transpose(
                        P[4][:, 128 * cc:128 * (cc + 1)],
                        qrot[:, 128 * cc:128 * (cc + 1)], sb["identF"][:, :]), deps)
                self.t_ttq = tt
                tqe = s.add("scalar", lambda sc, slot=slot, j=j: sc.copy(
                    qf[slot][:, :, 128 * j:128 * (j + 1)],
                    P[4][:, :].rearrange("p (c x) -> p c x", x=128)),
                    [tt, getattr(self, "_dbg_tok", None)])
                self.t_qe_prev = tqe
                tt = None
                for cc in range(4):
                    deps = [tk_tok]
                    if cc == 0:
                        deps = deps + [tqe, self.t_ke_prev]
                    tt = s.add("tensor", lambda t, cc=cc: t.transpose(
                        P[4][:, 128 * cc:128 * (cc + 1)],
                        krot[:, 128 * cc:128 * (cc + 1)], sb["identF"][:, :]), deps)
                self.t_ttk = tt
                tke = s.add("scalar", lambda sc, slot=slot, j=j: sc.copy(
                    kf_e[slot][0:64, :, 128 * j:128 * (j + 1)],
                    P[4][0:64, :].rearrange("p (c x) -> p c x", x=128)),
                    [tt, getattr(self, "_dbg_tok", None)])
                tke = s.add("scalar", lambda sc, slot=slot, j=j: sc.copy(
                    kf_o[slot][64:128, :, 128 * j:128 * (j + 1)],
                    P[4][64:128, :].rearrange("p (c x) -> p c x", x=128)), [tke])
                self.t_ke_prev = tke
                t_qkevac[slot] = tke
                t_qke[slot][j] = tke
                t_tke_last = tke

            for w in range(n_iter):
                slot = w % 3
                if w < n_proj:
                    piece_tab(w)
                    piece_v(w)
                    pend = None
                    for j in range(4):
                        if pend is not None:
                            piece_T(w, pend[0], pend[1], pend[2])
                        piece_q(w, j)
                        tqt, tkt = piece_k(w, j)
                        pend = (j, tqt, tkt)
                    piece_T(w, pend[0], pend[1], pend[2])

                if l == 0 and w == 2 and DBG_TAP:
                    dsemd = s.new_dma_sem(f"dbg{w}", group=True)
                    s.dma("gpsimd", self.dbg_qf[:, :, :], qf[w % 3][:, :, :], dsemd,
                          deps=[t_tke_last])
                    s.dma("gpsimd", self.dbg_kf[:, :, :], kf_e[w % 3][:, :, :], dsemd)
                    dtv = s.dma("gpsimd", self.dbg_vt[:, :, :], vt[w % 3][:, :, :], dsemd)
                    s.dma("gpsimd", self.dbg_kf2[:, :, :], kf_o[w % 3][:, :, :], dsemd)
                    dtv = s.dma("gpsimd", self.dbg_vt2[:, :, :], vt[(w - 1) % 3][:, :, :], dsemd)
                    self._dbg_tok = dtv
                B_wo = w - 1 - Bshift
                if 0 <= B_wo < n_O:
                    B = B_wo
                    bcol = qcol0 + 512 * B
                    mm = None
                    for c in range(NCH):
                        wbank = P[6] if c % 2 == 0 else P[7]
                        for fin in range(4):
                            deps = []
                            if c == 0 and fin == 0:
                                deps = [t_tt1, t_w, t_oacc, t_vevac.get(w % 3)]
                            if fin == 0:
                                deps = deps + [t_wres]
                            mm = s.add("tensor", lambda t, c=c, fin=fin, wbank=wbank, B=B, st=(fin == 0): t.matmul(
                                wbank[:, :], wo[:, fin, 128 * c:128 * (c + 1)],
                                o_sb[0][:, fin, :],
                                start=st, stop=(fin == 3)), deps)
                        t_wres = s.add("vector", lambda v, c=c, wbank=wbank, bcol=bcol: v.tensor_add(
                            sb["X"][:, c, bcol:bcol + 512],
                            sb["X"][:, c, bcol:bcol + 512], wbank[:, :]), [mm])
                    if bcol <= HALO - 1 < bcol + 512:
                        t_wres = s.add("vector", lambda v: v.tensor_mul(
                            sb["X"][:, :, HALO - 1:HALO],
                            sb["X"][:, :, HALO - 1:HALO],
                            bass.AP(sb["halo01"], 0, [[1, 128], [0, NCH], [0, 1]])),
                            [t_wres])
                        self.t_halofix = t_wres
                    t_wo2 = t_wres
                    t_wo = t_wres
                    if l == 0 and B == 2 and DBG_TAP:
                        dsemo = s.new_dma_sem("dbgo", group=True)
                        t_wo2 = s.dma("gpsimd", self.dbg_o[:, :, :], o_sb[0][:, :, :], dsemo,
                                      deps=[t_tt1])

                pb = w - CL
                B_left = w - Bshift
                B_right = B_left + 1
                do_left = 0 <= B_left < n_O
                do_right = 0 <= B_right < n_O
                if 0 <= pb < n_proj and (do_left or do_right):
                    kslot = pb % 3
                    jlast_r = -1
                    for j in range(4):
                        kt0_ = pcol0 + 512 * pb + 128 * j
                        qn_ = min(kt0_ + 640, TEXT) - kt0_
                        if qn_ > 512 - 128 * j:
                            jlast_r = j

                    for h in range(HEADS_F):
                        kfp = kf_e if h % 2 == 0 else kf_o
                        pend = None
                        rstarted = [False]

                        def emit_pv(j, e1):
                            split = 512 - 128 * j
                            qn_j = min(pcol0 + 512 * pb + 128 * j + 640, TEXT) - (
                                pcol0 + 512 * pb + 128 * j)
                            if do_left:
                                n_l = min(split, qn_j)
                                st = (j == 0)
                                deps = [e1] + ([t_tt1, t_vevac.get(w % 3), t_wo]
                                               if st else [])
                                s.add("tensor", lambda t, j=j, h=h, kslot=kslot, n_l=n_l, st=st: t.matmul(
                                    P[6][0:65, 128 * j:128 * j + n_l],
                                    vt[kslot][:, j, 65 * h:65 * h + 65],
                                    pt[j % 2][:, 0:n_l],
                                    start=st, stop=(j == 3)), deps)
                            if do_right and qn_j > split:
                                n_r = qn_j - split
                                s.add("tensor", lambda t, j=j, h=h, kslot=kslot, n_r=n_r, split=split, jl=jlast_r: t.matmul(
                                    P[7][0:65, 0:n_r],
                                    vt[kslot][:, j, 65 * h:65 * h + 65],
                                    pt[j % 2][:, split:split + n_r],
                                    start=False, stop=(j == jl)), deps=[e1])
                            return s.ops["tensor"][-1]

                        for j in range(4):
                            kt0 = pcol0 + 512 * pb + 128 * j
                            kti = kt0 // 128
                            qe = min(kt0 + 640, TEXT)
                            qn = qe - kt0
                            par = j % 2
                            bankF = P[5] if par == 0 else P[0]
                            bankT = P[3] if par == 0 else P[1]
                            segs = [(pb % 3, 128 * j, min(512 - 128 * j, qn), 0)]
                            rem = qn - segs[0][2]
                            if rem > 0:
                                segs.append((((pb + 1) % 3), 0, rem, segs[0][2]))
                            first = True
                            bstart = {}
                            for (qs_, qoff, qlen, pso) in segs:
                                a = 0
                                while a < qlen:
                                    p = pso + a
                                    if p < 512:
                                        take = min(qlen - a, 512 - p)
                                        bank, boff = bankF, p
                                    else:
                                        take = qlen - a
                                        bank, boff = bankT, p - 512
                                    deps = []
                                    if first:
                                        deps = [t_qkevac.get(pb % 3),
                                                t_qke[(pb + 1) % 3][j],
                                                t_expP[par]]
                                    st = not bstart.get(id(bank), False)
                                    bstart[id(bank)] = True
                                    s.add("tensor", lambda t, qs_=qs_, qoff=qoff, a=a, take=take, bank=bank, boff=boff, h=h, kslot=kslot, j=j, st=st, kfp=kfp: t.matmul(
                                        bank[:, boff:boff + take],
                                        kfp[kslot][:, h // 2, 128 * j:128 * (j + 1)],
                                        qf[qs_][:, h // 2, qoff + a:qoff + a + take],
                                        start=st, stop=False), deps)
                                    first = False
                                    a += take
                            t_stmm = s.add("tensor", lambda t, bankF=bankF: t.matmul(
                                bankF[:, 0:128], sb["negI"][:, :], sb["mk_hi"][:, :],
                                start=False, stop=True), [])
                            if qn > 512:
                                t_stmm = s.add("tensor", lambda t, qn=qn, bankT=bankT: t.matmul(
                                    bankT[:, 0:qn - 512], sb["negI"][:, :],
                                    sb["mk_lo"][:, 0:qn - 512], start=False, stop=True), [])
                            e1 = s.add("scalar", lambda sc, j=j, kti=kti, qn=qn, bankF=bankF: sc.activation(
                                pt[j % 2][:, 0:min(qn, 512)], bankF[:, 0:min(qn, 512)], AF.Exp,
                                bias=maskrowH[:, kti:kti + 1]), [t_stmm, t_pv, t_mrow])
                            if qn > 512:
                                e1 = s.add("scalar", lambda sc, j=j, kti=kti, qn=qn, bankT=bankT: sc.activation(
                                    pt[j % 2][:, 512:qn], bankT[:, 0:qn - 512], AF.Exp,
                                    bias=maskrowH[:, kti:kti + 1]), [e1])
                            t_expL = e1
                            t_expP[par] = e1
                            if j == 0 and do_right:
                                s.add("tensor", lambda t: t.matmul(
                                    P[7][0:65, :], sb["ones_row"][0:1, 0:65],
                                    sb["zrow"][0:1, 0:512], start=True, stop=False),
                                    [t_oacc, t_wo, self.t_consts])
                            if pend is not None:
                                t_pv = emit_pv(*pend)
                            pend = (j, e1)
                        t_pv = emit_pv(*pend)
                        if do_left:
                            tt1 = s.add("vector", lambda v, h=h: v.tensor_add(
                                tsum[:, :], P[6][0:65, :], O_acc[:, h, :]), [t_pv, t_oacc])
                            tcp = s.add("vector", lambda v: v.tensor_copy(
                                rrowf[0:1, :], tsum[64:65, :]), [tt1])
                            fm = None
                            for c4 in range(4):
                                fm = s.add("tensor", lambda t, c4=c4, st=(c4 == 0): t.matmul(
                                    P[4][:, 0:4],
                                    rrowf[0:1, 128 * c4:128 * (c4 + 1)],
                                    sb["e4"][0:1, 4 * c4:4 * c4 + 4],
                                    start=st, stop=(c4 == 3)),
                                    [tcp, self.t_cpr, t_tke_last] if c4 == 0 else [])
                            flr = s.add("vector", lambda v: v.tensor_scalar(
                                P[4][:, 0:4], P[4][:, 0:4], 1e-30, None, ALU.max), [fm])
                            rcp = s.add("vector", lambda v: v.reciprocal(
                                r4sb[:, :], P[4][:, 0:4]), [flr])
                            bt = None
                            for c4 in range(4):
                                bt = s.add("tensor", lambda t, c4=c4: t.transpose(
                                    P[4][0:1, 128 * c4:128 * (c4 + 1)],
                                    r4sb[:, c4:c4 + 1], sb["identF"][:, :]),
                                    [rcp] if c4 == 0 else [])
                            cpr = s.add("vector", lambda v: v.tensor_copy(
                                rrowb[0:1, :], P[4][0:1, 0:512]), [bt])
                            self.t_cpr = cpr
                            trp = s.add("tensor", lambda t: t.matmul(
                                P[2][0:64, 0:512], sb["ones_row"][0:1, 0:64], rrowb[0:1, :],
                                start=True, stop=True), [cpr, t_tt1, t_uq, t_uk])
                            t_rep = trp
                            to2 = s.add("vector", lambda v, h=h, B=B_left: v.tensor_tensor(
                                o_sb[0][64 * (h % 2):64 * (h % 2) + 64, h // 2, :],
                                tsum[0:64, :], P[2][0:64, :], ALU.mult), [trp, t_wo2])
                            t_tt1 = to2
                        if do_right:
                            t_oacc = s.add("vector", lambda v, h=h: v.tensor_copy(
                                O_acc[:, h, :], P[7][0:65, :]),
                                [t_pv] + ([t_tt1] if do_left else []))
                        else:
                            t_oacc = s.add("vector", lambda v, h=h: v.memset(
                                O_acc[:, h, :], 0.0), [t_tt1])
                    if False:
                        B = B_left
                        bcol = qcol0 + 512 * B
                        mm = None
                        for c in range(NCH):
                            wbank = P[6] if c % 2 == 0 else P[7]
                            for fin in range(4):
                                deps = []
                                if c == 0 and fin == 0:
                                    deps = [t_tt1, t_w, t_oacc]
                                if fin == 0:
                                    deps = deps + [t_wres]
                                mm = s.add("tensor", lambda t, c=c, fin=fin, wbank=wbank, B=B, st=(fin == 0): t.matmul(
                                    wbank[:, :], wo[:, fin, 128 * c:128 * (c + 1)],
                                    o_sb[0][:, fin, :],
                                    start=st, stop=(fin == 3)), deps)
                            t_wres = s.add("vector", lambda v, c=c, wbank=wbank, bcol=bcol: v.tensor_add(
                                sb["X"][:, c, bcol:bcol + 512],
                                sb["X"][:, c, bcol:bcol + 512], wbank[:, :]), [mm])
                        if bcol <= HALO - 1 < bcol + 512:
                            t_wres = s.add("vector", lambda v: v.tensor_mul(
                                sb["X"][:, :, HALO - 1:HALO],
                                sb["X"][:, :, HALO - 1:HALO],
                                bass.AP(sb["halo01"], 0, [[1, 128], [0, NCH], [0, 1]])),
                                [t_wres])
                            self.t_halofix = t_wres
                        t_wo2 = t_wres
                        t_wo = t_wres
                        if l == 0 and B == 2 and DBG_TAP:
                            dsemo = s.new_dma_sem("dbgo", group=True)
                            t_wo2 = s.dma("gpsimd", self.dbg_o[:, :, :], o_sb[0][:, :, :], dsemo,
                                          deps=[t_tt1])

        if do_mlp:
            self._mlp(l)

    def _mlp(self, l):
        s, nc, sb, inp = self.s, self.nc, self.sb, self.inp
        gate = self.gate()
        if l == 0:
            mc0, ntok = 512, 4736
        else:
            mc0, ntok = 1152, 4096
        megas = []
        a = 0
        while a < ntok:
            megas.append((mc0 + a, min(MEGA, ntok - a)))
            a += min(MEGA, ntok - a)

        with ExitStack() as es:
            h_buf = self.sbuf(f"hbuf{l}", (128, 22, MEGA), BF16, es)
            xt = self.sbuf(f"xt{l}", (128, NCH, MEGA), BF16, es)
            w1a = [self.sbuf(f"w1a{l}_{i}", (128, NCH, 128), BF16, es) for i in range(2)]
            w1g = [self.sbuf(f"w1g{l}_{i}", (128, NCH, 128), BF16, es) for i in range(2)]
            w2b = [self.sbuf(f"w2b{l}_{i}", (128, HID), BF16, es) for i in range(2)]
            gout = self.sbuf(f"gout{l}", (128, MEGA), BF16, es)
            c12 = self.sbuf(f"c12_{l}", (2, H2), BF16, es)
            c2row = self.sbuf(f"c2row{l}", (1, H2), BF16, es)
            misc2 = self.sbuf(f"misc2_{l}", (2, MEGA), BF16, es)
            xb = [self.sbuf(f"xb{l}_{i}", (128, NCH, 1), BF16, es) for i in range(2)]
            mrow_ = self.sbuf(f"mrow{l}", (1, MEGA), F32, es)
            srow = self.sbuf(f"srow{l}", (1, MEGA), F32, es)
            srowb = self.sbuf(f"srowb{l}", (1, MEGA), BF16, es)
            P = [self.psum(f"M{l}b{i}", (128, 512), F32, es) for i in range(8)]

            dw1a = [s.new_dma_sem(f"m{l}w1a{i}") for i in range(2)]
            dw1g = [s.new_dma_sem(f"m{l}w1g{i}") for i in range(2)]
            dw2 = [s.new_dma_sem(f"m{l}w2{i}") for i in range(2)]
            dmisc = s.new_dma_sem(f"mm{l}", group=True)
            t_b1 = s.dma("gpsimd", c2row[0:1, :], inp[f"b1row{l}"][:, :], dmisc, deps=gate)
            t_m2 = s.dma("gpsimd", misc2[:, :], inp["ones2row"][:, :], dmisc)
            w1hb = self.w1bf[l].rearrange("(c p) o -> p c o", p=128)

            t_xb = s.add("vector", lambda v: v.tensor_copy(
                xb[0][:, :, :], sb["X"][:, :, mc0 - 1:mc0]),
                gate + [getattr(self, "t_halofix", None)])
            t_ev = None
            t_h = None
            t_hP = [None, None]
            t_gel = None
            t_gelP = [None, None]
            t_stg2 = None
            t_c12done = None
            t_w1free = [None, None]
            t_w2free = [None, None]
            t_repread = None
            t_sqw = None
            t_smm = None
            t_c2dma = None

            for mi, (m0, Nm) in enumerate(megas):
                nmb = (Nm + 511) // 512
                if mi + 1 < len(megas):
                    t_xb2 = s.add("vector", lambda v, m0=m0, Nm=Nm, mi=mi: v.tensor_copy(
                        xb[(mi + 1) % 2][:, :, :], sb["X"][:, :, m0 + Nm - 1:m0 + Nm]),
                        [t_ev])
                t_sq1 = s.add("scalar", lambda sc, m0=m0, Nm=Nm: sc.activation(
                    xt[:, 0:4, 0:Nm], sb["X"][:, 0:4, m0:m0 + Nm], AF.Square),
                    [t_smm, t_ev, t_repread, getattr(self, "t_halofix", None)])
                t_sq2 = s.add("scalar", lambda sc, m0=m0, Nm=Nm: sc.activation(
                    xt[:, 4:8, 1:Nm], sb["X"][:, 4:8, m0:m0 + Nm - 1], AF.Square), [t_sq1])
                t_sqw = s.add("scalar", lambda sc, mi=mi: sc.activation(
                    xt[:, 4:8, 0:1], xb[mi % 2][:, 4:8, :], AF.Square), [t_sq2, t_xb])
                mm = None
                for c in range(NCH):
                    last_c = (c == NCH - 1)
                    for hb in range(nmb):
                        lo = 512 * hb
                        hi = min(Nm, lo + 512)
                        if c < 4:
                            mm = s.add("tensor", lambda t, c=c, hb=hb, lo=lo, hi=hi, m0=m0, st=(c == 0): t.matmul(
                                P[4 + hb][0:1, 0:hi - lo], sb["ones_col"][:, :],
                                sb["X"][:, c, m0 + lo:m0 + hi],
                                start=st, stop=False),
                                ([t_stg2, t_ev, getattr(self, "t_halofix", None)]
                                 + (gate if mi == 0 else [])) if (c == 0 and hb == 0) else [])
                        else:
                            if hb == 0:
                                mm = s.add("tensor", lambda t, c=c, mi=mi: t.matmul(
                                    P[4][0:1, 0:1], sb["ones_col"][:, :],
                                    xb[mi % 2][:, c, :],
                                    start=False, stop=False), [t_xb])
                                mm = s.add("tensor", lambda t, c=c, hi=hi, m0=m0, sp=(c == NCH - 1): t.matmul(
                                    P[4][0:1, 1:hi], sb["ones_col"][:, :],
                                    sb["X"][:, c, m0:m0 + hi - 1],
                                    start=False, stop=sp), [])
                            else:
                                mm = s.add("tensor", lambda t, c=c, lo=lo, hi=hi, m0=m0, sp=(c == NCH - 1): t.matmul(
                                    P[4 + hb][0:1, 0:hi - lo], sb["ones_col"][:, :],
                                    sb["X"][:, c, m0 + lo - 1:m0 + hi - 1],
                                    start=False, stop=sp), [])
                for c in range(NCH):
                    for hb in range(nmb):
                        lo = 512 * hb
                        hi = min(Nm, lo + 512)
                        mm = s.add("tensor", lambda t, c=c, hb=hb, lo=lo, hi=hi, st=(c == 0), sp=(c == NCH - 1): t.matmul(
                            P[6 + hb][0:1, 0:hi - lo], sb["ones_col"][:, :],
                            xt[:, c, lo:hi], start=st, stop=sp),
                            [t_sqw] if c == 0 and hb == 0 else [])
                t_smm = mm
                r1 = s.add("vector", lambda v, Nm=Nm: v.tensor_scalar(
                    mrow_[0:1, 0:min(Nm, 512)], P[4][0:1, 0:min(Nm, 512)],
                    1.0 / HID, None, ALU.mult), [t_smm])
                if Nm > 512:
                    r1 = s.add("vector", lambda v, Nm=Nm: v.tensor_scalar(
                        mrow_[0:1, 512:Nm], P[5][0:1, 0:Nm - 512],
                        1.0 / HID, None, ALU.mult), [r1])
                r2 = s.add("vector", lambda v, Nm=Nm: v.tensor_scalar(
                    srow[0:1, 0:min(Nm, 512)], P[6][0:1, 0:min(Nm, 512)],
                    1.0 / HID, None, ALU.mult), [r1])
                if Nm > 512:
                    r2 = s.add("vector", lambda v, Nm=Nm: v.tensor_scalar(
                        srow[0:1, 512:Nm], P[7][0:1, 0:Nm - 512],
                        1.0 / HID, None, ALU.mult), [r2])
                r3 = r2
                for hb in range(nmb):
                    lo = 512 * hb
                    hi = min(Nm, lo + 512)
                    r3 = s.add("vector", lambda v, hb=hb, lo=lo, hi=hi: v.tensor_mul(
                        P[4 + hb][0:1, 0:hi - lo], mrow_[0:1, lo:hi], mrow_[0:1, lo:hi]), [r3])
                    r3 = s.add("vector", lambda v, hb=hb, lo=lo, hi=hi: v.tensor_sub(
                        srow[0:1, lo:hi], srow[0:1, lo:hi], P[4 + hb][0:1, 0:hi - lo]), [r3])
                r5 = s.add("scalar", lambda sc, Nm=Nm: sc.activation(
                    srow[0:1, 0:Nm], srow[0:1, 0:Nm], AF.Sqrt,
                    bias=sb["eps"][0:1, 0:1]), [r3])
                r6 = s.add("vector", lambda v, Nm=Nm: v.reciprocal(
                    srow[0:1, 0:Nm], srow[0:1, 0:Nm]), [r5])
                r7 = r6
                for hb in range(nmb):
                    lo = 512 * hb
                    hi = min(Nm, lo + 512)
                    r7 = s.add("vector", lambda v, hb=hb, lo=lo, hi=hi: v.tensor_mul(
                        P[4 + hb][0:1, 0:hi - lo], mrow_[0:1, lo:hi], srow[0:1, lo:hi]), [r7])
                c1_ = s.add("vector", lambda v, Nm=Nm: v.tensor_copy(
                    srowb[0:1, 0:Nm], srow[0:1, 0:Nm]), [r7])
                c2_ = c1_
                for hb in range(nmb):
                    lo = 512 * hb
                    hi = min(Nm, lo + 512)
                    c2_ = s.add("vector", lambda v, hb=hb, lo=lo, hi=hi: v.tensor_scalar(
                        misc2[0:1, lo:hi], P[4 + hb][0:1, 0:hi - lo], -1.0, None,
                        ALU.mult), [c2_, t_m2])
                rp1 = s.add("tensor", lambda t, Nm=Nm: t.matmul(
                    P[0][:, 0:min(Nm, 512)], sb["ones_row"][0:1, 0:128],
                    srowb[0:1, 0:min(Nm, 512)], start=True, stop=True),
                    [c1_, t_h, t_smm])
                rp2 = None
                if Nm > 512:
                    rp2 = s.add("tensor", lambda t, Nm=Nm: t.matmul(
                        P[1][:, 0:Nm - 512], sb["ones_row"][0:1, 0:128],
                        srowb[0:1, 512:Nm], start=True, stop=True), [c1_, t_h])
                x1_ = s.add("vector", lambda v, m0=m0, Nm=Nm: v.tensor_tensor(
                    xt[:, 0:4, 0:min(Nm, 512)], sb["X"][:, 0:4, m0:m0 + min(Nm, 512)],
                    bass.AP(P[0], 0, [[512, 128], [0, 4], [1, min(Nm, 512)]]),
                    ALU.mult), [rp1, t_sqw, t_smm])
                if Nm > 512:
                    x1_ = s.add("vector", lambda v, m0=m0, Nm=Nm: v.tensor_tensor(
                        xt[:, 0:4, 512:Nm], sb["X"][:, 0:4, m0 + 512:m0 + Nm],
                        bass.AP(P[1], 0, [[512, 128], [0, 4], [1, Nm - 512]]),
                        ALU.mult), [rp2, x1_])
                x2_ = s.add("vector", lambda v, m0=m0, Nm=Nm: v.tensor_tensor(
                    xt[:, 4:8, 1:min(Nm, 512)], sb["X"][:, 4:8, m0:m0 + min(Nm, 512) - 1],
                    bass.AP(P[0], 1, [[512, 128], [0, 4], [1, min(Nm, 512) - 1]]),
                    ALU.mult), [x1_])
                if Nm > 512:
                    x2_ = s.add("vector", lambda v, m0=m0, Nm=Nm: v.tensor_tensor(
                        xt[:, 4:8, 512:Nm], sb["X"][:, 4:8, m0 + 511:m0 + Nm - 1],
                        bass.AP(P[1], 0, [[512, 128], [0, 4], [1, Nm - 512]]),
                        ALU.mult), [x2_])
                x3_ = s.add("vector", lambda v, mi=mi: v.tensor_tensor(
                    xt[:, 4:8, 0:1], xb[mi % 2][:, 4:8, :],
                    bass.AP(P[0], 0, [[512, 128], [0, 4], [1, 1]]),
                    ALU.mult), [x2_, t_xb])
                t_xt = s.add("vector", lambda v, Nm=Nm: v.tensor_tensor(
                    xt[:, :, 0:Nm], xt[:, :, 0:Nm],
                    bass.AP(sb[f"lg_col{l}"], 0, [[NCH, 128], [1, NCH], [0, Nm]]),
                    ALU.mult), [x3_])
                if mi + 1 < len(megas):
                    t_xb = t_xb2

                for j in range(22):
                    sl = j % 2
                    la = s.dma("sync", w1a[sl][:, :, :], w1hb[:, :, 128 * j:128 * (j + 1)],
                               dw1a[sl], deps=[t_w1free[sl], self.t_wcast]
                               + (gate if (mi == 0 and j < 2) else []))
                    lg = s.dma("scalar", w1g[sl][:, :, :],
                               w1hb[:, :, HPAD + 128 * j:HPAD + 128 * (j + 1)],
                               dw1g[sl], deps=[t_w1free[sl], self.t_wcast])
                    if mi == 0:
                        for (wt, coff) in ((w1a[sl], 128 * j), (w1g[sl], HPAD + 128 * j)):
                            for fin in range(NCH):
                                mm = s.add("tensor", lambda t, wt=wt, fin=fin, st=(fin == 0), sp=(fin == NCH - 1): t.matmul(
                                    P[6][0:1, 0:128], sb[f"lg_col{l}"][:, fin:fin + 1],
                                    wt[:, fin, :], start=st, stop=sp),
                                    [la, lg, t_gel, t_xt, t_c12done] if fin == 0 else [])
                            for fin in range(NCH):
                                mm = s.add("tensor", lambda t, wt=wt, fin=fin, st=(fin == 0), sp=(fin == NCH - 1): t.matmul(
                                    P[7][0:1, 0:128], sb[f"lb_col{l}"][:, fin:fin + 1],
                                    wt[:, fin, :], start=st, stop=sp), [])
                            ec1 = s.add("vector", lambda v, coff=coff: v.tensor_copy(
                                c12[0:1, coff:coff + 128], P[6][0:1, 0:128]), [mm])
                            ec2 = s.add("vector", lambda v, coff=coff: v.tensor_add(
                                c2row[0:1, coff:coff + 128], c2row[0:1, coff:coff + 128],
                                P[7][0:1, 128:256] if False else P[7][0:1, 0:128]), [ec1, t_b1])
                            t_c12done = ec2
                    jb = 4 * (j % 2)
                    for (banks, wt, dep0) in (((P[jb + 0], P[jb + 1]), w1a[sl],
                                               [la, t_xt, t_hP[j % 2], t_stg2]),
                                              ((P[jb + 2], P[jb + 3]), w1g[sl],
                                               [lg, t_gelP[j % 2]])):
                        for half in range(2 if Nm > 512 else 1):
                            nn = min(Nm - 512 * half, 512)
                            for fin in range(NCH):
                                deps = list(dep0) + ([t_c12done] if mi == 0 else []) if (half == 0 and fin == 0) else []
                                mm = s.add("tensor", lambda t, banks=banks, wt=wt, half=half, fin=fin, nn=nn, st=(fin == 0): t.matmul(
                                    banks[half][:, 0:nn], wt[:, fin, :],
                                    xt[:, fin, 512 * half:512 * half + nn],
                                    start=st, stop=False), deps)
                            if mi == 0:
                                coff = (128 * j) if wt is w1a[sl] else (HPAD + 128 * j)
                                mm = s.add("tensor", lambda t, banks=banks, half=half, nn=nn, coff=coff: t.matmul(
                                    banks[half][:, 0:nn], c12[0:1, coff:coff + 128],
                                    misc2[0:1, 512 * half:512 * half + nn],
                                    start=False, stop=False), [t_c12done, c2_])
                                mm = s.add("tensor", lambda t, banks=banks, half=half, nn=nn, coff=coff: t.matmul(
                                    banks[half][:, 0:nn], c2row[0:1, coff:coff + 128],
                                    sb["ones_row"][0:1, 0:nn],
                                    start=False, stop=True), [t_c12done])
                            else:
                                coff = (128 * j) if wt is w1a[sl] else (HPAD + 128 * j)
                                mm = s.add("tensor", lambda t, banks=banks, half=half, nn=nn, coff=coff: t.matmul(
                                    banks[half][:, 0:nn], c12[0:2, coff:coff + 128],
                                    misc2[0:2, 512 * half:512 * half + nn],
                                    start=False, stop=True), [t_c2dma, c2_])
                    t_ag = mm
                    t_w1free[sl] = t_ag
                    tg = None
                    for half in range(2 if Nm > 512 else 1):
                        nn = min(Nm - 512 * half, 512)
                        tg = s.add("scalar", lambda sc, jb=jb, half=half, nn=nn: sc.activation(
                            gout[:, 512 * half:512 * half + nn],
                            P[jb + 2 + half][:, 0:nn], AF.Gelu), [t_ag, t_h] if half == 0 else [tg])
                    t_gel = tg
                    t_gelP[j % 2] = tg
                    th = None
                    for half in range(2 if Nm > 512 else 1):
                        nn = min(Nm - 512 * half, 512)
                        th = s.add("vector", lambda v, j=j, jb=jb, half=half, nn=nn: v.scalar_tensor_tensor(
                            h_buf[:, j, 512 * half:512 * half + nn],
                            P[jb + half][:, 0:nn], 0.0, gout[:, 512 * half:512 * half + nn],
                            ALU.bypass, ALU.mult), [t_gel] if half == 0 else [th])
                    t_h = th
                    t_hP[j % 2] = th
                if mi == 0:
                    t_c2dma = s.dma("sync", c12[1:2, :], c2row[0:1, :],
                                    s.new_dma_sem(f"c2d{l}", group=True), deps=[t_c12done])

                for half in range(2 if Nm > 512 else 1):
                    nn = min(Nm - 512 * half, 512)
                    for j in range(22):
                        sl = j % 2
                        l2 = s.dma("sync" if sl == 0 else "scalar", w2b[sl][:, :],
                                   self.w2bf[l][128 * j:128 * (j + 1), :], dw2[sl],
                                   deps=[t_w2free[sl], self.t_wcast]
                                   + (gate if (mi == 0 and half == 0 and j < 2) else []))
                        for c in range(NCH):
                            deps = []
                            if c == 0 and j == 0:
                                deps = [l2, t_h, t_ev]
                            elif c == 0:
                                deps = [l2]
                            mm = s.add("tensor", lambda t, c=c, j=j, sl=sl, half=half, nn=nn, st=(j == 0), sp=(j == 21): t.matmul(
                                P[c][:, 0:nn], w2b[sl][:, 128 * c:128 * (c + 1)],
                                h_buf[:, j, 512 * half:512 * half + nn],
                                start=st, stop=sp), deps)
                        t_w2free[sl] = mm
                    t_stg2 = mm
                    ev = None
                    for c in range(NCH):
                        ev = s.add("vector", lambda v, c=c, half=half, nn=nn, m0=m0: v.scalar_tensor_tensor(
                            sb["X"][:, c, m0 + 512 * half:m0 + 512 * half + nn],
                            P[c][:, 0:nn], sb[f"b2_col{l}"][:, c:c + 1],
                            sb["X"][:, c, m0 + 512 * half:m0 + 512 * half + nn],
                            ALU.add, ALU.add), [t_stg2] if c == 0 else [ev])
                    if m0 + 512 * half <= HALO - 1 < m0 + 512 * half + nn:
                        ev = s.add("vector", lambda v: v.tensor_mul(
                            sb["X"][:, :, HALO - 1:HALO], sb["X"][:, :, HALO - 1:HALO],
                            bass.AP(sb["halo01"], 0, [[1, 128], [0, NCH], [0, 1]])), [ev])
                        self.t_halofix = ev
                    t_ev = ev
                t_repread = t_ev

    def build_final(self):
        s, nc, sb, inp = self.s, self.nc, self.sb, self.inp
        gate = self.gate()
        dout = [s.new_dma_sem(f"out{i}") for i in range(2)]
        dx1r = [s.new_dma_sem(f"x1r{i}") for i in range(2)]
        with ExitStack() as es:
            x1t = [self.sbuf(f"x1t{i}", (128, NCH, 128), BF16, es) for i in range(2)]
            tsb = self.sbuf("fin_t", (128, NCH, 128), F32, es)
            tsbb = self.sbuf("fin_tb", (128, NCH, 128), BF16, es)
            tq = self.sbuf("fin_q", (128, NCH, 128), BF16, es)
            of = [self.sbuf(f"fin_o{i}", (128, NCH, 128), F32, es) for i in range(2)]
            frow = [self.sbuf(f"frow{i}", (1, 128), F32, es) for i in range(4)]
            onesf = self.sbuf("onesf_s", (1, 128), F32, es)
            P = [self.psum(f"F{i}", (128, 512), F32, es) for i in range(4)]

            t_of = s.add("gpsimd", lambda gg: gg.memset(onesf[:, :], 1.0), gate)
            t_add = None
            t_sq = None
            t_norm = [None, None]
            t_store = [None, None]
            t_rows = None
            t_rep = None
            for ti in range(32):
                col = HALO + 128 * ti
                sl = ti % 2
                ld = s.dma("sync", x1t[sl][:, :, :], self.x1b[:, :, 128 * ti:128 * (ti + 1)],
                           dx1r[sl], deps=[t_norm[sl]] + (gate if ti < 2 else []))
                ta = s.add("vector", lambda v, col=col, sl=sl: v.tensor_add(
                    tsb[:, :, :], sb["X"][:, :, col:col + 128], x1t[sl][:, :, :]),
                    [ld, t_sq] + (gate if ti == 0 else []))
                tcb = s.add("vector", lambda v: v.tensor_copy(
                    tsbb[:, :, :], tsb[:, :, :]), [ta])
                t_sq = s.add("scalar", lambda sc: sc.activation(
                    tq[:, :, :], tsb[:, :, :], AF.Square), [ta, t_rows])
                mm = None
                for c in range(NCH):
                    mm = s.add("tensor", lambda t, c=c, st=(c == 0), sp=(c == NCH - 1): t.matmul(
                        P[0][0:1, 0:128], sb["ones_col"][:, :], tsbb[:, c, :],
                        start=st, stop=sp), [tcb, t_rows] if c == 0 else [])
                for c in range(NCH):
                    mm = s.add("tensor", lambda t, c=c, st=(c == 0), sp=(c == NCH - 1): t.matmul(
                        P[1][0:1, 0:128], sb["ones_col"][:, :], tq[:, c, :],
                        start=st, stop=sp), [t_sq] if c == 0 else [])
                r1 = s.add("vector", lambda v: v.tensor_scalar(
                    frow[0][0:1, :], P[0][0:1, 0:128], 1.0 / HID, None, ALU.mult), [mm])
                r2 = s.add("vector", lambda v: v.tensor_scalar(
                    frow[1][0:1, :], P[1][0:1, 0:128], 1.0 / HID, None, ALU.mult), [r1])
                r3 = s.add("vector", lambda v: v.tensor_mul(
                    frow[2][0:1, :], frow[0][0:1, :], frow[0][0:1, :]), [r2])
                r4 = s.add("vector", lambda v: v.tensor_sub(
                    frow[1][0:1, :], frow[1][0:1, :], frow[2][0:1, :]), [r3])
                r5 = s.add("scalar", lambda sc: sc.activation(
                    frow[1][0:1, :], frow[1][0:1, :], AF.Sqrt, bias=sb["eps"][0:1, 0:1]), [r4])
                r6 = s.add("vector", lambda v: v.reciprocal(
                    frow[1][0:1, :], frow[1][0:1, :]), [r5])
                r7 = s.add("vector", lambda v: v.tensor_mul(
                    frow[2][0:1, :], frow[0][0:1, :], frow[1][0:1, :]), [r6])
                t_rows = r7
                m1 = s.add("tensor", lambda t: t.matmul(
                    P[2][:, 0:128], onesf[0:1, 0:128], frow[1][0:1, :],
                    start=True, stop=True), [r6, t_rep, t_of])
                m2 = s.add("tensor", lambda t: t.matmul(
                    P[3][:, 0:128], onesf[0:1, 0:128], frow[2][0:1, :],
                    start=True, stop=True), [r7])
                t_rep = m2
                n1 = s.add("vector", lambda v, sl=sl: v.tensor_tensor(
                    of[sl][:, :, :], tsb[:, :, :],
                    bass.AP(P[2], 0, [[512, 128], [0, NCH], [1, 128]]), ALU.mult),
                    [m1, t_store[sl]])
                t_norm[sl] = s.add("vector", lambda v, sl=sl: v.tensor_sub(
                    of[sl][:, :, :],
                    of[sl][:, :, :],
                    bass.AP(P[3], 0, [[512, 128], [0, NCH], [1, 128]])), [n1, m2])
                t_rep = m2
                t_store[sl] = s.dma(
                    "sync", self.outT.rearrange("(c p) t -> p c t", p=128)[:, :, 128 * ti:128 * (ti + 1)],
                    of[sl][:, :, :], dout[sl], deps=[t_norm[sl]])
            def fin(sy):
                sy.wait_ge(dout[0].sem, dout[0].count)
                sy.wait_ge(dout[1].sem, dout[1].count)
                return None
            s.add("sync", fin)


_LAST_SCHED = None


def build_program(upto="all", mlp0=True, mlp1=True):
    global _LAST_SCHED
    nc = bass.Bass(trn_type="TRN2", target_bir_lowering=False)
    ctx = ExitStack()
    kb = KB(nc, ctx)
    _LAST_SCHED = kb.s
    with ctx:
        kb.build_io()
        kb.build_consts()
        with ExitStack() as es_kv:
            kb.es_kv = es_kv
            if upto != "consts":
                kb.build_p1()
            if upto not in ("consts", "p1"):
                kb.build_p2()
        if upto in ("L0", "L1", "all"):
            kb.build_layer(0, do_mlp=mlp0)
        if upto in ("L1", "all"):
            kb.build_layer(1, do_mlp=mlp1)
        if upto == "all":
            kb.build_final()
        else:
            gate = kb.gate()
            dsem = kb.s.new_dma_sem("dbgout", group=True)
            t = kb.s.dma("gpsimd", kb.outT.rearrange("(c p) t -> p c t", p=128)[:, :, 0:128],
                         kb.sb["X"][:, :, HALO:HALO + 128], dsem, deps=gate)
            def fin(sy):
                sy.wait_ge(dsem.sem, dsem.count)
                return None
            kb.s.add("sync", fin)
        kb.s.setup_sems(ctx)
        kb.s.emit()
    return nc


def kernel(**inputs):
    nc = build_program()
    in_maps = [make_core_inputs(inputs, c) for c in range(8)]
    res = run_bass_kernel_spmd(nc, in_maps, core_ids=list(range(8)))
    out = np.empty((4, 8192, 1024), np.float32)
    for c in range(8):
        b, h = c // 2, c % 2
        out[b, 4096 * h:4096 * (h + 1), :] = res.results[c]["outT"].T
    return out


# version 15
# speedup vs baseline: 1.0261x; 1.0261x over previous
import numpy as np
import ml_dtypes
import concourse.bass as bass
import concourse.mybir as mybir
from concourse.bass_utils import run_bass_kernel_spmd

F32 = mybir.dt.float32
BF16 = mybir.dt.bfloat16
AF = mybir.ActivationFunctionType
ALU = mybir.AluOpType
AX = mybir.AxisListType
BF = ml_dtypes.bfloat16

ENGINES = ("tensor", "vector", "scalar", "gpsimd", "sync")


class _Op:
    __slots__ = ("engine", "fn", "deps", "idx", "dma_sem", "value", "marked", "label")

    def __init__(self, engine, fn, deps, idx, dma_sem=None):
        self.engine = engine
        self.fn = fn
        self.deps = deps
        self.idx = idx
        self.dma_sem = dma_sem
        self.value = None
        self.marked = dma_sem is not None


class DmaSem:
    __slots__ = ("name", "sem", "count", "group")

    def __init__(self, name, group=False):
        self.name = name
        self.sem = None
        self.count = 0
        self.group = group


class Sched:
    def __init__(self, nc):
        self.nc = nc
        self.ops = {e: [] for e in ENGINES}
        self.all_ops = []
        self.sems = {}
        self.dma_sems = []

    def new_dma_sem(self, name, group=False):
        d = DmaSem(name, group)
        self.dma_sems.append(d)
        return d

    def setup_sems(self, ctx):
        for e in ENGINES:
            self.sems[e] = ctx.enter_context(self.nc.semaphore(f"chain_{e}"))
        for d in self.dma_sems:
            d.sem = ctx.enter_context(self.nc.semaphore(f"dma_{d.name}"))

    def add(self, engine, fn, deps=()):
        deps = tuple(d for d in deps if d is not None)
        op = _Op(engine, fn, deps, len(self.ops[engine]))
        self.ops[engine].append(op)
        self.all_ops.append(op)
        return op

    def dma(self, engine, out, in_, dsem, deps=()):
        deps = tuple(d for d in deps if d is not None)

        def fn(eng):
            return eng.dma_start(out=out, in_=in_)

        op = _Op(engine, fn, deps, len(self.ops[engine]), dma_sem=dsem)
        dsem.count += 16
        op.value = dsem.count
        self.ops[engine].append(op)
        self.all_ops.append(op)
        return op

    def barrier_token(self, engine):
        ops = self.ops[engine]
        return ops[-1] if ops else None

    def all_tokens(self):
        return [self.barrier_token(e) for e in ENGINES if self.ops[e]]

    def emit(self):
        for op in self.all_ops:
            for d in op.deps:
                d.marked = True
        counts = {e: 0 for e in ENGINES}
        for e in ENGINES:
            for op in self.ops[e]:
                if op.marked and op.dma_sem is None:
                    counts[e] += 1
                    op.value = counts[e]
        self.final_counts = dict(counts)
        self.inst_map = {}
        nc, sems, ops = self.nc, self.sems, self.ops
        with nc.Block() as block:
            def mk(e):
                def body(eng):
                    seen = {}
                    for op in ops[e]:
                        for d in op.deps:
                            if d.value is None:
                                raise RuntimeError(f"unmarked dep on {e} #{op.idx}")
                            if d.dma_sem is not None:
                                key, sem = d.dma_sem, d.dma_sem.sem
                                val = d.dma_sem.count if d.dma_sem.group else d.value
                            else:
                                key, sem, val = d.engine, sems[d.engine], d.value
                            if val > seen.get(key, 0):
                                eng.wait_ge(sem, val)
                                seen[key] = val
                        inst = op.fn(eng)
                        try:
                            self.inst_map[inst.ins.name] = (e, op.idx, getattr(op, 'label', None))
                        except Exception:
                            pass
                        if op.dma_sem is not None:
                            inst.then_inc(op.dma_sem.sem, 16)
                        elif op.marked:
                            inst.then_inc(sems[e], 1)
                    return None
                return body
            for e in ENGINES:
                getattr(block, e)(mk(e))


HID = 1024
NCH = 8
HALO = 1152
OW = 4096
TEXT = HALO + OW
NT_EXT = TEXT // 128
NT_REST = 4096 // 128
HEADS_A = 16
DA = 64
HEADS_F = 8
DF = 64
WIN = 512
INNER = 512
HPAD = 2816
H2 = 2 * HPAD
MEGA = 1024

L_CFG = [
    dict(blk0_col=128, nblk=10, mlp_col0=512, mlp_ntok=4736),
    dict(blk0_col=1152, nblk=8, mlp_col0=1152, mlp_ntok=4096),
]

NEG = -1.0e30
DBG_TAP = False
P2STOP = 99


def _perm_head_major():
    return np.array([d * 16 + h for h in range(16) for d in range(64)], dtype=np.int64)


def _rotary_tables():
    D = 64
    half = np.arange(0, D, 2, dtype=np.float64)
    inv_freq = 1.0 / 10000.0 ** (half / D)
    t = np.arange(8192, dtype=np.float64)
    freqs = t[:, None] * inv_freq[None, :]
    freqs = np.concatenate([freqs, freqs], axis=1)
    base = (half + 0.4 * D) / (1.4 * D)
    power = (t - 4096.0) / float(WIN // 2)
    scale = base[None, :] ** power[:, None]
    scale = np.concatenate([scale, scale], axis=1)
    cos = np.cos(freqs)
    sin = np.sin(freqs)
    cs = cos * scale
    csk = cos / scale
    sgn = np.concatenate([-np.ones(32), np.ones(32)])[None, :]
    ssn = sin * scale * sgn
    sskn = sin / scale * sgn
    return (cs.astype(np.float32), ssn.astype(np.float32),
            csk.astype(np.float32), sskn.astype(np.float32))


def _local_masks():
    qq = np.arange(128)[:, None]
    kk = np.arange(128)[None, :]
    mq_lo = (kk < qq).astype(np.float32)
    mq_hi = (kk > qq).astype(np.float32)
    return mq_lo, mq_hi, mq_lo.T.copy(), mq_hi.T.copy()


def make_core_inputs(inputs, core):
    b, h = core // 2, core % 2
    s = 4096 * h
    perm = _perm_head_major()
    x = np.asarray(inputs["L0_attn_x"])
    xT = np.ascontiguousarray(x[b].T)

    x_extT = np.zeros((HID, TEXT), np.float32)
    lo = s - HALO
    src_lo = max(lo, 0)
    x_extT[:, src_lo - lo:] = xT[:, src_lo:s + OW]

    x_restT = np.zeros((HID, 4096), np.float32)
    if h == 0:
        x_restT[:, :] = xT[:, 4096:8192]
        rest_valid = 4096
    else:
        x_restT[:, :2944] = xT[:, 0:2944]
        rest_valid = 2944

    ext_valid = np.zeros(TEXT, np.float32)
    ext_valid[HALO - min(HALO, s):] = 1.0
    if h == 1:
        ext_valid[:] = 1.0
    mask_ext_col = ext_valid.reshape(NT_EXT, 128).T.copy()
    rv = np.zeros(4096, np.float32)
    rv[:rest_valid] = 1.0
    mask_rest_col = rv.reshape(NT_REST, 128).T.copy()

    mrow = np.where(ext_valid > 0, 0.0, NEG).astype(np.float32)
    maskrow_col = mrow.reshape(NT_EXT, 128).T.copy()

    d = {
        "x_extT": x_extT, "x_restT": x_restT,
        "mask_ext_col": mask_ext_col, "mask_rest_col": mask_rest_col,
        "maskrow_col": maskrow_col,
    }

    d["Wh"] = np.ascontiguousarray(np.asarray(inputs["L0_attn_Wh"])[:, perm])
    d["Wq"] = np.ascontiguousarray(np.asarray(inputs["L0_attn_Wq"])[:, perm])
    d["Wk"] = np.ascontiguousarray(np.asarray(inputs["L0_attn_Wk"])[:, perm])
    d["Wv"] = np.ascontiguousarray(np.asarray(inputs["L0_attn_Wv"])[:, perm])
    d["Wo"] = np.ascontiguousarray(np.asarray(inputs["L0_attn_Wo"])[perm, :])
    d["bh_row"] = np.asarray(inputs["L0_attn_bh"])[perm][None, :].copy()
    d["beta_row"] = np.asarray(inputs["L0_attn_beta"])[perm][None, :].copy()
    d["bk_row"] = np.asarray(inputs["L0_attn_bk"])[perm][None, :].copy()
    d["bv_row"] = np.asarray(inputs["L0_attn_bv"])[perm][None, :].copy()
    d["bq_col"] = np.ascontiguousarray(
        np.asarray(inputs["L0_attn_bq"])[perm].reshape(NCH, 128).T)
    d["bo_col"] = np.ascontiguousarray(
        np.asarray(inputs["L0_attn_bo"]).reshape(NCH, 128).T)
    d["lnw_rep"] = np.ascontiguousarray(
        np.broadcast_to(np.asarray(inputs["L0_attn_ln_w"])[perm], (128, HID)))
    d["lnb_rep"] = np.ascontiguousarray(
        np.broadcast_to(np.asarray(inputs["L0_attn_ln_b"])[perm], (128, HID)))

    w1 = np.asarray(inputs["L0_ffn_w1"])
    b1 = np.asarray(inputs["L0_ffn_b1"])
    w2 = np.asarray(inputs["L0_ffn_w2"])
    for l in range(2):
        d[f"wqkv{l}"] = np.ascontiguousarray(np.asarray(inputs["L0_ffn_wqkv"])[l])
        d[f"wo{l}"] = np.ascontiguousarray(np.asarray(inputs["L0_ffn_wo"])[l])
        w1p = np.zeros((HID, H2), np.float32)
        w1p[:, 0:2730] = w1[l][:, 0:2730]
        w1p[:, HPAD:HPAD + 2730] = w1[l][:, 2730:5460]
        d[f"w1{l}"] = w1p
        b1p = np.zeros((1, H2), np.float32)
        b1p[0, 0:2730] = b1[l][0:2730]
        b1p[0, HPAD:HPAD + 2730] = b1[l][2730:5460]
        d[f"b1row{l}"] = b1p
        w2p = np.zeros((HPAD, HID), np.float32)
        w2p[0:2730, :] = w2[l]
        d[f"w2{l}"] = w2p
        d[f"lg_col{l}"] = np.ascontiguousarray(
            np.asarray(inputs["L0_ffn_ln_g"])[l].reshape(NCH, 128).T)
        d[f"lb_col{l}"] = np.ascontiguousarray(
            np.asarray(inputs["L0_ffn_ln_b"])[l].reshape(NCH, 128).T)
        d[f"b2_col{l}"] = np.ascontiguousarray(
            np.asarray(inputs["L0_ffn_b2"])[l].reshape(NCH, 128).T)
        d[f"qs_rep{l}"] = np.ascontiguousarray(np.broadcast_to(
            np.tile(np.asarray(inputs["L0_ffn_q_scale"])[l], HEADS_F), (128, INNER)))
        d[f"ks_rep{l}"] = np.ascontiguousarray(np.broadcast_to(
            np.tile(np.asarray(inputs["L0_ffn_k_scale"])[l], HEADS_F), (128, INNER)))

    cs, ssn, csk, sskn = _rotary_tables()
    pos = np.clip(np.arange(s - HALO, s + OW), 0, 8191)
    d["tab"] = np.ascontiguousarray(
        np.stack([cs[pos], ssn[pos], csk[pos], sskn[pos]], axis=1)
    ).astype(np.float32)
    mq_lo, mq_hi, mk_lo, mk_hi = _local_masks()
    d["mq_lo"], d["mq_hi"], d["mk_lo"], d["mk_hi"] = mq_lo, mq_hi, mk_lo, mk_hi
    d["ident"] = np.eye(128, dtype=np.float32)
    o2 = np.zeros((2, MEGA), np.float32)
    o2[1, :] = 1.0
    d["ones2row"] = o2
    d["negI"] = (NEG * np.eye(128)).astype(np.float32)
    hs = np.zeros((8, 8, 128), np.float32)
    for hh in range(8):
        hs[hh, hh, :] = 1.0
    d["hsel"] = hs
    e4 = np.zeros((1, 16), np.float32)
    for c in range(4):
        e4[0, 4 * c + c] = 1.0
    d["e4"] = e4
    d["halo01"] = np.full((128, 1), 1.0 if s > 0 else 0.0, np.float32)
    return d



from contextlib import ExitStack


class KB:

    def __init__(self, nc, ctx):
        self.nc = nc
        self.ctx = ctx
        self.s = Sched(nc)
        self.inp = {}
        self.sb = {}

    def din(self, name, shape):
        ap = self.nc.dram_tensor(name, list(shape), F32, kind="ExternalInput").ap()
        self.inp[name] = ap
        return ap

    def sbuf(self, name, shape, dtype=BF16, ctx=None):
        return (ctx or self.ctx).enter_context(
            self.nc.sbuf_tensor(name, list(shape), dtype))

    def psum(self, name, shape, dtype=F32, ctx=None):
        return (ctx or self.ctx).enter_context(
            self.nc.psum_tensor(name, list(shape), dtype))

    def gate(self):
        return [t for t in (self.s.barrier_token(e) for e in ENGINES) if t is not None]

    def build_io(self):
        d = self.din
        d("x_extT", (HID, TEXT))
        d("x_restT", (HID, 4096))
        d("mask_ext_col", (128, NT_EXT))
        d("mask_rest_col", (128, NT_REST))
        d("maskrow_col", (128, NT_EXT))
        for w in ("Wh", "Wq", "Wk", "Wv", "Wo"):
            d(w, (HID, HID))
        for r in ("bh_row", "beta_row", "bk_row", "bv_row"):
            d(r, (1, HID))
        d("bq_col", (128, NCH))
        d("bo_col", (128, NCH))
        d("lnw_rep", (128, HID))
        d("lnb_rep", (128, HID))
        for l in range(2):
            d(f"wqkv{l}", (HID, 3 * INNER))
            d(f"wo{l}", (INNER, HID))
            d(f"w1{l}", (HID, H2))
            d(f"b1row{l}", (1, H2))
            d(f"w2{l}", (HPAD, HID))
            d(f"lg_col{l}", (128, NCH))
            d(f"lb_col{l}", (128, NCH))
            d(f"b2_col{l}", (128, NCH))
            d(f"qs_rep{l}", (128, INNER))
            d(f"ks_rep{l}", (128, INNER))
        d("tab", (TEXT, 4, DF))
        d("ones2row", (2, MEGA))
        d("halo01", (128, 1))
        d("e4", (1, 16))
        d("hsel", (8, 8, 128))
        for m in ("mq_lo", "mq_hi", "mk_lo", "mk_hi", "ident", "negI"):
            d(m, (128, 128))
        self.outT = self.nc.dram_tensor("outT", [HID, OW], F32, kind="ExternalOutput").ap()
        self.dbg_qf = self.nc.dram_tensor("dbg_qf", [128, 4, 512], F32, kind="ExternalOutput").ap()
        self.dbg_kf = self.nc.dram_tensor("dbg_kf", [128, 4, 512], F32, kind="ExternalOutput").ap()
        self.dbg_vt = self.nc.dram_tensor("dbg_vt", [128, 4, 520], F32, kind="ExternalOutput").ap()
        self.dbg_m = self.nc.dram_tensor("dbg_m", [8, TEXT], F32, kind="ExternalOutput").ap()
        self.dbg_o = self.nc.dram_tensor("dbg_o", [128, 4, 512], F32, kind="ExternalOutput").ap()
        self.dbg_kf2 = self.nc.dram_tensor("dbg_kf2", [128, 4, 512], F32, kind="ExternalOutput").ap()
        self.dbg_vt2 = self.nc.dram_tensor("dbg_vt2", [128, 4, 520], F32, kind="ExternalOutput").ap()
        self.x1b = self.nc.dram_tensor("x1b", [128, NCH, OW], BF16, kind="Internal").ap()
        self.w1bf = [self.nc.dram_tensor(f"w1bf{l}", [HID, H2], BF16, kind="Internal").ap()
                     for l in range(2)]
        self.w2bf = [self.nc.dram_tensor(f"w2bf{l}", [HPAD, HID], BF16, kind="Internal").ap()
                     for l in range(2)]

    def build_consts(self):
        s, sb, inp = self.s, self.sb, self.inp
        for n, shp, dt in [
            ("ones_row", (1, 640), BF16), ("ones_col", (128, 1), BF16),
            ("identF", (128, 128), F32),
            ("negI", (128, 128), BF16),
            ("mk_lo", (128, 128), BF16), ("mk_hi", (128, 128), BF16),
            ("eps", (128, 1), F32), ("maskrow", (128, NT_EXT), F32),
            ("zrow", (1, 512), BF16),
            ("X", (128, NCH, TEXT), BF16),
            ("lg_col0", (128, NCH), BF16), ("lb_col0", (128, NCH), BF16),
            ("lg_col1", (128, NCH), BF16), ("lb_col1", (128, NCH), BF16),
            ("b2_col0", (128, NCH), F32), ("b2_col1", (128, NCH), F32),
            ("halo01", (128, 1), F32), ("e4", (1, 16), F32),
        ]:
            sb[n] = self.sbuf(n + "_g", shp, dt)

        dco = s.new_dma_sem("consts", group=True)
        g = "gpsimd"
        for m in ("negI", "mk_lo", "mk_hi"):
            s.dma(g, sb[m][:, :], inp[m][:, :], dco)
        s.dma(g, sb["identF"][:, :], inp["ident"][:, :], dco)
        s.dma(g, sb["maskrow"][:, :], inp["maskrow_col"][:, :], dco)
        s.dma(g, sb["halo01"][:, :], inp["halo01"][:, :], dco)
        s.dma(g, sb["e4"][:, :], inp["e4"][:, :], dco)
        for l in range(2):
            s.dma(g, sb[f"lg_col{l}"][:, :], inp[f"lg_col{l}"][:, :], dco)
            s.dma(g, sb[f"lb_col{l}"][:, :], inp[f"lb_col{l}"][:, :], dco)
            s.dma(g, sb[f"b2_col{l}"][:, :], inp[f"b2_col{l}"][:, :], dco)
        t_cd = s.ops[g][-1]
        s.add(g, lambda gg: gg.memset(sb["ones_row"][0:1, 0:640], 1.0), [t_cd])
        s.add(g, lambda gg: gg.memset(sb["ones_col"][:, :], 1.0))
        s.add(g, lambda gg: gg.memset(sb["zrow"][:, :], 0.0))
        t_ms = s.add(g, lambda gg: gg.memset(sb["eps"][:, :], 1e-5))
        self.d_consts = dco
        self.t_consts = t_ms

        dX = s.new_dma_sem("xload", group=True)
        xin = inp["x_extT"].rearrange("(c p) t -> p c t", p=128)
        tx = None
        for i in range(4):
            tx = s.dma(g, sb["X"][:, 2 * i:2 * i + 2, :], xin[:, 2 * i:2 * i + 2, :], dX)
        self.t_X = tx

        dwc = s.new_dma_sem("wcast", group=True)
        tw = None
        for l in range(2):
            for i in range(4):
                tw = s.dma(g, self.w1bf[l][:, i * (H2 // 4):(i + 1) * (H2 // 4)],
                           inp[f"w1{l}"][:, i * (H2 // 4):(i + 1) * (H2 // 4)], dwc)
            for i in range(4):
                tw = s.dma(g, self.w2bf[l][i * (HPAD // 4):(i + 1) * (HPAD // 4), :],
                           inp[f"w2{l}"][i * (HPAD // 4):(i + 1) * (HPAD // 4), :], dwc)
        self.t_wcast = tw

    def build_p1(self):
        s, nc, sb, inp = self.s, self.nc, self.sb, self.inp
        sb["kv_sb"] = self.sbuf("kv_sb", (128, 1040), BF16, self.es_kv)
        sb["kv2"] = self.sbuf("kv2", (128, 1040), BF16, self.es_kv)
        with ExitStack() as es:
            Wk = self.sbuf("WkSb", (128, NCH, HID), BF16, es)
            Wv = self.sbuf("WvSb", (128, NCH, HID), BF16, es)
            xr = [self.sbuf(f"xr{i}", (128, NCH, 128), BF16, es) for i in range(2)]
            ktile = self.sbuf("ktile", (128, HID), BF16, es)
            brow1 = self.sbuf("brow1", (1, 2 * HID), BF16, es)
            vtile = self.sbuf("vtile", (128, 16 * 65), BF16, es)
            sb["mask_ext"] = self.sbuf("mask_ext_s", (128, NT_EXT), F32, es)
            sb["mask_rest"] = self.sbuf("mask_rest_s", (128, NT_REST), F32, es)
            kps = [self.psum(f"kps{i}", (128, 512), F32, es) for i in range(2)]
            vps = [self.psum(f"vps{i}", (128, 512), F32, es) for i in range(2)]
            kvp = [self.psum(f"kvp{i}", (64, 512), F32, es) for i in range(3)]

            dW = s.new_dma_sem("p1w", group=True)
            s.dma("gpsimd", sb["mask_ext"][:, :], inp["mask_ext_col"][:, :], dW)
            s.dma("gpsimd", sb["mask_rest"][:, :], inp["mask_rest_col"][:, :], dW)
            s.dma("gpsimd", brow1[0:1, 0:HID], inp["bk_row"][:, :], dW)
            s.dma("gpsimd", brow1[0:1, HID:2 * HID], inp["bv_row"][:, :], dW)
            wk_in = inp["Wk"].rearrange("(c p) o -> p c o", p=128)
            wv_in = inp["Wv"].rearrange("(c p) o -> p c o", p=128)
            tw = None
            for i in range(4):
                s.dma("gpsimd", Wk[:, 2 * i:2 * i + 2, :], wk_in[:, 2 * i:2 * i + 2, :], dW)
                tw = s.dma("gpsimd", Wv[:, 2 * i:2 * i + 2, :], wv_in[:, 2 * i:2 * i + 2, :], dW)
            vones = s.add("vector", lambda v: v.memset(
                bass.AP(vtile, 64, [[16 * 65, 128], [65, 16]]), 1.0), [])

            xrest_in = inp["x_restT"].rearrange("(c p) t -> p c t", p=128)
            dxr = [s.new_dma_sem(f"xr{i}") for i in range(2)]

            t_kevac = t_vevac = t_kmask = t_vcopy = t_kvmm = None
            t_xr_free = [None, None]
            started = {}
            n_total = NT_EXT + NT_REST

            for it in range(n_total):
                is_ext = it < NT_EXT
                ti = it if is_ext else it - NT_EXT
                if is_ext:
                    def xsl(c, ti=ti):
                        return sb["X"][:, c, 128 * ti:128 * (ti + 1)]
                    xdep = self.t_X
                    mcol = sb["mask_ext"][:, ti:ti + 1]
                else:
                    slot = ti % 2
                    ld = s.dma("gpsimd", xr[slot][:, :, :],
                               xrest_in[:, :, 128 * ti:128 * (ti + 1)], dxr[slot],
                               deps=[t_xr_free[slot]])

                    def xsl(c, slot=slot):
                        return xr[slot][:, c, :]
                    xdep = ld
                    mcol = sb["mask_rest"][:, ti:ti + 1]

                mm = None
                for half in range(2):
                    for c in range(NCH):
                        deps = [xdep, tw, t_kevac, t_kmask, self.t_consts] if (half == 0 and c == 0) else []
                        mm = s.add("tensor", lambda t, c=c, half=half, xs=xsl, st=(c == 0): t.matmul(
                            kps[half][:, :], xs(c), Wk[:, c, 512 * half:512 * (half + 1)],
                            start=st, stop=False), deps)
                    mm = s.add("tensor", lambda t, half=half: t.matmul(
                        kps[half][:, :], sb["ones_row"][0:1, 0:128],
                        brow1[0:1, 512 * half:512 * (half + 1)],
                        start=False, stop=True), [tw, self.t_consts] if it == 0 else [])
                t_kmm = mm
                for half in range(2):
                    for c in range(NCH):
                        deps = [t_vevac] if (half == 0 and c == 0) else []
                        mm = s.add("tensor", lambda t, c=c, half=half, xs=xsl, st=(c == 0): t.matmul(
                            vps[half][:, :], xs(c), Wv[:, c, 512 * half:512 * (half + 1)],
                            start=st, stop=False), deps)
                    mm = s.add("tensor", lambda t, half=half: t.matmul(
                        vps[half][:, :], sb["ones_row"][0:1, 0:128],
                        brow1[0:1, HID + 512 * half:HID + 512 * (half + 1)],
                        start=False, stop=True), [])
                t_vmm = mm
                if not is_ext:
                    t_xr_free[ti % 2] = t_vmm

                t1 = s.add("scalar", lambda sc: sc.activation(
                    ktile[:, 0:512], kps[0][:, :], AF.Sigmoid), [t_kmm, t_kvmm])
                t_kevac = s.add("scalar", lambda sc: sc.activation(
                    ktile[:, 512:1024], kps[1][:, :], AF.Sigmoid), [t1])
                t_kmask = s.add("vector", lambda v, mcol=mcol: v.tensor_scalar(
                    ktile[:, :], ktile[:, :], mcol, None, ALU.mult),
                    [t_kevac, self.t_consts])
                t1 = s.add("vector", lambda v: v.tensor_copy(
                    bass.AP(vtile, 0, [[16 * 65, 128], [65, 8], [1, 64]]),
                    vps[0][:, :].rearrange("p (h m) -> p h m", m=64)),
                    [t_vmm, t_kvmm, vones])
                t_vcopy = s.add("vector", lambda v: v.tensor_copy(
                    bass.AP(vtile, 8 * 65, [[16 * 65, 128], [65, 8], [1, 64]]),
                    vps[1][:, :].rearrange("p (h m) -> p h m", m=64)), [t1])
                t_vevac = t_vcopy

                for hd in range(16):
                    piece = 0 if hd < 7 else (1 if hd < 14 else 2)
                    cb = hd - (0, 7, 14)[piece]
                    st = not started.get(piece, False)
                    started[piece] = True
                    last = (it == n_total - 1) and (hd in (6, 13, 15))
                    mm = s.add("tensor", lambda t, hd=hd, piece=piece, cb=cb, st=st, last=last: t.matmul(
                        kvp[piece][:, 65 * cb:65 * cb + 65],
                        ktile[:, 64 * hd:64 * (hd + 1)],
                        vtile[:, 65 * hd:65 * (hd + 1)],
                        start=st, stop=last),
                        [t_kmask, t_vcopy] if hd == 0 else [])
                t_kvmm = mm

            mzk1 = s.add("vector", lambda v: v.memset(sb["kv_sb"][64:128, :], 0.0))
            mzk2 = s.add("vector", lambda v: v.memset(sb["kv2"][0:64, :], 0.0), [mzk1])
            e1 = s.add("vector", lambda v: v.tensor_copy(
                sb["kv_sb"][0:64, 0:455], kvp[0][:, 0:455]), [t_kvmm])
            e2 = s.add("vector", lambda v: v.tensor_copy(
                sb["kv_sb"][0:64, 455:910], kvp[1][:, 0:455]), [e1])
            e3 = s.add("vector", lambda v: v.tensor_copy(
                sb["kv_sb"][0:64, 910:1040], kvp[2][:, 0:130]), [e2])
            self.t_kv = s.dma("sync", sb["kv2"][64:128, :], sb["kv_sb"][0:64, :],
                              s.new_dma_sem("kvdup", group=True),
                              deps=[e3, mzk2])

    def build_p2(self):
        s, nc, sb, inp = self.s, self.nc, self.sb, self.inp
        gate = self.gate()
        with ExitStack() as es:
            Wq = self.sbuf("WqSb", (128, NCH, HID), BF16, es)
            Wh = self.sbuf("WhSb", (128, NCH, HID), BF16, es)
            Wo = self.sbuf("WoSb", (128, NCH, HID), BF16, es)
            lnw_rep = self.sbuf("lnw_rep_s", (128, HID), BF16, es)
            lnb_rep = self.sbuf("lnb_rep_s", (128, HID), BF16, es)
            brow2 = self.sbuf("brow2", (1, 2 * HID), BF16, es)
            qf = self.sbuf("qf", (128, NCH, 512), BF16, es)
            gatedT = self.sbuf("gatedT", (128, NCH, 512), BF16, es)
            ao = self.sbuf("ao", (128, NCH, 512), BF16, es)
            htok = self.sbuf("htok", (128, HID), BF16, es)
            attn = self.sbuf("attn", (128, HID), BF16, es)
            lnout = self.sbuf("lnout", (128, HID), BF16, es)
            gated = self.sbuf("gated", (128, HID), BF16, es)
            bn = self.sbuf("bn", (128, 16), F32, es)
            sb["ident"] = self.sbuf("ident_s", (128, 128), BF16, es)
            sb["bq_col"] = self.sbuf("bq_col_s", (128, NCH), F32, es)
            sb["bo_col"] = self.sbuf("bo_col_s", (128, NCH), F32, es)
            denb = self.sbuf("denb", (128, 16), F32, es)
            xsq = self.sbuf("xsq", (128, NCH, 512), BF16, es)
            rowA = self.sbuf("rowA", (1, 512), F32, es)
            rowB = self.sbuf("rowB", (1, 512), F32, es)
            rowC = self.sbuf("rowC", (1, 512), F32, es)
            rowBb = self.sbuf("rowBb", (1, 512), BF16, es)
            rowCb = self.sbuf("rowCb", (1, 512), BF16, es)
            P = [self.psum(f"p2b{i}", (128, 512), F32, es) for i in range(7)]
            TP = self.psum("p2tp", (128, 1024), BF16, es)

            dW = s.new_dma_sem("p2w", group=True)
            s.dma("gpsimd", sb["ident"][:, :], inp["ident"][:, :], dW, deps=gate)
            s.dma("gpsimd", sb["bq_col"][:, :], inp["bq_col"][:, :], dW)
            s.dma("gpsimd", sb["bo_col"][:, :], inp["bo_col"][:, :], dW)
            s.dma("gpsimd", lnw_rep[:, :], inp["lnw_rep"][:, :], dW)
            s.dma("gpsimd", lnb_rep[:, :], inp["lnb_rep"][:, :], dW)
            s.dma("gpsimd", brow2[0:1, 0:HID], inp["bh_row"][:, :], dW)
            t_bb = s.dma("gpsimd", brow2[0:1, HID:2 * HID], inp["beta_row"][:, :], dW)
            t_brow = s.add("vector", lambda v: v.tensor_add(
                brow2[0:1, 0:HID], brow2[0:1, 0:HID],
                brow2[0:1, HID:2 * HID]), [t_bb])
            tw = None
            for nm, t in (("Wq", Wq), ("Wh", Wh), ("Wo", Wo)):
                win = inp[nm].rearrange("(c p) o -> p c o", p=128)
                for i in range(2):
                    tw = s.dma("gpsimd", t[:, 4 * i:4 * i + 4, :],
                               win[:, 4 * i:4 * i + 4, :], dW, deps=gate if nm == "Wq" and i == 0 else [])
            t_w = tw
            dx1 = s.new_dma_sem("x1store", group=True)
            self.d_x1b = dx1

            HEAD_PIECE = [(0, 0), (0, 1), (0, 2), (0, 3), (0, 4), (0, 5), (0, 6),
                          (1, 0), (1, 1), (1, 2), (1, 3), (1, 4), (1, 5), (1, 6),
                          (2, 0), (2, 1)]
            NP = (7, 7, 2)
            KV = sb["kv_sb"]

            groups = [(512 * g, 512) for g in range(10)] + [(5120, 128)]
            t_qevac = [None, None]
            t_hevac = None
            t_div = None
            t_tevac = None
            t_aoevac = None
            t_resid = None
            t_stats = None
            t_x1 = None
            t_wo = None
            t_num = None
            t_sq = None
            t_g_prev = None

            for (g0, Ng) in groups:
                ntile = Ng // 128
                for c in range(NCH):
                    pb = c % 2
                    for fin in range(NCH):
                        deps = []
                        if c == 0 and fin == 0:
                            deps = [t_w, self.t_X, t_div, t_aoevac, t_stats] + (gate if g0 == 0 else [])
                        if fin == 0:
                            deps = deps + [t_qevac[pb]]
                        mm = s.add("tensor", lambda t, c=c, fin=fin, pb=pb, g0=g0, Ng=Ng, st=(fin == 0): t.matmul(
                            P[pb][:, 0:Ng], Wq[:, fin, 128 * c:128 * (c + 1)],
                            sb["X"][:, fin, g0:g0 + Ng],
                            start=st, stop=(fin == NCH - 1)), deps)
                    t_qevac[pb] = s.add("scalar", lambda sc, c=c, pb=pb, Ng=Ng: sc.activation(
                        qf[:, c, 0:Ng], P[pb][:, 0:Ng], AF.Sigmoid,
                        bias=sb["bq_col"][:, c:c + 1]), [mm])
                t_qf = t_qevac[(NCH - 1) % 2]

                for ti in range(ntile):
                    tcol = g0 + 128 * ti
                    ext_ti = tcol // 128
                    if P2STOP < 2:
                        continue
                    mm = None
                    for half in range(2):
                        pb = 2 + half
                        for c in range(NCH):
                            deps = []
                            if half == 0 and c == 0:
                                deps = [t_hevac, t_w]
                            mm = s.add("tensor", lambda t, c=c, half=half, pb=pb, tcol=tcol, st=(c == 0): t.matmul(
                                P[pb][:, :], sb["X"][:, c, tcol:tcol + 128],
                                Wh[:, c, 512 * half:512 * (half + 1)],
                                start=st, stop=False), deps)
                        mm = s.add("tensor", lambda t, half=half, pb=pb: t.matmul(
                            P[pb][:, :], sb["ones_row"][0:1, 0:128],
                            brow2[0:1, 512 * half:512 * (half + 1)],
                            start=False, stop=True), [t_brow] if g0 == 0 and ti == 0 else [])
                    t_hmm = mm
                    th1 = s.add("vector", lambda v: v.tensor_copy(
                        htok[:, 0:512], P[2][:, :]), [t_hmm, t_tevac])
                    t_hevac = s.add("vector", lambda v: v.tensor_copy(
                        htok[:, 512:1024], P[3][:, :]), [th1])

                    if P2STOP < 3:
                        continue
                    mm = None
                    for hd in range(16):
                        piece, slot = HEAD_PIECE[hd]
                        deps = []
                        if hd == 0:
                            deps = [t_qf, self.t_kv, t_div, t_stats]
                        KVp = KV if hd % 2 == 0 else sb["kv2"]
                        mm = s.add("tensor", lambda t, hd=hd, piece=piece, slot=slot, ti=ti, KVp=KVp: t.matmul(
                            P[4 + piece][:, 65 * slot:65 * slot + 65],
                            qf[:, hd // 2, 128 * ti:128 * (ti + 1)],
                            KVp[:, 65 * hd:65 * hd + 65],
                            start=(slot == 0), stop=(slot == NP[piece] - 1)), deps)
                    t_num = mm

                    if P2STOP < 4:
                        continue
                    td = None
                    for piece in range(3):
                        nh = NP[piece]
                        base = (0, 7, 14)[piece]
                        td = s.add("scalar", lambda sc, piece=piece, nh=nh, base=base: sc.copy(
                            bass.AP(denb, base, [[16, 128], [1, nh], [1, 1]]),
                            bass.AP(P[4 + piece], 64, [[512, 128], [65, nh], [1, 1]])),
                            [t_num, t_g_prev] if piece == 0 else [])
                        td = s.add("vector", lambda v, nh=nh, base=base: v.reciprocal(
                            denb[:, base:base + nh], denb[:, base:base + nh]), [td])
                        td = s.add("vector", lambda v, piece=piece, nh=nh, base=base: v.tensor_tensor(
                            bass.AP(attn, 64 * base, [[HID, 128], [64, nh], [1, 64]]),
                            bass.AP(P[4 + piece], 0, [[512, 128], [65, nh], [1, 64]]),
                            bass.AP(denb, base, [[16, 128], [1, nh], [0, 64]]),
                            ALU.mult), [td])
                    t_div = td

                    if P2STOP < 5:
                        continue
                    b1 = s.add("vector", lambda v: v.bn_stats(bn[:, 0:6], attn[:, 0:512]), [t_div])
                    b2 = s.add("vector", lambda v: v.bn_stats(bn[:, 6:12], attn[:, 512:1024]), [b1])
                    b3 = s.add("vector", lambda v: v.bn_aggr(bn[:, 12:14], bn[:, 0:12]), [b2])
                    sq = s.add("scalar", lambda sc: sc.activation(
                        bn[:, 14:15], bn[:, 13:14], AF.Sqrt, bias=sb["eps"][:, :]), [b3])
                    t_sq = s.add("vector", lambda v: v.reciprocal(bn[:, 15:16], bn[:, 14:15]), [sq])
                    t_g = s.add("vector", lambda v: v.tensor_scalar(
                        lnout[:, :], attn[:, :], bn[:, 12:13], bn[:, 15:16],
                        ALU.subtract, ALU.mult), [t_sq])
                    t_g = s.add("vector", lambda v: v.scalar_tensor_tensor(
                        lnout[:, :], lnout[:, :], 0.0, lnw_rep[:, :],
                        ALU.bypass, ALU.mult), [t_g])
                    t_g = s.add("vector", lambda v: v.tensor_add(
                        lnout[:, :], lnout[:, :], lnb_rep[:, :]), [t_g])
                    t_g = s.add("vector", lambda v: v.tensor_mul(
                        gated[:, :], lnout[:, :], htok[:, :]), [t_g, t_hevac])
                    t_g_prev = t_g

                    if P2STOP < 6:
                        continue
                    tt = None
                    for c in range(NCH):
                        deps = [t_g, self.t_consts] if c == 0 else []
                        if c == 0:
                            deps = deps + [t_tevac]
                        tt = s.add("tensor", lambda t, c=c: t.transpose(
                            TP[:, 128 * c:128 * (c + 1)], gated[:, 128 * c:128 * (c + 1)],
                            sb["ident"][:, :]), deps)
                    t_tevac = s.add("scalar", lambda sc, ti=ti: sc.copy(
                        gatedT[:, :, 128 * ti:128 * (ti + 1)],
                        TP[:, :].rearrange("p (c x) -> p c x", x=128)), [tt])

                if P2STOP < 7:
                    continue
                mm = None
                for c in range(NCH):
                    pb = c % 2
                    for fin in range(NCH):
                        deps = []
                        if c == 0 and fin == 0:
                            deps = [t_tevac, t_qf]
                        if fin == 0:
                            deps = deps + [t_aoevac]
                        mm = s.add("tensor", lambda t, c=c, fin=fin, pb=pb, Ng=Ng, st=(fin == 0): t.matmul(
                            P[pb][:, 0:Ng], Wo[:, fin, 128 * c:128 * (c + 1)],
                            gatedT[:, fin, 0:Ng],
                            start=st, stop=(fin == NCH - 1)), deps)
                    t_aoevac = s.add("scalar", lambda sc, c=c, pb=pb, Ng=Ng: sc.activation(
                        ao[:, c, 0:Ng], P[pb][:, 0:Ng], AF.Relu,
                        bias=sb["bo_col"][:, c:c + 1]), [mm])

                if P2STOP < 8:
                    continue
                t_resid = s.add("vector", lambda v, g0=g0, Ng=Ng: v.tensor_add(
                    sb["X"][:, :, g0:g0 + Ng],
                    sb["X"][:, :, g0:g0 + Ng], ao[:, :, 0:Ng]), [t_aoevac, t_stats])

                if P2STOP < 9:
                    continue
                t_sqr = s.add("scalar", lambda sc, g0=g0, Ng=Ng: sc.activation(
                    xsq[:, :, 0:Ng], sb["X"][:, :, g0:g0 + Ng], AF.Square), [t_resid])
                mm = None
                for c in range(NCH):
                    mm = s.add("tensor", lambda t, c=c, g0=g0, Ng=Ng, st=(c == 0): t.matmul(
                        P[4][0:1, 0:Ng], sb["ones_col"][:, :], sb["X"][:, c, g0:g0 + Ng],
                        start=st, stop=(c == NCH - 1)),
                        [t_resid, t_num, t_div] if c == 0 else [])
                for c in range(NCH):
                    mm = s.add("tensor", lambda t, c=c, Ng=Ng, st=(c == 0): t.matmul(
                        P[5][0:1, 0:Ng], sb["ones_col"][:, :], xsq[:, c, 0:Ng],
                        start=st, stop=(c == NCH - 1)), [t_sqr] if c == 0 else [])
                t_smm = mm
                r1 = s.add("vector", lambda v, Ng=Ng: v.tensor_scalar(
                    rowA[0:1, 0:Ng], P[4][0:1, 0:Ng], 1.0 / HID, None, ALU.mult), [t_smm])
                r2 = s.add("vector", lambda v, Ng=Ng: v.tensor_scalar(
                    rowB[0:1, 0:Ng], P[5][0:1, 0:Ng], 1.0 / HID, None, ALU.mult), [r1])
                r3 = s.add("vector", lambda v, Ng=Ng: v.tensor_mul(
                    rowC[0:1, 0:Ng], rowA[0:1, 0:Ng], rowA[0:1, 0:Ng]), [r2])
                r4 = s.add("vector", lambda v, Ng=Ng: v.tensor_sub(
                    rowB[0:1, 0:Ng], rowB[0:1, 0:Ng], rowC[0:1, 0:Ng]), [r3])
                r5 = s.add("scalar", lambda sc, Ng=Ng: sc.activation(
                    rowB[0:1, 0:Ng], rowB[0:1, 0:Ng], AF.Sqrt, bias=sb["eps"][0:1, 0:1]), [r4])
                r6 = s.add("vector", lambda v, Ng=Ng: v.reciprocal(
                    rowB[0:1, 0:Ng], rowB[0:1, 0:Ng]), [r5])
                r7 = s.add("vector", lambda v, Ng=Ng: v.tensor_mul(
                    rowC[0:1, 0:Ng], rowA[0:1, 0:Ng], rowB[0:1, 0:Ng]), [r6])
                c1 = s.add("vector", lambda v, Ng=Ng: v.tensor_copy(
                    rowBb[0:1, 0:Ng], rowB[0:1, 0:Ng]), [r7])
                c2 = s.add("vector", lambda v, Ng=Ng: v.tensor_copy(
                    rowCb[0:1, 0:Ng], rowC[0:1, 0:Ng]), [c1])
                m1 = s.add("tensor", lambda t, Ng=Ng: t.matmul(
                    P[4][:, 0:Ng], sb["ones_row"][0:1, 0:128], rowBb[0:1, 0:Ng],
                    start=True, stop=True), [c1, r1])
                m2 = s.add("tensor", lambda t, Ng=Ng: t.matmul(
                    P[5][:, 0:Ng], sb["ones_row"][0:1, 0:128], rowCb[0:1, 0:Ng],
                    start=True, stop=True), [c2, r2])
                x1a = s.add("vector", lambda v, g0=g0, Ng=Ng: v.tensor_mul(
                    sb["X"][:, :, g0:g0 + Ng], sb["X"][:, :, g0:g0 + Ng],
                    bass.AP(P[4], 0, [[512, 128], [0, NCH], [1, Ng]])), [m1])
                t_x1 = x1a
                t_stats = s.add("vector", lambda v, g0=g0, Ng=Ng: v.tensor_sub(
                    sb["X"][:, :, g0:g0 + Ng], sb["X"][:, :, g0:g0 + Ng],
                    bass.AP(P[5], 0, [[512, 128], [0, NCH], [1, Ng]])), [t_x1, m2])

                lo = max(g0, HALO)
                hi = g0 + Ng
                if hi > lo:
                    s.dma("sync", self.x1b[:, :, lo - HALO:hi - HALO],
                          sb["X"][:, :, lo:hi], dx1, deps=[t_stats])
            self.t_p2 = s.add("vector", lambda v: v.tensor_mul(
                sb["X"][:, :, HALO - 1:HALO], sb["X"][:, :, HALO - 1:HALO],
                bass.AP(sb["halo01"], 0, [[1, 128], [0, NCH], [0, 1]])),
                [t_stats, self.t_consts])

    def build_layer(self, l, do_mlp=True):
        s, nc, sb, inp = self.s, self.nc, self.sb, self.inp
        gate = self.gate()
        if l == 0:
            pcol0, n_proj, qcol0, CL = 128, 10, 128, 1
        else:
            pcol0, n_proj, qcol0, CL = 128, 10, 640, 1
        sqd = (qcol0 - pcol0) // 512
        n_O = (TEXT - qcol0) // 512
        n_iter = n_proj + CL + 1
        Bshift = CL + sqd

        with ExitStack() as es:
            wqkv = self.sbuf(f"wqkv_l{l}", (128, NCH, 3 * INNER), BF16, es)
            wo = self.sbuf(f"wo_l{l}", (128, 4, HID), BF16, es)
            qs_rep = self.sbuf(f"qsr{l}", (128, INNER), BF16, es)
            ks_rep = self.sbuf(f"ksr{l}", (128, INNER), BF16, es)
            qf = [self.sbuf(f"qf{l}_{i}", (128, 4, 512), BF16, es) for i in range(3)]
            kf_e = [self.sbuf(f"kfe{l}_{i}", (128, 4, 512), BF16, es) for i in range(3)]
            kf_o = [self.sbuf(f"kfo{l}_{i}", (128, 4, 512), BF16, es) for i in range(3)]
            vt = [self.sbuf(f"vt{l}_{i}", (128, 4, 520), BF16, es) for i in range(3)]
            tabs = self.sbuf(f"tab{l}", (128, 4, 4, DF), BF16, es)
            pt = [self.sbuf(f"pt{l}_{i}", (128, 640), BF16, es) for i in range(2)]
            m8 = self.sbuf(f"m8_{l}", (128, 1), F32, es)
            maskrowH = self.sbuf(f"mrowH{l}", (128, NT_EXT), F32, es)
            O_acc = self.sbuf(f"Oacc{l}", (65, HEADS_F, 512), BF16, es)
            o_sb = [self.sbuf(f"osb{l}_0", (128, 4, 512), BF16, es)]
            tsum = self.sbuf(f"tsum{l}", (65, 512), F32, es)
            rrowb = self.sbuf(f"rrowb{l}", (1, 512), BF16, es)
            rrowf = self.sbuf(f"rrowf{l}", (1, 512), F32, es)
            r4sb = self.sbuf(f"r4sb{l}", (128, 4), F32, es)
            ubuf = self.sbuf(f"ubuf{l}", (128, 512), BF16, es)
            qhat = self.sbuf(f"qhat{l}", (128, 512), BF16, es)
            qrot = self.sbuf(f"qrot{l}", (128, 512), F32, es)
            krot = self.sbuf(f"krot{l}", (128, 512), F32, es)
            xqk = self.sbuf(f"xqk{l}", (128, 512), BF16, es)
            qraw = self.sbuf(f"qraw{l}", (128, 512), BF16, es)
            bnf = self.sbuf(f"bnf{l}", (128, 32), F32, es)
            P = [self.psum(f"L{l}b{i}", (128, 512), F32, es) for i in range(8)]

            dW = s.new_dma_sem(f"L{l}w", group=True)
            win = inp[f"wqkv{l}"].rearrange("(c p) o -> p c o", p=128)
            tw = None
            for i in range(4):
                tw = s.dma("gpsimd", wqkv[:, 2 * i:2 * i + 2, :],
                           win[:, 2 * i:2 * i + 2, :], dW, deps=gate if i == 0 else [])
            woin = inp[f"wo{l}"].rearrange("(c p) o -> p c o", p=128)
            tw = s.dma("gpsimd", wo[:, :, :], woin[:, :, :], dW)
            tw = s.dma("gpsimd", qs_rep[:, :], inp[f"qs_rep{l}"][:, :], dW)
            tw = s.dma("gpsimd", ks_rep[:, :], inp[f"ks_rep{l}"][:, :], dW)
            t_w = tw
            t_w8 = s.add("vector", lambda v: v.tensor_scalar(
                qs_rep[:, :], qs_rep[:, :], 8.0, None, ALU.mult), [t_w])
            t_mr = s.add("vector", lambda v: v.tensor_mul(
                ubuf[:, :], qs_rep[:, :], ks_rep[:, :]), [t_w8])
            t_mr = s.add("vector", lambda v: v.tensor_reduce(
                m8[:, 0:1], ubuf[:, :], AX.X, ALU.max), [t_mr])
            t_mrow = s.add("vector", lambda v: v.tensor_scalar(
                maskrowH[:, :], sb["maskrow"][:, :], m8[:, 0:1], None,
                ALU.subtract), [t_mr, self.t_consts])
            dtab = s.new_dma_sem(f"L{l}tab")

            mz = []
            mz.append(s.add("vector", lambda v: v.memset(kf_e[2][:, :, :], 0.0), gate))
            mz.append(s.add("vector", lambda v: v.memset(kf_o[2][:, :, :], 0.0), []))
            for i in range(2):
                mz.append(s.add("vector", lambda v, i=i: v.memset(
                    kf_e[i][64:128, :, :], 0.0), []))
                mz.append(s.add("vector", lambda v, i=i: v.memset(
                    kf_o[i][0:64, :, :], 0.0), []))
            for i in range(3):
                mz.append(s.add("vector", lambda v, i=i: v.memset(
                    bass.AP(vt[i], 64, [[4 * 520, 128], [520, 4], [65, 8]]), 1.0), []))
            mz.append(s.add("vector", lambda v: v.memset(O_acc[:, :, :], 0.0), []))
            t_init = mz[-1]

            t_tab = None
            t_tabuse = None
            t_vevac = {}
            t_qkevac = {}
            t_qke = {0: [None] * 4, 1: [None] * 4, 2: [None] * 4}
            t_expL = None
            t_expP = [None, None]
            t_ksq_last = None
            self.t_ttq = None
            self.t_ttk = None
            self.t_qe_prev = None
            self.t_ke_prev = None
            self.t_cpr = None
            t_pv = None
            t_oacc = None
            t_tt1 = None
            t_wo = None
            t_wo2 = None
            t_wres = None
            t_uq = None
            t_uk = None
            t_qsqr = None
            t_qrd = None
            t_rep = None
            t_tke_last = None
            t_projmm = None
            t_stmm = None

            def piece_tab(pw):
                nonlocal t_tab
                t_tab = s.dma("gpsimd", tabs[:, :, :, :],
                              inp["tab"][pcol0 + 512 * pw:pcol0 + 512 * pw + 512, :, :]
                              .rearrange("(j p) a d -> p j a d", p=128),
                              dtab, deps=[t_tabuse])

            def piece_v(pw):
                slot = pw % 3
                for j in range(4):
                    tc = pcol0 + 512 * pw + 128 * j
                    mm = None
                    for c in range(NCH):
                        off = -1 if c >= 4 else 0
                        deps = []
                        if c == 0:
                            deps = [t_vevac.get(slot), t_tt1, t_w,
                                    self.t_p2 if l == 0 else
                                    getattr(self, "t_halofix", None),
                                    getattr(self, "_dbg_tok", None)]
                            if pw == 0 and j == 0:
                                deps = deps + gate
                        mm = s.add("tensor", lambda t, c=c, tc=tc, off=off, st=(c == 0): t.matmul(
                            P[6][:, :], sb["X"][:, c, tc + off:tc + off + 128],
                            wqkv[:, c, 1024:1536],
                            start=st, stop=(c == NCH - 1)), deps)
                    t_vevac[slot] = s.add("vector", lambda v, slot=slot, j=j: v.tensor_copy(
                        bass.AP(vt[slot], 520 * j, [[4 * 520, 128], [65, 8], [1, 64]]),
                        P[6][:, :].rearrange("p (h m) -> p h m", m=64)),
                        [mm, t_init, t_pv, getattr(self, "_dbg_tok", None)])

            def piece_q(pw, j):
                nonlocal t_projmm, t_qsqr, t_qrd, t_uq
                tc = pcol0 + 512 * pw + 128 * j
                deps0 = [t_w, self.t_p2 if l == 0 else
                         getattr(self, "t_halofix", None)]
                mm = None
                for c in range(NCH):
                    off = -1 if c >= 4 else 0
                    deps = []
                    if c == 0:
                        deps = deps0 + [t_uq, t_tt1]
                    mm = s.add("tensor", lambda t, c=c, tc=tc, off=off, st=(c == 0): t.matmul(
                        P[2][:, :], sb["X"][:, c, tc + off:tc + off + 128],
                        wqkv[:, c, 0:512],
                        start=st, stop=(c == NCH - 1)), deps)
                t_projmm = mm
                t_qsqr = s.add("scalar", lambda sc: sc.activation(
                    xqk[:, :], P[2][:, :], AF.Square), [mm, t_qrd])
                rd = s.add("vector", lambda v: v.tensor_reduce(
                    bnf[:, 0:8], xqk[:, :].rearrange("p (h d) -> p h d", d=64),
                    AX.X, ALU.add), [t_qsqr])
                t_qrd = rd
                tqr = s.add("vector", lambda v: v.tensor_copy(
                    qraw[:, :], P[2][:, :]), [rd])
                t_uq = tqr

            def piece_k(pw, j):
                nonlocal t_projmm, t_ksq_last, t_qrd, t_uk, t_tabuse
                tc = pcol0 + 512 * pw + 128 * j
                mm = None
                for c in range(NCH):
                    off = -1 if c >= 4 else 0
                    deps = []
                    if c == 0:
                        deps = [t_uk, t_uq, t_tt1]
                    mm = s.add("tensor", lambda t, c=c, tc=tc, off=off, st=(c == 0): t.matmul(
                        P[2][:, :], sb["X"][:, c, tc + off:tc + off + 128],
                        wqkv[:, c, 512:1024],
                        start=st, stop=(c == NCH - 1)), deps)
                t_projmm = mm
                tks = s.add("scalar", lambda sc: sc.activation(
                    xqk[:, :], P[2][:, :], AF.Square), [mm, t_qrd])
                t_ksq_last = tks
                rd = s.add("vector", lambda v: v.tensor_reduce(
                    bnf[:, 8:16], xqk[:, :].rearrange("p (h d) -> p h d", d=64),
                    AX.X, ALU.add), [tks])
                t_qrd = rd
                sq = s.add("scalar", lambda sc: sc.activation(
                    bnf[:, 16:32], bnf[:, 0:16], AF.Sqrt), [rd])
                mx = s.add("vector", lambda v: v.tensor_scalar(
                    bnf[:, 16:32], bnf[:, 16:32], 1e-12, None, ALU.max), [sq])
                rc = s.add("vector", lambda v: v.reciprocal(
                    bnf[:, 16:32], bnf[:, 16:32]), [mx])
                t1 = s.add("vector", lambda v: v.tensor_tensor(
                    ubuf[:, :], qraw[:, :],
                    bass.AP(bnf, 16, [[32, 128], [1, 8], [0, 64]]),
                    ALU.mult), [rc])
                t2 = s.add("vector", lambda v: v.tensor_mul(
                    qhat[:, :], ubuf[:, :], qs_rep[:, :]), [t1, t_w8])
                ta = s.add("vector", lambda v, j=j: v.tensor_tensor(
                    ubuf[:, :], qhat[:, :],
                    bass.AP(tabs, 256 * j + 0 * DF, [[4 * 4 * DF, 128], [0, 8], [1, DF]]),
                    ALU.mult), [t2, t_tab])
                tb = s.add("vector", lambda v, j=j: v.tensor_tensor(
                    qrot[:, :].rearrange("p (h t x) -> p h t x", h=8, t=2),
                    bass.AP(qhat, 32, [[512, 128], [64, 8], [-32, 2], [1, 32]]),
                    bass.AP(tabs, 256 * j + 1 * DF, [[4 * 4 * DF, 128], [0, 8], [32, 2], [1, 32]]),
                    ALU.mult), [ta, self.t_ttq])
                tq = s.add("vector", lambda v: v.tensor_add(
                    qrot[:, :], qrot[:, :], ubuf[:, :]), [tb])
                t1 = s.add("vector", lambda v: v.tensor_tensor(
                    ubuf[:, :], P[2][:, :],
                    bass.AP(bnf, 24, [[32, 128], [1, 8], [0, 64]]),
                    ALU.mult), [tq])
                t_uk = t1
                t2 = s.add("vector", lambda v: v.tensor_mul(
                    qhat[:, :], ubuf[:, :], ks_rep[:, :]), [t1])
                ta = s.add("vector", lambda v, j=j: v.tensor_tensor(
                    ubuf[:, :], qhat[:, :],
                    bass.AP(tabs, 256 * j + 2 * DF, [[4 * 4 * DF, 128], [0, 8], [1, DF]]),
                    ALU.mult), [t2])
                tb = s.add("vector", lambda v, j=j: v.tensor_tensor(
                    krot[:, :].rearrange("p (h t x) -> p h t x", h=8, t=2),
                    bass.AP(qhat, 32, [[512, 128], [64, 8], [-32, 2], [1, 32]]),
                    bass.AP(tabs, 256 * j + 3 * DF, [[4 * 4 * DF, 128], [0, 8], [32, 2], [1, 32]]),
                    ALU.mult), [ta, self.t_ttk])
                tk = s.add("vector", lambda v: v.tensor_add(
                    krot[:, :], krot[:, :], ubuf[:, :]), [tb])
                t_tabuse = tk
                return tq, tk

            def piece_T(pw, j, tq_tok, tk_tok):
                nonlocal t_qkevac, t_tke_last
                slot = pw % 3
                tt = None
                for cc in range(4):
                    deps = [tq_tok, t_w]
                    if cc == 0:
                        deps = deps + [t_qkevac.get(slot), self.t_qe_prev]
                    tt = s.add("tensor", lambda t, cc=cc: t.transpose(
                        P[4][:, 128 * cc:128 * (cc + 1)],
                        qrot[:, 128 * cc:128 * (cc + 1)], sb["identF"][:, :]), deps)
                self.t_ttq = tt
                tqe = s.add("scalar", lambda sc, slot=slot, j=j: sc.copy(
                    qf[slot][:, :, 128 * j:128 * (j + 1)],
                    P[4][:, :].rearrange("p (c x) -> p c x", x=128)),
                    [tt, getattr(self, "_dbg_tok", None)])
                self.t_qe_prev = tqe
                tt = None
                for cc in range(4):
                    deps = [tk_tok]
                    if cc == 0:
                        deps = deps + [tqe, self.t_ke_prev]
                    tt = s.add("tensor", lambda t, cc=cc: t.transpose(
                        P[4][:, 128 * cc:128 * (cc + 1)],
                        krot[:, 128 * cc:128 * (cc + 1)], sb["identF"][:, :]), deps)
                self.t_ttk = tt
                tke = s.add("scalar", lambda sc, slot=slot, j=j: sc.copy(
                    kf_e[slot][0:64, :, 128 * j:128 * (j + 1)],
                    P[4][0:64, :].rearrange("p (c x) -> p c x", x=128)),
                    [tt, getattr(self, "_dbg_tok", None)])
                tke = s.add("scalar", lambda sc, slot=slot, j=j: sc.copy(
                    kf_o[slot][64:128, :, 128 * j:128 * (j + 1)],
                    P[4][64:128, :].rearrange("p (c x) -> p c x", x=128)), [tke])
                self.t_ke_prev = tke
                t_qkevac[slot] = tke
                t_qke[slot][j] = tke
                t_tke_last = tke

            for w in range(n_iter):
                slot = w % 3
                if w < n_proj:
                    piece_tab(w)
                    piece_v(w)
                    pend = None
                    for j in range(4):
                        if pend is not None:
                            piece_T(w, pend[0], pend[1], pend[2])
                        piece_q(w, j)
                        tqt, tkt = piece_k(w, j)
                        pend = (j, tqt, tkt)
                    piece_T(w, pend[0], pend[1], pend[2])

                if l == 0 and w == 2 and DBG_TAP:
                    dsemd = s.new_dma_sem(f"dbg{w}", group=True)
                    s.dma("gpsimd", self.dbg_qf[:, :, :], qf[w % 3][:, :, :], dsemd,
                          deps=[t_tke_last])
                    s.dma("gpsimd", self.dbg_kf[:, :, :], kf_e[w % 3][:, :, :], dsemd)
                    dtv = s.dma("gpsimd", self.dbg_vt[:, :, :], vt[w % 3][:, :, :], dsemd)
                    s.dma("gpsimd", self.dbg_kf2[:, :, :], kf_o[w % 3][:, :, :], dsemd)
                    dtv = s.dma("gpsimd", self.dbg_vt2[:, :, :], vt[(w - 1) % 3][:, :, :], dsemd)
                    self._dbg_tok = dtv
                B_wo = w - 1 - Bshift
                if 0 <= B_wo < n_O:
                    B = B_wo
                    bcol = qcol0 + 512 * B
                    mm = None
                    for c in range(NCH):
                        wbank = P[6] if c % 2 == 0 else P[7]
                        for fin in range(4):
                            deps = []
                            if c == 0 and fin == 0:
                                deps = [t_tt1, t_w, t_oacc, t_vevac.get(w % 3)]
                            if fin == 0:
                                deps = deps + [t_wres]
                            mm = s.add("tensor", lambda t, c=c, fin=fin, wbank=wbank, B=B, st=(fin == 0): t.matmul(
                                wbank[:, :], wo[:, fin, 128 * c:128 * (c + 1)],
                                o_sb[0][:, fin, :],
                                start=st, stop=(fin == 3)), deps)
                        t_wres = s.add("vector", lambda v, c=c, wbank=wbank, bcol=bcol: v.tensor_add(
                            sb["X"][:, c, bcol:bcol + 512],
                            sb["X"][:, c, bcol:bcol + 512], wbank[:, :]), [mm])
                    if bcol <= HALO - 1 < bcol + 512:
                        t_wres = s.add("vector", lambda v: v.tensor_mul(
                            sb["X"][:, :, HALO - 1:HALO],
                            sb["X"][:, :, HALO - 1:HALO],
                            bass.AP(sb["halo01"], 0, [[1, 128], [0, NCH], [0, 1]])),
                            [t_wres])
                        self.t_halofix = t_wres
                    t_wo2 = t_wres
                    t_wo = t_wres
                    if l == 0 and B == 2 and DBG_TAP:
                        dsemo = s.new_dma_sem("dbgo", group=True)
                        t_wo2 = s.dma("gpsimd", self.dbg_o[:, :, :], o_sb[0][:, :, :], dsemo,
                                      deps=[t_tt1])

                pb = w - CL
                B_left = w - Bshift
                B_right = B_left + 1
                do_left = 0 <= B_left < n_O
                do_right = 0 <= B_right < n_O
                if 0 <= pb < n_proj and (do_left or do_right):
                    kslot = pb % 3
                    jlast_r = -1
                    for j in range(4):
                        kt0_ = pcol0 + 512 * pb + 128 * j
                        qn_ = min(kt0_ + 640, TEXT) - kt0_
                        if qn_ > 512 - 128 * j:
                            jlast_r = j

                    for h in range(HEADS_F):
                        kfp = kf_e if h % 2 == 0 else kf_o
                        pend = None
                        rstarted = [False]

                        def emit_pv(j, e1):
                            split = 512 - 128 * j
                            qn_j = min(pcol0 + 512 * pb + 128 * j + 640, TEXT) - (
                                pcol0 + 512 * pb + 128 * j)
                            if do_left:
                                n_l = min(split, qn_j)
                                st = (j == 0)
                                deps = [e1] + ([t_tt1, t_vevac.get(w % 3), t_wo]
                                               if st else [])
                                s.add("tensor", lambda t, j=j, h=h, kslot=kslot, n_l=n_l, st=st: t.matmul(
                                    P[6][0:65, 128 * j:128 * j + n_l],
                                    vt[kslot][:, j, 65 * h:65 * h + 65],
                                    pt[j % 2][:, 0:n_l],
                                    start=st, stop=(j == 3)), deps)
                            if do_right and qn_j > split:
                                n_r = qn_j - split
                                s.add("tensor", lambda t, j=j, h=h, kslot=kslot, n_r=n_r, split=split, jl=jlast_r: t.matmul(
                                    P[7][0:65, 0:n_r],
                                    vt[kslot][:, j, 65 * h:65 * h + 65],
                                    pt[j % 2][:, split:split + n_r],
                                    start=False, stop=(j == jl)), deps=[e1])
                            return s.ops["tensor"][-1]

                        for j in range(4):
                            kt0 = pcol0 + 512 * pb + 128 * j
                            kti = kt0 // 128
                            qe = min(kt0 + 640, TEXT)
                            qn = qe - kt0
                            par = j % 2
                            bankF = P[5] if par == 0 else P[0]
                            bankT = P[3] if par == 0 else P[1]
                            segs = [(pb % 3, 128 * j, min(512 - 128 * j, qn), 0)]
                            rem = qn - segs[0][2]
                            if rem > 0:
                                segs.append((((pb + 1) % 3), 0, rem, segs[0][2]))
                            first = True
                            bstart = {}
                            for (qs_, qoff, qlen, pso) in segs:
                                a = 0
                                while a < qlen:
                                    p = pso + a
                                    if p < 512:
                                        take = min(qlen - a, 512 - p)
                                        bank, boff = bankF, p
                                    else:
                                        take = qlen - a
                                        bank, boff = bankT, p - 512
                                    deps = []
                                    if first:
                                        deps = [t_qkevac.get(pb % 3),
                                                t_qke[(pb + 1) % 3][j],
                                                t_expP[par]]
                                    st = not bstart.get(id(bank), False)
                                    bstart[id(bank)] = True
                                    s.add("tensor", lambda t, qs_=qs_, qoff=qoff, a=a, take=take, bank=bank, boff=boff, h=h, kslot=kslot, j=j, st=st, kfp=kfp: t.matmul(
                                        bank[:, boff:boff + take],
                                        kfp[kslot][:, h // 2, 128 * j:128 * (j + 1)],
                                        qf[qs_][:, h // 2, qoff + a:qoff + a + take],
                                        start=st, stop=False), deps)
                                    first = False
                                    a += take
                            t_stmm = s.add("tensor", lambda t, bankF=bankF: t.matmul(
                                bankF[:, 0:128], sb["negI"][:, :], sb["mk_hi"][:, :],
                                start=False, stop=True), [])
                            if qn > 512:
                                t_stmm = s.add("tensor", lambda t, qn=qn, bankT=bankT: t.matmul(
                                    bankT[:, 0:qn - 512], sb["negI"][:, :],
                                    sb["mk_lo"][:, 0:qn - 512], start=False, stop=True), [])
                            e1 = s.add("scalar", lambda sc, j=j, kti=kti, qn=qn, bankF=bankF: sc.activation(
                                pt[j % 2][:, 0:min(qn, 512)], bankF[:, 0:min(qn, 512)], AF.Exp,
                                bias=maskrowH[:, kti:kti + 1]), [t_stmm, t_pv, t_mrow])
                            if qn > 512:
                                e1 = s.add("scalar", lambda sc, j=j, kti=kti, qn=qn, bankT=bankT: sc.activation(
                                    pt[j % 2][:, 512:qn], bankT[:, 0:qn - 512], AF.Exp,
                                    bias=maskrowH[:, kti:kti + 1]), [e1])
                            t_expL = e1
                            t_expP[par] = e1
                            if j == 0 and do_right:
                                s.add("tensor", lambda t: t.matmul(
                                    P[7][0:65, :], sb["ones_row"][0:1, 0:65],
                                    sb["zrow"][0:1, 0:512], start=True, stop=False),
                                    [t_oacc, t_wo, self.t_consts])
                            if pend is not None:
                                t_pv = emit_pv(*pend)
                            pend = (j, e1)
                        t_pv = emit_pv(*pend)
                        if do_left:
                            tt1 = s.add("vector", lambda v, h=h: v.tensor_add(
                                tsum[:, :], P[6][0:65, :], O_acc[:, h, :]), [t_pv, t_oacc])
                            tcp = s.add("vector", lambda v: v.tensor_copy(
                                rrowf[0:1, :], tsum[64:65, :]), [tt1])
                            fm = None
                            for c4 in range(4):
                                fm = s.add("tensor", lambda t, c4=c4, st=(c4 == 0): t.matmul(
                                    P[4][:, 0:4],
                                    rrowf[0:1, 128 * c4:128 * (c4 + 1)],
                                    sb["e4"][0:1, 4 * c4:4 * c4 + 4],
                                    start=st, stop=(c4 == 3)),
                                    [tcp, self.t_cpr, t_tke_last] if c4 == 0 else [])
                            flr = s.add("vector", lambda v: v.tensor_scalar(
                                P[4][:, 0:4], P[4][:, 0:4], 1e-30, None, ALU.max), [fm])
                            rcp = s.add("vector", lambda v: v.reciprocal(
                                r4sb[:, :], P[4][:, 0:4]), [flr])
                            bt = None
                            for c4 in range(4):
                                bt = s.add("tensor", lambda t, c4=c4: t.transpose(
                                    P[4][0:1, 128 * c4:128 * (c4 + 1)],
                                    r4sb[:, c4:c4 + 1], sb["identF"][:, :]),
                                    [rcp] if c4 == 0 else [])
                            cpr = s.add("vector", lambda v: v.tensor_copy(
                                rrowb[0:1, :], P[4][0:1, 0:512]), [bt])
                            self.t_cpr = cpr
                            trp = s.add("tensor", lambda t: t.matmul(
                                P[2][0:64, 0:512], sb["ones_row"][0:1, 0:64], rrowb[0:1, :],
                                start=True, stop=True), [cpr, t_tt1, t_uq, t_uk])
                            t_rep = trp
                            to2 = s.add("vector", lambda v, h=h, B=B_left: v.tensor_tensor(
                                o_sb[0][64 * (h % 2):64 * (h % 2) + 64, h // 2, :],
                                tsum[0:64, :], P[2][0:64, :], ALU.mult), [trp, t_wo2])
                            t_tt1 = to2
                        if do_right:
                            t_oacc = s.add("vector", lambda v, h=h: v.tensor_copy(
                                O_acc[:, h, :], P[7][0:65, :]),
                                [t_pv] + ([t_tt1] if do_left else []))
                        else:
                            t_oacc = s.add("vector", lambda v, h=h: v.memset(
                                O_acc[:, h, :], 0.0), [t_tt1])
                    if False:
                        B = B_left
                        bcol = qcol0 + 512 * B
                        mm = None
                        for c in range(NCH):
                            wbank = P[6] if c % 2 == 0 else P[7]
                            for fin in range(4):
                                deps = []
                                if c == 0 and fin == 0:
                                    deps = [t_tt1, t_w, t_oacc]
                                if fin == 0:
                                    deps = deps + [t_wres]
                                mm = s.add("tensor", lambda t, c=c, fin=fin, wbank=wbank, B=B, st=(fin == 0): t.matmul(
                                    wbank[:, :], wo[:, fin, 128 * c:128 * (c + 1)],
                                    o_sb[0][:, fin, :],
                                    start=st, stop=(fin == 3)), deps)
                            t_wres = s.add("vector", lambda v, c=c, wbank=wbank, bcol=bcol: v.tensor_add(
                                sb["X"][:, c, bcol:bcol + 512],
                                sb["X"][:, c, bcol:bcol + 512], wbank[:, :]), [mm])
                        if bcol <= HALO - 1 < bcol + 512:
                            t_wres = s.add("vector", lambda v: v.tensor_mul(
                                sb["X"][:, :, HALO - 1:HALO],
                                sb["X"][:, :, HALO - 1:HALO],
                                bass.AP(sb["halo01"], 0, [[1, 128], [0, NCH], [0, 1]])),
                                [t_wres])
                            self.t_halofix = t_wres
                        t_wo2 = t_wres
                        t_wo = t_wres
                        if l == 0 and B == 2 and DBG_TAP:
                            dsemo = s.new_dma_sem("dbgo", group=True)
                            t_wo2 = s.dma("gpsimd", self.dbg_o[:, :, :], o_sb[0][:, :, :], dsemo,
                                          deps=[t_tt1])

        if do_mlp:
            self._mlp(l)

    def _mlp(self, l):
        s, nc, sb, inp = self.s, self.nc, self.sb, self.inp
        gate = self.gate()
        if l == 0:
            mc0, ntok = 512, 4736
        else:
            mc0, ntok = 1152, 4096
        megas = []
        a = 0
        while a < ntok:
            megas.append((mc0 + a, min(MEGA, ntok - a)))
            a += min(MEGA, ntok - a)

        with ExitStack() as es:
            h_buf = self.sbuf(f"hbuf{l}", (128, 22, MEGA), BF16, es)
            xt = self.sbuf(f"xt{l}", (128, NCH, MEGA), BF16, es)
            w1a = [self.sbuf(f"w1a{l}_{i}", (128, NCH, 128), BF16, es) for i in range(2)]
            w1g = [self.sbuf(f"w1g{l}_{i}", (128, NCH, 128), BF16, es) for i in range(2)]
            w2b = [self.sbuf(f"w2b{l}_{i}", (128, HID), BF16, es) for i in range(3)]
            gout = self.sbuf(f"gout{l}", (128, MEGA), BF16, es)
            c12 = self.sbuf(f"c12_{l}", (2, H2), BF16, es)
            c2row = self.sbuf(f"c2row{l}", (1, H2), BF16, es)
            misc2 = self.sbuf(f"misc2_{l}", (2, MEGA), BF16, es)
            xb = [self.sbuf(f"xb{l}_{i}", (128, NCH, 1), BF16, es) for i in range(2)]
            mrow_ = self.sbuf(f"mrow{l}", (1, MEGA), F32, es)
            srow = self.sbuf(f"srow{l}", (1, MEGA), F32, es)
            srowb = self.sbuf(f"srowb{l}", (1, MEGA), BF16, es)
            P = [self.psum(f"M{l}b{i}", (128, 512), F32, es) for i in range(8)]

            dw1a = [s.new_dma_sem(f"m{l}w1a{i}") for i in range(2)]
            dw1g = [s.new_dma_sem(f"m{l}w1g{i}") for i in range(2)]
            dw2 = [s.new_dma_sem(f"m{l}w2{i}") for i in range(3)]
            dmisc = s.new_dma_sem(f"mm{l}", group=True)
            t_b1 = s.dma("gpsimd", c2row[0:1, :], inp[f"b1row{l}"][:, :], dmisc, deps=gate)
            t_m2 = s.dma("gpsimd", misc2[:, :], inp["ones2row"][:, :], dmisc)
            w1hb = self.w1bf[l].rearrange("(c p) o -> p c o", p=128)

            t_xb = s.add("vector", lambda v: v.tensor_copy(
                xb[0][:, :, :], sb["X"][:, :, mc0 - 1:mc0]),
                gate + [getattr(self, "t_halofix", None)])
            t_ev = None
            t_h = None
            t_hP = [None, None]
            t_gel = None
            t_gelP = [None, None]
            t_stg2 = None
            t_c12done = None
            t_w1free = [None, None]
            t_w2free = [None, None, None]
            t_evc = [None] * NCH
            t_repread = None
            t_sqw = None
            t_smm = None
            t_c2dma = None

            for mi, (m0, Nm) in enumerate(megas):
                nmb = (Nm + 511) // 512
                if mi + 1 < len(megas):
                    t_xb2 = s.add("vector", lambda v, m0=m0, Nm=Nm, mi=mi: v.tensor_copy(
                        xb[(mi + 1) % 2][:, :, :], sb["X"][:, :, m0 + Nm - 1:m0 + Nm]),
                        [t_ev])
                t_sq1 = s.add("scalar", lambda sc, m0=m0, Nm=Nm: sc.activation(
                    xt[:, 0:4, 0:Nm], sb["X"][:, 0:4, m0:m0 + Nm], AF.Square),
                    [t_smm, t_ev, t_repread, getattr(self, "t_halofix", None)])
                t_sq2 = s.add("scalar", lambda sc, m0=m0, Nm=Nm: sc.activation(
                    xt[:, 4:8, 1:Nm], sb["X"][:, 4:8, m0:m0 + Nm - 1], AF.Square), [t_sq1])
                t_sqw = s.add("scalar", lambda sc, mi=mi: sc.activation(
                    xt[:, 4:8, 0:1], xb[mi % 2][:, 4:8, :], AF.Square), [t_sq2, t_xb])
                mm = None
                for c in range(NCH):
                    last_c = (c == NCH - 1)
                    for hb in range(nmb):
                        lo = 512 * hb
                        hi = min(Nm, lo + 512)
                        if c < 4:
                            mm = s.add("tensor", lambda t, c=c, hb=hb, lo=lo, hi=hi, m0=m0, st=(c == 0): t.matmul(
                                P[4 + hb][0:1, 0:hi - lo], sb["ones_col"][:, :],
                                sb["X"][:, c, m0 + lo:m0 + hi],
                                start=st, stop=False),
                                ([t_stg2, t_ev, getattr(self, "t_halofix", None)]
                                 + (gate if mi == 0 else [])) if (c == 0 and hb == 0) else [])
                        else:
                            if hb == 0:
                                mm = s.add("tensor", lambda t, c=c, mi=mi: t.matmul(
                                    P[4][0:1, 0:1], sb["ones_col"][:, :],
                                    xb[mi % 2][:, c, :],
                                    start=False, stop=False), [t_xb])
                                mm = s.add("tensor", lambda t, c=c, hi=hi, m0=m0, sp=(c == NCH - 1): t.matmul(
                                    P[4][0:1, 1:hi], sb["ones_col"][:, :],
                                    sb["X"][:, c, m0:m0 + hi - 1],
                                    start=False, stop=sp), [])
                            else:
                                mm = s.add("tensor", lambda t, c=c, lo=lo, hi=hi, m0=m0, sp=(c == NCH - 1): t.matmul(
                                    P[4 + hb][0:1, 0:hi - lo], sb["ones_col"][:, :],
                                    sb["X"][:, c, m0 + lo - 1:m0 + hi - 1],
                                    start=False, stop=sp), [])
                for c in range(NCH):
                    for hb in range(nmb):
                        lo = 512 * hb
                        hi = min(Nm, lo + 512)
                        mm = s.add("tensor", lambda t, c=c, hb=hb, lo=lo, hi=hi, st=(c == 0), sp=(c == NCH - 1): t.matmul(
                            P[6 + hb][0:1, 0:hi - lo], sb["ones_col"][:, :],
                            xt[:, c, lo:hi], start=st, stop=sp),
                            [t_sqw] if c == 0 and hb == 0 else [])
                t_smm = mm
                r1 = s.add("vector", lambda v, Nm=Nm: v.tensor_scalar(
                    mrow_[0:1, 0:min(Nm, 512)], P[4][0:1, 0:min(Nm, 512)],
                    1.0 / HID, None, ALU.mult), [t_smm])
                if Nm > 512:
                    r1 = s.add("vector", lambda v, Nm=Nm: v.tensor_scalar(
                        mrow_[0:1, 512:Nm], P[5][0:1, 0:Nm - 512],
                        1.0 / HID, None, ALU.mult), [r1])
                r2 = s.add("vector", lambda v, Nm=Nm: v.tensor_scalar(
                    srow[0:1, 0:min(Nm, 512)], P[6][0:1, 0:min(Nm, 512)],
                    1.0 / HID, None, ALU.mult), [r1])
                if Nm > 512:
                    r2 = s.add("vector", lambda v, Nm=Nm: v.tensor_scalar(
                        srow[0:1, 512:Nm], P[7][0:1, 0:Nm - 512],
                        1.0 / HID, None, ALU.mult), [r2])
                r3 = r2
                for hb in range(nmb):
                    lo = 512 * hb
                    hi = min(Nm, lo + 512)
                    r3 = s.add("vector", lambda v, hb=hb, lo=lo, hi=hi: v.tensor_mul(
                        P[4 + hb][0:1, 0:hi - lo], mrow_[0:1, lo:hi], mrow_[0:1, lo:hi]), [r3])
                    r3 = s.add("vector", lambda v, hb=hb, lo=lo, hi=hi: v.tensor_sub(
                        srow[0:1, lo:hi], srow[0:1, lo:hi], P[4 + hb][0:1, 0:hi - lo]), [r3])
                r5 = s.add("scalar", lambda sc, Nm=Nm: sc.activation(
                    srow[0:1, 0:Nm], srow[0:1, 0:Nm], AF.Sqrt,
                    bias=sb["eps"][0:1, 0:1]), [r3])
                r6 = s.add("vector", lambda v, Nm=Nm: v.reciprocal(
                    srow[0:1, 0:Nm], srow[0:1, 0:Nm]), [r5])
                r7 = r6
                for hb in range(nmb):
                    lo = 512 * hb
                    hi = min(Nm, lo + 512)
                    r7 = s.add("vector", lambda v, hb=hb, lo=lo, hi=hi: v.tensor_mul(
                        P[4 + hb][0:1, 0:hi - lo], mrow_[0:1, lo:hi], srow[0:1, lo:hi]), [r7])
                c1_ = s.add("vector", lambda v, Nm=Nm: v.tensor_copy(
                    srowb[0:1, 0:Nm], srow[0:1, 0:Nm]), [r7])
                c2_ = c1_
                for hb in range(nmb):
                    lo = 512 * hb
                    hi = min(Nm, lo + 512)
                    c2_ = s.add("vector", lambda v, hb=hb, lo=lo, hi=hi: v.tensor_scalar(
                        misc2[0:1, lo:hi], P[4 + hb][0:1, 0:hi - lo], -1.0, None,
                        ALU.mult), [c2_, t_m2])
                rp1 = s.add("tensor", lambda t, Nm=Nm: t.matmul(
                    P[0][:, 0:min(Nm, 512)], sb["ones_row"][0:1, 0:128],
                    srowb[0:1, 0:min(Nm, 512)], start=True, stop=True),
                    [c1_, t_h, t_smm])
                rp2 = None
                if Nm > 512:
                    rp2 = s.add("tensor", lambda t, Nm=Nm: t.matmul(
                        P[1][:, 0:Nm - 512], sb["ones_row"][0:1, 0:128],
                        srowb[0:1, 512:Nm], start=True, stop=True), [c1_, t_h])
                x1_ = s.add("vector", lambda v, m0=m0, Nm=Nm: v.tensor_tensor(
                    xt[:, 0:4, 0:min(Nm, 512)], sb["X"][:, 0:4, m0:m0 + min(Nm, 512)],
                    bass.AP(P[0], 0, [[512, 128], [0, 4], [1, min(Nm, 512)]]),
                    ALU.mult), [rp1, t_sqw, t_smm])
                if Nm > 512:
                    x1_ = s.add("vector", lambda v, m0=m0, Nm=Nm: v.tensor_tensor(
                        xt[:, 0:4, 512:Nm], sb["X"][:, 0:4, m0 + 512:m0 + Nm],
                        bass.AP(P[1], 0, [[512, 128], [0, 4], [1, Nm - 512]]),
                        ALU.mult), [rp2, x1_])
                x2_ = s.add("vector", lambda v, m0=m0, Nm=Nm: v.tensor_tensor(
                    xt[:, 4:8, 1:min(Nm, 512)], sb["X"][:, 4:8, m0:m0 + min(Nm, 512) - 1],
                    bass.AP(P[0], 1, [[512, 128], [0, 4], [1, min(Nm, 512) - 1]]),
                    ALU.mult), [x1_])
                if Nm > 512:
                    x2_ = s.add("vector", lambda v, m0=m0, Nm=Nm: v.tensor_tensor(
                        xt[:, 4:8, 512:Nm], sb["X"][:, 4:8, m0 + 511:m0 + Nm - 1],
                        bass.AP(P[1], 0, [[512, 128], [0, 4], [1, Nm - 512]]),
                        ALU.mult), [x2_])
                x3_ = s.add("vector", lambda v, mi=mi: v.tensor_tensor(
                    xt[:, 4:8, 0:1], xb[mi % 2][:, 4:8, :],
                    bass.AP(P[0], 0, [[512, 128], [0, 4], [1, 1]]),
                    ALU.mult), [x2_, t_xb])
                t_xt = s.add("vector", lambda v, Nm=Nm: v.tensor_tensor(
                    xt[:, :, 0:Nm], xt[:, :, 0:Nm],
                    bass.AP(sb[f"lg_col{l}"], 0, [[NCH, 128], [1, NCH], [0, Nm]]),
                    ALU.mult), [x3_])
                if mi + 1 < len(megas):
                    t_xb = t_xb2

                for j in range(22):
                    sl = j % 2
                    la = s.dma("sync", w1a[sl][:, :, :], w1hb[:, :, 128 * j:128 * (j + 1)],
                               dw1a[sl], deps=[t_w1free[sl], self.t_wcast]
                               + (gate if (mi == 0 and j < 2) else []))
                    lg = s.dma("scalar", w1g[sl][:, :, :],
                               w1hb[:, :, HPAD + 128 * j:HPAD + 128 * (j + 1)],
                               dw1g[sl], deps=[t_w1free[sl], self.t_wcast])
                    if mi == 0:
                        for (wt, coff) in ((w1a[sl], 128 * j), (w1g[sl], HPAD + 128 * j)):
                            for fin in range(NCH):
                                mm = s.add("tensor", lambda t, wt=wt, fin=fin, st=(fin == 0), sp=(fin == NCH - 1): t.matmul(
                                    P[6][0:1, 0:128], sb[f"lg_col{l}"][:, fin:fin + 1],
                                    wt[:, fin, :], start=st, stop=sp),
                                    [la, lg, t_gel, t_xt, t_c12done] if fin == 0 else [])
                            for fin in range(NCH):
                                mm = s.add("tensor", lambda t, wt=wt, fin=fin, st=(fin == 0), sp=(fin == NCH - 1): t.matmul(
                                    P[7][0:1, 0:128], sb[f"lb_col{l}"][:, fin:fin + 1],
                                    wt[:, fin, :], start=st, stop=sp), [])
                            ec1 = s.add("vector", lambda v, coff=coff: v.tensor_copy(
                                c12[0:1, coff:coff + 128], P[6][0:1, 0:128]), [mm])
                            ec2 = s.add("vector", lambda v, coff=coff: v.tensor_add(
                                c2row[0:1, coff:coff + 128], c2row[0:1, coff:coff + 128],
                                P[7][0:1, 128:256] if False else P[7][0:1, 0:128]), [ec1, t_b1])
                            t_c12done = ec2
                    jb = 4 * (j % 2)
                    for (banks, wt, dep0) in (((P[jb + 0], P[jb + 1]), w1a[sl],
                                               [la, t_xt, t_hP[j % 2], t_stg2]),
                                              ((P[jb + 2], P[jb + 3]), w1g[sl],
                                               [lg, t_gelP[j % 2]])):
                        for half in range(2 if Nm > 512 else 1):
                            nn = min(Nm - 512 * half, 512)
                            for fin in range(NCH):
                                deps = list(dep0) + ([t_c12done] if mi == 0 else []) if (half == 0 and fin == 0) else []
                                mm = s.add("tensor", lambda t, banks=banks, wt=wt, half=half, fin=fin, nn=nn, st=(fin == 0): t.matmul(
                                    banks[half][:, 0:nn], wt[:, fin, :],
                                    xt[:, fin, 512 * half:512 * half + nn],
                                    start=st, stop=False), deps)
                            if mi == 0:
                                coff = (128 * j) if wt is w1a[sl] else (HPAD + 128 * j)
                                mm = s.add("tensor", lambda t, banks=banks, half=half, nn=nn, coff=coff: t.matmul(
                                    banks[half][:, 0:nn], c12[0:1, coff:coff + 128],
                                    misc2[0:1, 512 * half:512 * half + nn],
                                    start=False, stop=False), [t_c12done, c2_])
                                mm = s.add("tensor", lambda t, banks=banks, half=half, nn=nn, coff=coff: t.matmul(
                                    banks[half][:, 0:nn], c2row[0:1, coff:coff + 128],
                                    sb["ones_row"][0:1, 0:nn],
                                    start=False, stop=True), [t_c12done])
                            else:
                                coff = (128 * j) if wt is w1a[sl] else (HPAD + 128 * j)
                                mm = s.add("tensor", lambda t, banks=banks, half=half, nn=nn, coff=coff: t.matmul(
                                    banks[half][:, 0:nn], c12[0:2, coff:coff + 128],
                                    misc2[0:2, 512 * half:512 * half + nn],
                                    start=False, stop=True), [t_c2dma, c2_])
                    t_ag = mm
                    t_w1free[sl] = t_ag
                    tg = None
                    for half in range(2 if Nm > 512 else 1):
                        nn = min(Nm - 512 * half, 512)
                        tg = s.add("scalar", lambda sc, jb=jb, half=half, nn=nn: sc.activation(
                            gout[:, 512 * half:512 * half + nn],
                            P[jb + 2 + half][:, 0:nn], AF.Gelu), [t_ag, t_h] if half == 0 else [tg])
                    t_gel = tg
                    t_gelP[j % 2] = tg
                    th = None
                    for half in range(2 if Nm > 512 else 1):
                        nn = min(Nm - 512 * half, 512)
                        th = s.add("vector", lambda v, j=j, jb=jb, half=half, nn=nn: v.scalar_tensor_tensor(
                            h_buf[:, j, 512 * half:512 * half + nn],
                            P[jb + half][:, 0:nn], 0.0, gout[:, 512 * half:512 * half + nn],
                            ALU.bypass, ALU.mult), [t_gel] if half == 0 else [th])
                    t_h = th
                    t_hP[j % 2] = th
                if mi == 0:
                    t_c2dma = s.dma("sync", c12[1:2, :], c2row[0:1, :],
                                    s.new_dma_sem(f"c2d{l}", group=True), deps=[t_c12done])

                for half in range(2 if Nm > 512 else 1):
                    nn = min(Nm - 512 * half, 512)
                    for j in range(22):
                        sl = j % 3
                        l2 = s.dma(("sync", "scalar", "gpsimd")[sl],
                                   w2b[sl][:, :],
                                   self.w2bf[l][128 * j:128 * (j + 1), :], dw2[sl],
                                   deps=[t_w2free[sl], self.t_wcast]
                                   + (gate if (mi == 0 and half == 0 and j < 3) else []))
                        for c in range(NCH):
                            deps = []
                            if c == 0 and j == 0:
                                deps = [l2, t_h]
                            elif c == 0:
                                deps = [l2]
                            if j == 0:
                                deps = deps + [t_evc[c]]
                            mm = s.add("tensor", lambda t, c=c, j=j, sl=sl, half=half, nn=nn, st=(j == 0), sp=(j == 21): t.matmul(
                                P[c][:, 0:nn], w2b[sl][:, 128 * c:128 * (c + 1)],
                                h_buf[:, j, 512 * half:512 * half + nn],
                                start=st, stop=sp), deps)
                        t_w2free[sl] = mm
                    t_stg2 = mm
                    ev = None
                    for c in range(NCH):
                        ev = s.add("vector", lambda v, c=c, half=half, nn=nn, m0=m0: v.scalar_tensor_tensor(
                            sb["X"][:, c, m0 + 512 * half:m0 + 512 * half + nn],
                            P[c][:, 0:nn], sb[f"b2_col{l}"][:, c:c + 1],
                            sb["X"][:, c, m0 + 512 * half:m0 + 512 * half + nn],
                            ALU.add, ALU.add), [t_stg2] if c == 0 else [ev])
                        t_evc[c] = ev
                    if m0 + 512 * half <= HALO - 1 < m0 + 512 * half + nn:
                        ev = s.add("vector", lambda v: v.tensor_mul(
                            sb["X"][:, :, HALO - 1:HALO], sb["X"][:, :, HALO - 1:HALO],
                            bass.AP(sb["halo01"], 0, [[1, 128], [0, NCH], [0, 1]])), [ev])
                        self.t_halofix = ev
                    t_ev = ev
                t_repread = t_ev

    def build_final(self):
        s, nc, sb, inp = self.s, self.nc, self.sb, self.inp
        gate = self.gate()
        dout = [s.new_dma_sem(f"out{i}") for i in range(2)]
        dx1r = [s.new_dma_sem(f"x1r{i}") for i in range(2)]
        with ExitStack() as es:
            x1t = [self.sbuf(f"x1t{i}", (128, NCH, 128), BF16, es) for i in range(2)]
            tsb = self.sbuf("fin_t", (128, NCH, 128), F32, es)
            tsbb = self.sbuf("fin_tb", (128, NCH, 128), BF16, es)
            tq = self.sbuf("fin_q", (128, NCH, 128), BF16, es)
            of = [self.sbuf(f"fin_o{i}", (128, NCH, 128), F32, es) for i in range(2)]
            frow = [self.sbuf(f"frow{i}", (1, 128), F32, es) for i in range(4)]
            onesf = self.sbuf("onesf_s", (1, 128), F32, es)
            P = [self.psum(f"F{i}", (128, 512), F32, es) for i in range(4)]

            t_of = s.add("gpsimd", lambda gg: gg.memset(onesf[:, :], 1.0), gate)
            t_add = None
            t_sq = None
            t_norm = [None, None]
            t_store = [None, None]
            t_rows = None
            t_rep = None
            for ti in range(32):
                col = HALO + 128 * ti
                sl = ti % 2
                ld = s.dma("sync", x1t[sl][:, :, :], self.x1b[:, :, 128 * ti:128 * (ti + 1)],
                           dx1r[sl], deps=[t_norm[sl]] + (gate if ti < 2 else []))
                ta = s.add("vector", lambda v, col=col, sl=sl: v.tensor_add(
                    tsb[:, :, :], sb["X"][:, :, col:col + 128], x1t[sl][:, :, :]),
                    [ld, t_sq] + (gate if ti == 0 else []))
                tcb = s.add("vector", lambda v: v.tensor_copy(
                    tsbb[:, :, :], tsb[:, :, :]), [ta])
                t_sq = s.add("scalar", lambda sc: sc.activation(
                    tq[:, :, :], tsb[:, :, :], AF.Square), [ta, t_rows])
                mm = None
                for c in range(NCH):
                    mm = s.add("tensor", lambda t, c=c, st=(c == 0), sp=(c == NCH - 1): t.matmul(
                        P[0][0:1, 0:128], sb["ones_col"][:, :], tsbb[:, c, :],
                        start=st, stop=sp), [tcb, t_rows] if c == 0 else [])
                for c in range(NCH):
                    mm = s.add("tensor", lambda t, c=c, st=(c == 0), sp=(c == NCH - 1): t.matmul(
                        P[1][0:1, 0:128], sb["ones_col"][:, :], tq[:, c, :],
                        start=st, stop=sp), [t_sq] if c == 0 else [])
                r1 = s.add("vector", lambda v: v.tensor_scalar(
                    frow[0][0:1, :], P[0][0:1, 0:128], 1.0 / HID, None, ALU.mult), [mm])
                r2 = s.add("vector", lambda v: v.tensor_scalar(
                    frow[1][0:1, :], P[1][0:1, 0:128], 1.0 / HID, None, ALU.mult), [r1])
                r3 = s.add("vector", lambda v: v.tensor_mul(
                    frow[2][0:1, :], frow[0][0:1, :], frow[0][0:1, :]), [r2])
                r4 = s.add("vector", lambda v: v.tensor_sub(
                    frow[1][0:1, :], frow[1][0:1, :], frow[2][0:1, :]), [r3])
                r5 = s.add("scalar", lambda sc: sc.activation(
                    frow[1][0:1, :], frow[1][0:1, :], AF.Sqrt, bias=sb["eps"][0:1, 0:1]), [r4])
                r6 = s.add("vector", lambda v: v.reciprocal(
                    frow[1][0:1, :], frow[1][0:1, :]), [r5])
                r7 = s.add("vector", lambda v: v.tensor_mul(
                    frow[2][0:1, :], frow[0][0:1, :], frow[1][0:1, :]), [r6])
                t_rows = r7
                m1 = s.add("tensor", lambda t: t.matmul(
                    P[2][:, 0:128], onesf[0:1, 0:128], frow[1][0:1, :],
                    start=True, stop=True), [r6, t_rep, t_of])
                m2 = s.add("tensor", lambda t: t.matmul(
                    P[3][:, 0:128], onesf[0:1, 0:128], frow[2][0:1, :],
                    start=True, stop=True), [r7])
                t_rep = m2
                n1 = s.add("vector", lambda v, sl=sl: v.tensor_tensor(
                    of[sl][:, :, :], tsb[:, :, :],
                    bass.AP(P[2], 0, [[512, 128], [0, NCH], [1, 128]]), ALU.mult),
                    [m1, t_store[sl]])
                t_norm[sl] = s.add("vector", lambda v, sl=sl: v.tensor_sub(
                    of[sl][:, :, :],
                    of[sl][:, :, :],
                    bass.AP(P[3], 0, [[512, 128], [0, NCH], [1, 128]])), [n1, m2])
                t_rep = m2
                t_store[sl] = s.dma(
                    "sync", self.outT.rearrange("(c p) t -> p c t", p=128)[:, :, 128 * ti:128 * (ti + 1)],
                    of[sl][:, :, :], dout[sl], deps=[t_norm[sl]])
            def fin(sy):
                sy.wait_ge(dout[0].sem, dout[0].count)
                sy.wait_ge(dout[1].sem, dout[1].count)
                return None
            s.add("sync", fin)


_LAST_SCHED = None


def build_program(upto="all", mlp0=True, mlp1=True):
    global _LAST_SCHED
    nc = bass.Bass(trn_type="TRN2", target_bir_lowering=False)
    ctx = ExitStack()
    kb = KB(nc, ctx)
    _LAST_SCHED = kb.s
    with ctx:
        kb.build_io()
        kb.build_consts()
        with ExitStack() as es_kv:
            kb.es_kv = es_kv
            if upto != "consts":
                kb.build_p1()
            if upto not in ("consts", "p1"):
                kb.build_p2()
        if upto in ("L0", "L1", "all"):
            kb.build_layer(0, do_mlp=mlp0)
        if upto in ("L1", "all"):
            kb.build_layer(1, do_mlp=mlp1)
        if upto == "all":
            kb.build_final()
        else:
            gate = kb.gate()
            dsem = kb.s.new_dma_sem("dbgout", group=True)
            t = kb.s.dma("gpsimd", kb.outT.rearrange("(c p) t -> p c t", p=128)[:, :, 0:128],
                         kb.sb["X"][:, :, HALO:HALO + 128], dsem, deps=gate)
            def fin(sy):
                sy.wait_ge(dsem.sem, dsem.count)
                return None
            kb.s.add("sync", fin)
        kb.s.setup_sems(ctx)
        kb.s.emit()
    return nc


def kernel(**inputs):
    nc = build_program()
    in_maps = [make_core_inputs(inputs, c) for c in range(8)]
    res = run_bass_kernel_spmd(nc, in_maps, core_ids=list(range(8)))
    out = np.empty((4, 8192, 1024), np.float32)
    for c in range(8):
        b, h = c // 2, c % 2
        out[b, 4096 * h:4096 * (h + 1), :] = res.results[c]["outT"].T
    return out


# version 16
# speedup vs baseline: 1.0382x; 1.0118x over previous
import numpy as np
import ml_dtypes
import concourse.bass as bass
import concourse.mybir as mybir
from concourse.bass_utils import run_bass_kernel_spmd

F32 = mybir.dt.float32
BF16 = mybir.dt.bfloat16
AF = mybir.ActivationFunctionType
ALU = mybir.AluOpType
AX = mybir.AxisListType
BF = ml_dtypes.bfloat16

ENGINES = ("tensor", "vector", "scalar", "gpsimd", "sync")


class _Op:
    __slots__ = ("engine", "fn", "deps", "idx", "dma_sem", "value", "marked", "label")

    def __init__(self, engine, fn, deps, idx, dma_sem=None):
        self.engine = engine
        self.fn = fn
        self.deps = deps
        self.idx = idx
        self.dma_sem = dma_sem
        self.value = None
        self.marked = dma_sem is not None


class DmaSem:
    __slots__ = ("name", "sem", "count", "group")

    def __init__(self, name, group=False):
        self.name = name
        self.sem = None
        self.count = 0
        self.group = group


class Sched:
    def __init__(self, nc):
        self.nc = nc
        self.ops = {e: [] for e in ENGINES}
        self.all_ops = []
        self.sems = {}
        self.dma_sems = []

    def new_dma_sem(self, name, group=False):
        d = DmaSem(name, group)
        self.dma_sems.append(d)
        return d

    def setup_sems(self, ctx):
        for e in ENGINES:
            self.sems[e] = ctx.enter_context(self.nc.semaphore(f"chain_{e}"))
        for d in self.dma_sems:
            d.sem = ctx.enter_context(self.nc.semaphore(f"dma_{d.name}"))

    def add(self, engine, fn, deps=()):
        deps = tuple(d for d in deps if d is not None)
        op = _Op(engine, fn, deps, len(self.ops[engine]))
        self.ops[engine].append(op)
        self.all_ops.append(op)
        return op

    def dma(self, engine, out, in_, dsem, deps=()):
        deps = tuple(d for d in deps if d is not None)

        def fn(eng):
            return eng.dma_start(out=out, in_=in_)

        op = _Op(engine, fn, deps, len(self.ops[engine]), dma_sem=dsem)
        dsem.count += 16
        op.value = dsem.count
        self.ops[engine].append(op)
        self.all_ops.append(op)
        return op

    def barrier_token(self, engine):
        ops = self.ops[engine]
        return ops[-1] if ops else None

    def all_tokens(self):
        return [self.barrier_token(e) for e in ENGINES if self.ops[e]]

    def emit(self):
        for op in self.all_ops:
            for d in op.deps:
                d.marked = True
        counts = {e: 0 for e in ENGINES}
        for e in ENGINES:
            for op in self.ops[e]:
                if op.marked and op.dma_sem is None:
                    counts[e] += 1
                    op.value = counts[e]
        self.final_counts = dict(counts)
        self.inst_map = {}
        nc, sems, ops = self.nc, self.sems, self.ops
        with nc.Block() as block:
            def mk(e):
                def body(eng):
                    seen = {}
                    for op in ops[e]:
                        for d in op.deps:
                            if d.value is None:
                                raise RuntimeError(f"unmarked dep on {e} #{op.idx}")
                            if d.dma_sem is not None:
                                key, sem = d.dma_sem, d.dma_sem.sem
                                val = d.dma_sem.count if d.dma_sem.group else d.value
                            else:
                                key, sem, val = d.engine, sems[d.engine], d.value
                            if val > seen.get(key, 0):
                                eng.wait_ge(sem, val)
                                seen[key] = val
                        inst = op.fn(eng)
                        try:
                            self.inst_map[inst.ins.name] = (e, op.idx, getattr(op, 'label', None))
                        except Exception:
                            pass
                        if op.dma_sem is not None:
                            inst.then_inc(op.dma_sem.sem, 16)
                        elif op.marked:
                            inst.then_inc(sems[e], 1)
                    return None
                return body
            for e in ENGINES:
                getattr(block, e)(mk(e))


HID = 1024
NCH = 8
HALO = 1152
OW = 4096
TEXT = HALO + OW
NT_EXT = TEXT // 128
NT_REST = 4096 // 128
HEADS_A = 16
DA = 64
HEADS_F = 8
DF = 64
WIN = 512
INNER = 512
HPAD = 2816
H2 = 2 * HPAD
MEGA = 1024

L_CFG = [
    dict(blk0_col=128, nblk=10, mlp_col0=512, mlp_ntok=4736),
    dict(blk0_col=1152, nblk=8, mlp_col0=1152, mlp_ntok=4096),
]

NEG = -1.0e30
DBG_TAP = False
P2STOP = 99


def _perm_head_major():
    return np.array([d * 16 + h for h in range(16) for d in range(64)], dtype=np.int64)


def _rotary_tables():
    D = 64
    half = np.arange(0, D, 2, dtype=np.float64)
    inv_freq = 1.0 / 10000.0 ** (half / D)
    t = np.arange(8192, dtype=np.float64)
    freqs = t[:, None] * inv_freq[None, :]
    freqs = np.concatenate([freqs, freqs], axis=1)
    base = (half + 0.4 * D) / (1.4 * D)
    power = (t - 4096.0) / float(WIN // 2)
    scale = base[None, :] ** power[:, None]
    scale = np.concatenate([scale, scale], axis=1)
    cos = np.cos(freqs)
    sin = np.sin(freqs)
    cs = cos * scale
    csk = cos / scale
    sgn = np.concatenate([-np.ones(32), np.ones(32)])[None, :]
    ssn = sin * scale * sgn
    sskn = sin / scale * sgn
    return (cs.astype(np.float32), ssn.astype(np.float32),
            csk.astype(np.float32), sskn.astype(np.float32))


def _local_masks():
    qq = np.arange(128)[:, None]
    kk = np.arange(128)[None, :]
    mq_lo = (kk < qq).astype(np.float32)
    mq_hi = (kk > qq).astype(np.float32)
    return mq_lo, mq_hi, mq_lo.T.copy(), mq_hi.T.copy()


def make_core_inputs(inputs, core):
    b, h = core // 2, core % 2
    s = 4096 * h
    perm = _perm_head_major()
    x = np.asarray(inputs["L0_attn_x"])
    xT = np.ascontiguousarray(x[b].T)

    x_extT = np.zeros((HID, TEXT), np.float32)
    lo = s - HALO
    src_lo = max(lo, 0)
    x_extT[:, src_lo - lo:] = xT[:, src_lo:s + OW]

    x_restT = np.zeros((HID, 4096), np.float32)
    if h == 0:
        x_restT[:, :] = xT[:, 4096:8192]
        rest_valid = 4096
    else:
        x_restT[:, :2944] = xT[:, 0:2944]
        rest_valid = 2944

    ext_valid = np.zeros(TEXT, np.float32)
    ext_valid[HALO - min(HALO, s):] = 1.0
    if h == 1:
        ext_valid[:] = 1.0
    mask_ext_col = ext_valid.reshape(NT_EXT, 128).T.copy()
    rv = np.zeros(4096, np.float32)
    rv[:rest_valid] = 1.0
    mask_rest_col = rv.reshape(NT_REST, 128).T.copy()

    mrow = np.where(ext_valid > 0, 0.0, NEG).astype(np.float32)
    maskrow_col = mrow.reshape(NT_EXT, 128).T.copy()

    d = {
        "x_extT": x_extT, "x_restT": x_restT,
        "mask_ext_col": mask_ext_col, "mask_rest_col": mask_rest_col,
        "maskrow_col": maskrow_col,
    }

    d["Wh"] = np.ascontiguousarray(np.asarray(inputs["L0_attn_Wh"])[:, perm])
    d["Wq"] = np.ascontiguousarray(np.asarray(inputs["L0_attn_Wq"])[:, perm])
    d["Wk"] = np.ascontiguousarray(np.asarray(inputs["L0_attn_Wk"])[:, perm])
    d["Wv"] = np.ascontiguousarray(np.asarray(inputs["L0_attn_Wv"])[:, perm])
    d["Wo"] = np.ascontiguousarray(np.asarray(inputs["L0_attn_Wo"])[perm, :])
    d["bh_row"] = np.asarray(inputs["L0_attn_bh"])[perm][None, :].copy()
    d["beta_row"] = np.asarray(inputs["L0_attn_beta"])[perm][None, :].copy()
    d["bk_row"] = np.asarray(inputs["L0_attn_bk"])[perm][None, :].copy()
    d["bv_row"] = np.asarray(inputs["L0_attn_bv"])[perm][None, :].copy()
    d["bq_col"] = np.ascontiguousarray(
        np.asarray(inputs["L0_attn_bq"])[perm].reshape(NCH, 128).T)
    d["bo_col"] = np.ascontiguousarray(
        np.asarray(inputs["L0_attn_bo"]).reshape(NCH, 128).T)
    d["lnw_rep"] = np.ascontiguousarray(
        np.broadcast_to(np.asarray(inputs["L0_attn_ln_w"])[perm], (128, HID)))
    d["lnb_rep"] = np.ascontiguousarray(
        np.broadcast_to(np.asarray(inputs["L0_attn_ln_b"])[perm], (128, HID)))

    w1 = np.asarray(inputs["L0_ffn_w1"])
    b1 = np.asarray(inputs["L0_ffn_b1"])
    w2 = np.asarray(inputs["L0_ffn_w2"])
    for l in range(2):
        d[f"wqkv{l}"] = np.ascontiguousarray(np.asarray(inputs["L0_ffn_wqkv"])[l])
        d[f"wo{l}"] = np.ascontiguousarray(np.asarray(inputs["L0_ffn_wo"])[l])
        w1p = np.zeros((HID, H2), np.float32)
        w1p[:, 0:2730] = w1[l][:, 0:2730]
        w1p[:, HPAD:HPAD + 2730] = w1[l][:, 2730:5460]
        d[f"w1{l}"] = w1p
        b1p = np.zeros((1, H2), np.float32)
        b1p[0, 0:2730] = b1[l][0:2730]
        b1p[0, HPAD:HPAD + 2730] = b1[l][2730:5460]
        d[f"b1row{l}"] = b1p
        w2p = np.zeros((HPAD, HID), np.float32)
        w2p[0:2730, :] = w2[l]
        d[f"w2{l}"] = w2p
        d[f"lg_col{l}"] = np.ascontiguousarray(
            np.asarray(inputs["L0_ffn_ln_g"])[l].reshape(NCH, 128).T)
        d[f"lb_col{l}"] = np.ascontiguousarray(
            np.asarray(inputs["L0_ffn_ln_b"])[l].reshape(NCH, 128).T)
        d[f"b2_col{l}"] = np.ascontiguousarray(
            np.asarray(inputs["L0_ffn_b2"])[l].reshape(NCH, 128).T)
        d[f"qs_rep{l}"] = np.ascontiguousarray(np.broadcast_to(
            np.tile(np.asarray(inputs["L0_ffn_q_scale"])[l], HEADS_F), (128, INNER)))
        d[f"ks_rep{l}"] = np.ascontiguousarray(np.broadcast_to(
            np.tile(np.asarray(inputs["L0_ffn_k_scale"])[l], HEADS_F), (128, INNER)))

    cs, ssn, csk, sskn = _rotary_tables()
    pos = np.clip(np.arange(s - HALO, s + OW), 0, 8191)
    d["tab"] = np.ascontiguousarray(
        np.stack([cs[pos], ssn[pos], csk[pos], sskn[pos]], axis=1)
    ).astype(np.float32)
    mq_lo, mq_hi, mk_lo, mk_hi = _local_masks()
    d["mq_lo"], d["mq_hi"], d["mk_lo"], d["mk_hi"] = mq_lo, mq_hi, mk_lo, mk_hi
    d["ident"] = np.eye(128, dtype=np.float32)
    o2 = np.zeros((2, MEGA), np.float32)
    o2[1, :] = 1.0
    d["ones2row"] = o2
    d["negI"] = (NEG * np.eye(128)).astype(np.float32)
    hs = np.zeros((8, 8, 128), np.float32)
    for hh in range(8):
        hs[hh, hh, :] = 1.0
    d["hsel"] = hs
    e4 = np.zeros((1, 16), np.float32)
    for c in range(4):
        e4[0, 4 * c + c] = 1.0
    d["e4"] = e4
    d["halo01"] = np.full((128, 1), 1.0 if s > 0 else 0.0, np.float32)
    return d



from contextlib import ExitStack


class KB:

    def __init__(self, nc, ctx):
        self.nc = nc
        self.ctx = ctx
        self.s = Sched(nc)
        self.inp = {}
        self.sb = {}

    def din(self, name, shape):
        ap = self.nc.dram_tensor(name, list(shape), F32, kind="ExternalInput").ap()
        self.inp[name] = ap
        return ap

    def sbuf(self, name, shape, dtype=BF16, ctx=None):
        return (ctx or self.ctx).enter_context(
            self.nc.sbuf_tensor(name, list(shape), dtype))

    def psum(self, name, shape, dtype=F32, ctx=None):
        return (ctx or self.ctx).enter_context(
            self.nc.psum_tensor(name, list(shape), dtype))

    def gate(self):
        return [t for t in (self.s.barrier_token(e) for e in ENGINES) if t is not None]

    def build_io(self):
        d = self.din
        d("x_extT", (HID, TEXT))
        d("x_restT", (HID, 4096))
        d("mask_ext_col", (128, NT_EXT))
        d("mask_rest_col", (128, NT_REST))
        d("maskrow_col", (128, NT_EXT))
        for w in ("Wh", "Wq", "Wk", "Wv", "Wo"):
            d(w, (HID, HID))
        for r in ("bh_row", "beta_row", "bk_row", "bv_row"):
            d(r, (1, HID))
        d("bq_col", (128, NCH))
        d("bo_col", (128, NCH))
        d("lnw_rep", (128, HID))
        d("lnb_rep", (128, HID))
        for l in range(2):
            d(f"wqkv{l}", (HID, 3 * INNER))
            d(f"wo{l}", (INNER, HID))
            d(f"w1{l}", (HID, H2))
            d(f"b1row{l}", (1, H2))
            d(f"w2{l}", (HPAD, HID))
            d(f"lg_col{l}", (128, NCH))
            d(f"lb_col{l}", (128, NCH))
            d(f"b2_col{l}", (128, NCH))
            d(f"qs_rep{l}", (128, INNER))
            d(f"ks_rep{l}", (128, INNER))
        d("tab", (TEXT, 4, DF))
        d("ones2row", (2, MEGA))
        d("halo01", (128, 1))
        d("e4", (1, 16))
        d("hsel", (8, 8, 128))
        for m in ("mq_lo", "mq_hi", "mk_lo", "mk_hi", "ident", "negI"):
            d(m, (128, 128))
        self.outT = self.nc.dram_tensor("outT", [HID, OW], F32, kind="ExternalOutput").ap()
        self.dbg_qf = self.nc.dram_tensor("dbg_qf", [128, 4, 512], F32, kind="ExternalOutput").ap()
        self.dbg_kf = self.nc.dram_tensor("dbg_kf", [128, 4, 512], F32, kind="ExternalOutput").ap()
        self.dbg_vt = self.nc.dram_tensor("dbg_vt", [128, 4, 520], F32, kind="ExternalOutput").ap()
        self.dbg_m = self.nc.dram_tensor("dbg_m", [8, TEXT], F32, kind="ExternalOutput").ap()
        self.dbg_o = self.nc.dram_tensor("dbg_o", [128, 4, 512], F32, kind="ExternalOutput").ap()
        self.dbg_kf2 = self.nc.dram_tensor("dbg_kf2", [128, 4, 512], F32, kind="ExternalOutput").ap()
        self.dbg_vt2 = self.nc.dram_tensor("dbg_vt2", [128, 4, 520], F32, kind="ExternalOutput").ap()
        self.x1b = self.nc.dram_tensor("x1b", [128, NCH, OW], BF16, kind="Internal").ap()
        self.w1bf = [self.nc.dram_tensor(f"w1bf{l}", [HID, H2], BF16, kind="Internal").ap()
                     for l in range(2)]
        self.w2bf = [self.nc.dram_tensor(f"w2bf{l}", [HPAD, HID], BF16, kind="Internal").ap()
                     for l in range(2)]

    def build_consts(self):
        s, sb, inp = self.s, self.sb, self.inp
        for n, shp, dt in [
            ("ones_row", (1, 640), BF16), ("ones_col", (128, 1), BF16),
            ("identF", (128, 128), F32),
            ("negI", (128, 128), BF16),
            ("mk_lo", (128, 128), BF16), ("mk_hi", (128, 128), BF16),
            ("eps", (128, 1), F32), ("maskrow", (128, NT_EXT), F32),
            ("zrow", (1, 512), BF16),
            ("X", (128, NCH, TEXT), BF16),
            ("lg_col0", (128, NCH), BF16), ("lb_col0", (128, NCH), BF16),
            ("lg_col1", (128, NCH), BF16), ("lb_col1", (128, NCH), BF16),
            ("b2_col0", (128, NCH), F32), ("b2_col1", (128, NCH), F32),
            ("halo01", (128, 1), F32), ("e4", (1, 16), F32),
        ]:
            sb[n] = self.sbuf(n + "_g", shp, dt)

        dco = s.new_dma_sem("consts", group=True)
        g = "gpsimd"
        for m in ("negI", "mk_lo", "mk_hi"):
            s.dma(g, sb[m][:, :], inp[m][:, :], dco)
        s.dma(g, sb["identF"][:, :], inp["ident"][:, :], dco)
        s.dma(g, sb["maskrow"][:, :], inp["maskrow_col"][:, :], dco)
        s.dma(g, sb["halo01"][:, :], inp["halo01"][:, :], dco)
        s.dma(g, sb["e4"][:, :], inp["e4"][:, :], dco)
        for l in range(2):
            s.dma(g, sb[f"lg_col{l}"][:, :], inp[f"lg_col{l}"][:, :], dco)
            s.dma(g, sb[f"lb_col{l}"][:, :], inp[f"lb_col{l}"][:, :], dco)
            s.dma(g, sb[f"b2_col{l}"][:, :], inp[f"b2_col{l}"][:, :], dco)
        t_cd = s.ops[g][-1]
        s.add(g, lambda gg: gg.memset(sb["ones_row"][0:1, 0:640], 1.0), [t_cd])
        s.add(g, lambda gg: gg.memset(sb["ones_col"][:, :], 1.0))
        s.add(g, lambda gg: gg.memset(sb["zrow"][:, :], 0.0))
        t_ms = s.add(g, lambda gg: gg.memset(sb["eps"][:, :], 1e-5))
        self.d_consts = dco
        self.t_consts = t_ms

        dX = s.new_dma_sem("xload", group=True)
        xin = inp["x_extT"].rearrange("(c p) t -> p c t", p=128)
        tx = None
        for i in range(4):
            tx = s.dma(g, sb["X"][:, 2 * i:2 * i + 2, :], xin[:, 2 * i:2 * i + 2, :], dX)
        self.t_X = tx

        dwc = s.new_dma_sem("wcast", group=True)
        tw = None
        for l in range(2):
            for i in range(4):
                tw = s.dma(g, self.w1bf[l][:, i * (H2 // 4):(i + 1) * (H2 // 4)],
                           inp[f"w1{l}"][:, i * (H2 // 4):(i + 1) * (H2 // 4)], dwc)
            for i in range(4):
                tw = s.dma(g, self.w2bf[l][i * (HPAD // 4):(i + 1) * (HPAD // 4), :],
                           inp[f"w2{l}"][i * (HPAD // 4):(i + 1) * (HPAD // 4), :], dwc)
        self.t_wcast = tw

    def build_p1(self):
        s, nc, sb, inp = self.s, self.nc, self.sb, self.inp
        sb["kv_sb"] = self.sbuf("kv_sb", (128, 1040), BF16, self.es_kv)
        sb["kv2"] = self.sbuf("kv2", (128, 1040), BF16, self.es_kv)
        with ExitStack() as es:
            Wk = self.sbuf("WkSb", (128, NCH, HID), BF16, es)
            Wv = self.sbuf("WvSb", (128, NCH, HID), BF16, es)
            xr = [self.sbuf(f"xr{i}", (128, NCH, 128), BF16, es) for i in range(2)]
            ktile = self.sbuf("ktile", (128, HID), BF16, es)
            brow1 = self.sbuf("brow1", (1, 2 * HID), BF16, es)
            vtile = self.sbuf("vtile", (128, 16 * 65), BF16, es)
            sb["mask_ext"] = self.sbuf("mask_ext_s", (128, NT_EXT), F32, es)
            sb["mask_rest"] = self.sbuf("mask_rest_s", (128, NT_REST), F32, es)
            kps = [self.psum(f"kps{i}", (128, 512), F32, es) for i in range(2)]
            vps = [self.psum(f"vps{i}", (128, 512), F32, es) for i in range(2)]
            kvp = [self.psum(f"kvp{i}", (64, 512), F32, es) for i in range(3)]

            dW = s.new_dma_sem("p1w", group=True)
            s.dma("gpsimd", sb["mask_ext"][:, :], inp["mask_ext_col"][:, :], dW)
            s.dma("gpsimd", sb["mask_rest"][:, :], inp["mask_rest_col"][:, :], dW)
            s.dma("gpsimd", brow1[0:1, 0:HID], inp["bk_row"][:, :], dW)
            s.dma("gpsimd", brow1[0:1, HID:2 * HID], inp["bv_row"][:, :], dW)
            wk_in = inp["Wk"].rearrange("(c p) o -> p c o", p=128)
            wv_in = inp["Wv"].rearrange("(c p) o -> p c o", p=128)
            tw = None
            for i in range(4):
                s.dma("gpsimd", Wk[:, 2 * i:2 * i + 2, :], wk_in[:, 2 * i:2 * i + 2, :], dW)
                tw = s.dma("gpsimd", Wv[:, 2 * i:2 * i + 2, :], wv_in[:, 2 * i:2 * i + 2, :], dW)
            vones = s.add("vector", lambda v: v.memset(
                bass.AP(vtile, 64, [[16 * 65, 128], [65, 16]]), 1.0), [])

            xrest_in = inp["x_restT"].rearrange("(c p) t -> p c t", p=128)
            dxr = [s.new_dma_sem(f"xr{i}") for i in range(2)]

            t_kevac = t_vevac = t_kmask = t_vcopy = t_kvmm = None
            t_xr_free = [None, None]
            started = {}
            n_total = NT_EXT + NT_REST

            for it in range(n_total):
                is_ext = it < NT_EXT
                ti = it if is_ext else it - NT_EXT
                if is_ext:
                    def xsl(c, ti=ti):
                        return sb["X"][:, c, 128 * ti:128 * (ti + 1)]
                    xdep = self.t_X
                    mcol = sb["mask_ext"][:, ti:ti + 1]
                else:
                    slot = ti % 2
                    ld = s.dma("gpsimd", xr[slot][:, :, :],
                               xrest_in[:, :, 128 * ti:128 * (ti + 1)], dxr[slot],
                               deps=[t_xr_free[slot]])

                    def xsl(c, slot=slot):
                        return xr[slot][:, c, :]
                    xdep = ld
                    mcol = sb["mask_rest"][:, ti:ti + 1]

                mm = None
                for half in range(2):
                    for c in range(NCH):
                        deps = [xdep, tw, t_kevac, t_kmask, self.t_consts] if (half == 0 and c == 0) else []
                        mm = s.add("tensor", lambda t, c=c, half=half, xs=xsl, st=(c == 0): t.matmul(
                            kps[half][:, :], xs(c), Wk[:, c, 512 * half:512 * (half + 1)],
                            start=st, stop=False), deps)
                    mm = s.add("tensor", lambda t, half=half: t.matmul(
                        kps[half][:, :], sb["ones_row"][0:1, 0:128],
                        brow1[0:1, 512 * half:512 * (half + 1)],
                        start=False, stop=True), [tw, self.t_consts] if it == 0 else [])
                t_kmm = mm
                for half in range(2):
                    for c in range(NCH):
                        deps = [t_vevac] if (half == 0 and c == 0) else []
                        mm = s.add("tensor", lambda t, c=c, half=half, xs=xsl, st=(c == 0): t.matmul(
                            vps[half][:, :], xs(c), Wv[:, c, 512 * half:512 * (half + 1)],
                            start=st, stop=False), deps)
                    mm = s.add("tensor", lambda t, half=half: t.matmul(
                        vps[half][:, :], sb["ones_row"][0:1, 0:128],
                        brow1[0:1, HID + 512 * half:HID + 512 * (half + 1)],
                        start=False, stop=True), [])
                t_vmm = mm
                if not is_ext:
                    t_xr_free[ti % 2] = t_vmm

                t1 = s.add("scalar", lambda sc: sc.activation(
                    ktile[:, 0:512], kps[0][:, :], AF.Sigmoid), [t_kmm, t_kvmm])
                t_kevac = s.add("scalar", lambda sc: sc.activation(
                    ktile[:, 512:1024], kps[1][:, :], AF.Sigmoid), [t1])
                t_kmask = s.add("vector", lambda v, mcol=mcol: v.tensor_scalar(
                    ktile[:, :], ktile[:, :], mcol, None, ALU.mult),
                    [t_kevac, self.t_consts])
                t1 = s.add("vector", lambda v: v.tensor_copy(
                    bass.AP(vtile, 0, [[16 * 65, 128], [65, 8], [1, 64]]),
                    vps[0][:, :].rearrange("p (h m) -> p h m", m=64)),
                    [t_vmm, t_kvmm, vones])
                t_vcopy = s.add("vector", lambda v: v.tensor_copy(
                    bass.AP(vtile, 8 * 65, [[16 * 65, 128], [65, 8], [1, 64]]),
                    vps[1][:, :].rearrange("p (h m) -> p h m", m=64)), [t1])
                t_vevac = t_vcopy

                for hd in range(16):
                    piece = 0 if hd < 7 else (1 if hd < 14 else 2)
                    cb = hd - (0, 7, 14)[piece]
                    st = not started.get(piece, False)
                    started[piece] = True
                    last = (it == n_total - 1) and (hd in (6, 13, 15))
                    mm = s.add("tensor", lambda t, hd=hd, piece=piece, cb=cb, st=st, last=last: t.matmul(
                        kvp[piece][:, 65 * cb:65 * cb + 65],
                        ktile[:, 64 * hd:64 * (hd + 1)],
                        vtile[:, 65 * hd:65 * (hd + 1)],
                        start=st, stop=last),
                        [t_kmask, t_vcopy] if hd == 0 else [])
                t_kvmm = mm

            mzk1 = s.add("vector", lambda v: v.memset(sb["kv_sb"][64:128, :], 0.0))
            mzk2 = s.add("vector", lambda v: v.memset(sb["kv2"][0:64, :], 0.0), [mzk1])
            e1 = s.add("vector", lambda v: v.tensor_copy(
                sb["kv_sb"][0:64, 0:455], kvp[0][:, 0:455]), [t_kvmm])
            e2 = s.add("vector", lambda v: v.tensor_copy(
                sb["kv_sb"][0:64, 455:910], kvp[1][:, 0:455]), [e1])
            e3 = s.add("vector", lambda v: v.tensor_copy(
                sb["kv_sb"][0:64, 910:1040], kvp[2][:, 0:130]), [e2])
            self.t_kv = s.dma("sync", sb["kv2"][64:128, :], sb["kv_sb"][0:64, :],
                              s.new_dma_sem("kvdup", group=True),
                              deps=[e3, mzk2])

    def build_p2(self):
        s, nc, sb, inp = self.s, self.nc, self.sb, self.inp
        gate = self.gate()
        with ExitStack() as es:
            Wq = self.sbuf("WqSb", (128, NCH, HID), BF16, es)
            Wh = self.sbuf("WhSb", (128, NCH, HID), BF16, es)
            Wo = self.sbuf("WoSb", (128, NCH, HID), BF16, es)
            lnw_rep = self.sbuf("lnw_rep_s", (128, HID), BF16, es)
            lnb_rep = self.sbuf("lnb_rep_s", (128, HID), BF16, es)
            brow2 = self.sbuf("brow2", (1, 2 * HID), BF16, es)
            qf = self.sbuf("qf", (128, NCH, 512), BF16, es)
            gatedT = self.sbuf("gatedT", (128, NCH, 512), BF16, es)
            ao = self.sbuf("ao", (128, NCH, 512), BF16, es)
            htok = self.sbuf("htok", (128, HID), BF16, es)
            attn = self.sbuf("attn", (128, HID), BF16, es)
            lnout = self.sbuf("lnout", (128, HID), BF16, es)
            gated = self.sbuf("gated", (128, HID), BF16, es)
            bn = self.sbuf("bn", (128, 16), F32, es)
            sb["ident"] = self.sbuf("ident_s", (128, 128), BF16, es)
            sb["bq_col"] = self.sbuf("bq_col_s", (128, NCH), F32, es)
            sb["bo_col"] = self.sbuf("bo_col_s", (128, NCH), F32, es)
            denb = self.sbuf("denb", (128, 16), F32, es)
            xsq = self.sbuf("xsq", (128, NCH, 512), BF16, es)
            rowA = self.sbuf("rowA", (1, 512), F32, es)
            rowB = self.sbuf("rowB", (1, 512), F32, es)
            rowC = self.sbuf("rowC", (1, 512), F32, es)
            rowBb = self.sbuf("rowBb", (1, 512), BF16, es)
            rowCb = self.sbuf("rowCb", (1, 512), BF16, es)
            P = [self.psum(f"p2b{i}", (128, 512), F32, es) for i in range(7)]
            TP = self.psum("p2tp", (128, 1024), BF16, es)

            dW = s.new_dma_sem("p2w", group=True)
            s.dma("gpsimd", sb["ident"][:, :], inp["ident"][:, :], dW, deps=gate)
            s.dma("gpsimd", sb["bq_col"][:, :], inp["bq_col"][:, :], dW)
            s.dma("gpsimd", sb["bo_col"][:, :], inp["bo_col"][:, :], dW)
            s.dma("gpsimd", lnw_rep[:, :], inp["lnw_rep"][:, :], dW)
            s.dma("gpsimd", lnb_rep[:, :], inp["lnb_rep"][:, :], dW)
            s.dma("gpsimd", brow2[0:1, 0:HID], inp["bh_row"][:, :], dW)
            t_bb = s.dma("gpsimd", brow2[0:1, HID:2 * HID], inp["beta_row"][:, :], dW)
            t_brow = s.add("vector", lambda v: v.tensor_add(
                brow2[0:1, 0:HID], brow2[0:1, 0:HID],
                brow2[0:1, HID:2 * HID]), [t_bb])
            tw = None
            for nm, t in (("Wq", Wq), ("Wh", Wh), ("Wo", Wo)):
                win = inp[nm].rearrange("(c p) o -> p c o", p=128)
                for i in range(2):
                    tw = s.dma("gpsimd", t[:, 4 * i:4 * i + 4, :],
                               win[:, 4 * i:4 * i + 4, :], dW, deps=gate if nm == "Wq" and i == 0 else [])
            t_w = tw
            dx1 = s.new_dma_sem("x1store", group=True)
            self.d_x1b = dx1

            HEAD_PIECE = [(0, 0), (0, 1), (0, 2), (0, 3), (0, 4), (0, 5), (0, 6),
                          (1, 0), (1, 1), (1, 2), (1, 3), (1, 4), (1, 5), (1, 6),
                          (2, 0), (2, 1)]
            NP = (7, 7, 2)
            KV = sb["kv_sb"]

            groups = [(512 * g, 512) for g in range(10)] + [(5120, 128)]
            t_qevac = [None, None]
            t_hevac = None
            t_div = None
            t_tevac = None
            t_aoevac = None
            t_resid = None
            t_stats = None
            t_x1 = None
            t_wo = None
            t_num = None
            t_sq = None
            t_g_prev = None

            for (g0, Ng) in groups:
                ntile = Ng // 128
                for c in range(NCH):
                    pb = c % 2
                    for fin in range(NCH):
                        deps = []
                        if c == 0 and fin == 0:
                            deps = [t_w, self.t_X, t_div, t_aoevac, t_stats] + (gate if g0 == 0 else [])
                        if fin == 0:
                            deps = deps + [t_qevac[pb]]
                        mm = s.add("tensor", lambda t, c=c, fin=fin, pb=pb, g0=g0, Ng=Ng, st=(fin == 0): t.matmul(
                            P[pb][:, 0:Ng], Wq[:, fin, 128 * c:128 * (c + 1)],
                            sb["X"][:, fin, g0:g0 + Ng],
                            start=st, stop=(fin == NCH - 1)), deps)
                    t_qevac[pb] = s.add("scalar", lambda sc, c=c, pb=pb, Ng=Ng: sc.activation(
                        qf[:, c, 0:Ng], P[pb][:, 0:Ng], AF.Sigmoid,
                        bias=sb["bq_col"][:, c:c + 1]), [mm])
                t_qf = t_qevac[(NCH - 1) % 2]

                for ti in range(ntile):
                    tcol = g0 + 128 * ti
                    ext_ti = tcol // 128
                    if P2STOP < 2:
                        continue
                    mm = None
                    for half in range(2):
                        pb = 2 + half
                        for c in range(NCH):
                            deps = []
                            if half == 0 and c == 0:
                                deps = [t_hevac, t_w]
                            mm = s.add("tensor", lambda t, c=c, half=half, pb=pb, tcol=tcol, st=(c == 0): t.matmul(
                                P[pb][:, :], sb["X"][:, c, tcol:tcol + 128],
                                Wh[:, c, 512 * half:512 * (half + 1)],
                                start=st, stop=False), deps)
                        mm = s.add("tensor", lambda t, half=half, pb=pb: t.matmul(
                            P[pb][:, :], sb["ones_row"][0:1, 0:128],
                            brow2[0:1, 512 * half:512 * (half + 1)],
                            start=False, stop=True), [t_brow] if g0 == 0 and ti == 0 else [])
                    t_hmm = mm
                    th1 = s.add("vector", lambda v: v.tensor_copy(
                        htok[:, 0:512], P[2][:, :]), [t_hmm, t_tevac])
                    t_hevac = s.add("vector", lambda v: v.tensor_copy(
                        htok[:, 512:1024], P[3][:, :]), [th1])

                    if P2STOP < 3:
                        continue
                    mm = None
                    for hd in range(16):
                        piece, slot = HEAD_PIECE[hd]
                        deps = []
                        if hd == 0:
                            deps = [t_qf, self.t_kv, t_div, t_stats]
                        KVp = KV if hd % 2 == 0 else sb["kv2"]
                        mm = s.add("tensor", lambda t, hd=hd, piece=piece, slot=slot, ti=ti, KVp=KVp: t.matmul(
                            P[4 + piece][:, 65 * slot:65 * slot + 65],
                            qf[:, hd // 2, 128 * ti:128 * (ti + 1)],
                            KVp[:, 65 * hd:65 * hd + 65],
                            start=(slot == 0), stop=(slot == NP[piece] - 1)), deps)
                    t_num = mm

                    if P2STOP < 4:
                        continue
                    td = None
                    for piece in range(3):
                        nh = NP[piece]
                        base = (0, 7, 14)[piece]
                        td = s.add("scalar", lambda sc, piece=piece, nh=nh, base=base: sc.copy(
                            bass.AP(denb, base, [[16, 128], [1, nh], [1, 1]]),
                            bass.AP(P[4 + piece], 64, [[512, 128], [65, nh], [1, 1]])),
                            [t_num, t_g_prev] if piece == 0 else [])
                        td = s.add("vector", lambda v, nh=nh, base=base: v.reciprocal(
                            denb[:, base:base + nh], denb[:, base:base + nh]), [td])
                        td = s.add("vector", lambda v, piece=piece, nh=nh, base=base: v.tensor_tensor(
                            bass.AP(attn, 64 * base, [[HID, 128], [64, nh], [1, 64]]),
                            bass.AP(P[4 + piece], 0, [[512, 128], [65, nh], [1, 64]]),
                            bass.AP(denb, base, [[16, 128], [1, nh], [0, 64]]),
                            ALU.mult), [td])
                    t_div = td

                    if P2STOP < 5:
                        continue
                    b1 = s.add("vector", lambda v: v.bn_stats(bn[:, 0:6], attn[:, 0:512]), [t_div])
                    b2 = s.add("vector", lambda v: v.bn_stats(bn[:, 6:12], attn[:, 512:1024]), [b1])
                    b3 = s.add("vector", lambda v: v.bn_aggr(bn[:, 12:14], bn[:, 0:12]), [b2])
                    sq = s.add("scalar", lambda sc: sc.activation(
                        bn[:, 14:15], bn[:, 13:14], AF.Sqrt, bias=sb["eps"][:, :]), [b3])
                    t_sq = s.add("vector", lambda v: v.reciprocal(bn[:, 15:16], bn[:, 14:15]), [sq])
                    t_g = s.add("vector", lambda v: v.tensor_scalar(
                        lnout[:, :], attn[:, :], bn[:, 12:13], bn[:, 15:16],
                        ALU.subtract, ALU.mult), [t_sq])
                    t_g = s.add("vector", lambda v: v.scalar_tensor_tensor(
                        lnout[:, :], lnout[:, :], 0.0, lnw_rep[:, :],
                        ALU.bypass, ALU.mult), [t_g])
                    t_g = s.add("vector", lambda v: v.tensor_add(
                        lnout[:, :], lnout[:, :], lnb_rep[:, :]), [t_g])
                    t_g = s.add("vector", lambda v: v.tensor_mul(
                        gated[:, :], lnout[:, :], htok[:, :]), [t_g, t_hevac])
                    t_g_prev = t_g

                    if P2STOP < 6:
                        continue
                    tt = None
                    for c in range(NCH):
                        deps = [t_g, self.t_consts] if c == 0 else []
                        if c == 0:
                            deps = deps + [t_tevac]
                        tt = s.add("tensor", lambda t, c=c: t.transpose(
                            TP[:, 128 * c:128 * (c + 1)], gated[:, 128 * c:128 * (c + 1)],
                            sb["ident"][:, :]), deps)
                    t_tevac = s.add("scalar", lambda sc, ti=ti: sc.copy(
                        gatedT[:, :, 128 * ti:128 * (ti + 1)],
                        TP[:, :].rearrange("p (c x) -> p c x", x=128)), [tt])

                if P2STOP < 7:
                    continue
                mm = None
                for c in range(NCH):
                    pb = c % 2
                    for fin in range(NCH):
                        deps = []
                        if c == 0 and fin == 0:
                            deps = [t_tevac, t_qf]
                        if fin == 0:
                            deps = deps + [t_aoevac]
                        mm = s.add("tensor", lambda t, c=c, fin=fin, pb=pb, Ng=Ng, st=(fin == 0): t.matmul(
                            P[pb][:, 0:Ng], Wo[:, fin, 128 * c:128 * (c + 1)],
                            gatedT[:, fin, 0:Ng],
                            start=st, stop=(fin == NCH - 1)), deps)
                    t_aoevac = s.add("scalar", lambda sc, c=c, pb=pb, Ng=Ng: sc.activation(
                        ao[:, c, 0:Ng], P[pb][:, 0:Ng], AF.Relu,
                        bias=sb["bo_col"][:, c:c + 1]), [mm])

                if P2STOP < 8:
                    continue
                t_resid = s.add("vector", lambda v, g0=g0, Ng=Ng: v.tensor_add(
                    sb["X"][:, :, g0:g0 + Ng],
                    sb["X"][:, :, g0:g0 + Ng], ao[:, :, 0:Ng]), [t_aoevac, t_stats])

                if P2STOP < 9:
                    continue
                t_sqr = s.add("scalar", lambda sc, g0=g0, Ng=Ng: sc.activation(
                    xsq[:, :, 0:Ng], sb["X"][:, :, g0:g0 + Ng], AF.Square), [t_resid])
                mm = None
                for c in range(NCH):
                    mm = s.add("tensor", lambda t, c=c, g0=g0, Ng=Ng, st=(c == 0): t.matmul(
                        P[4][0:1, 0:Ng], sb["ones_col"][:, :], sb["X"][:, c, g0:g0 + Ng],
                        start=st, stop=(c == NCH - 1)),
                        [t_resid, t_num, t_div] if c == 0 else [])
                for c in range(NCH):
                    mm = s.add("tensor", lambda t, c=c, Ng=Ng, st=(c == 0): t.matmul(
                        P[5][0:1, 0:Ng], sb["ones_col"][:, :], xsq[:, c, 0:Ng],
                        start=st, stop=(c == NCH - 1)), [t_sqr] if c == 0 else [])
                t_smm = mm
                r1 = s.add("vector", lambda v, Ng=Ng: v.tensor_scalar(
                    rowA[0:1, 0:Ng], P[4][0:1, 0:Ng], 1.0 / HID, None, ALU.mult), [t_smm])
                r2 = s.add("vector", lambda v, Ng=Ng: v.tensor_scalar(
                    rowB[0:1, 0:Ng], P[5][0:1, 0:Ng], 1.0 / HID, None, ALU.mult), [r1])
                r3 = s.add("vector", lambda v, Ng=Ng: v.tensor_mul(
                    rowC[0:1, 0:Ng], rowA[0:1, 0:Ng], rowA[0:1, 0:Ng]), [r2])
                r4 = s.add("vector", lambda v, Ng=Ng: v.tensor_sub(
                    rowB[0:1, 0:Ng], rowB[0:1, 0:Ng], rowC[0:1, 0:Ng]), [r3])
                r5 = s.add("scalar", lambda sc, Ng=Ng: sc.activation(
                    rowB[0:1, 0:Ng], rowB[0:1, 0:Ng], AF.Sqrt, bias=sb["eps"][0:1, 0:1]), [r4])
                r6 = s.add("vector", lambda v, Ng=Ng: v.reciprocal(
                    rowB[0:1, 0:Ng], rowB[0:1, 0:Ng]), [r5])
                r7 = s.add("vector", lambda v, Ng=Ng: v.tensor_mul(
                    rowC[0:1, 0:Ng], rowA[0:1, 0:Ng], rowB[0:1, 0:Ng]), [r6])
                c1 = s.add("vector", lambda v, Ng=Ng: v.tensor_copy(
                    rowBb[0:1, 0:Ng], rowB[0:1, 0:Ng]), [r7])
                c2 = s.add("vector", lambda v, Ng=Ng: v.tensor_copy(
                    rowCb[0:1, 0:Ng], rowC[0:1, 0:Ng]), [c1])
                m1 = s.add("tensor", lambda t, Ng=Ng: t.matmul(
                    P[4][:, 0:Ng], sb["ones_row"][0:1, 0:128], rowBb[0:1, 0:Ng],
                    start=True, stop=True), [c1, r1])
                m2 = s.add("tensor", lambda t, Ng=Ng: t.matmul(
                    P[5][:, 0:Ng], sb["ones_row"][0:1, 0:128], rowCb[0:1, 0:Ng],
                    start=True, stop=True), [c2, r2])
                x1a = s.add("vector", lambda v, g0=g0, Ng=Ng: v.tensor_mul(
                    sb["X"][:, :, g0:g0 + Ng], sb["X"][:, :, g0:g0 + Ng],
                    bass.AP(P[4], 0, [[512, 128], [0, NCH], [1, Ng]])), [m1])
                t_x1 = x1a
                t_stats = s.add("vector", lambda v, g0=g0, Ng=Ng: v.tensor_sub(
                    sb["X"][:, :, g0:g0 + Ng], sb["X"][:, :, g0:g0 + Ng],
                    bass.AP(P[5], 0, [[512, 128], [0, NCH], [1, Ng]])), [t_x1, m2])

                lo = max(g0, HALO)
                hi = g0 + Ng
                if hi > lo:
                    s.dma("sync", self.x1b[:, :, lo - HALO:hi - HALO],
                          sb["X"][:, :, lo:hi], dx1, deps=[t_stats])
            self.t_p2 = s.add("vector", lambda v: v.tensor_mul(
                sb["X"][:, :, HALO - 1:HALO], sb["X"][:, :, HALO - 1:HALO],
                bass.AP(sb["halo01"], 0, [[1, 128], [0, NCH], [0, 1]])),
                [t_stats, self.t_consts])

    def build_layer(self, l, do_mlp=True):
        s, nc, sb, inp = self.s, self.nc, self.sb, self.inp
        gate = self.gate()
        if l == 0:
            pcol0, n_proj, qcol0, CL = 128, 10, 128, 1
        else:
            pcol0, n_proj, qcol0, CL = 128, 10, 640, 1
        sqd = (qcol0 - pcol0) // 512
        n_O = (TEXT - qcol0) // 512
        n_iter = n_proj + CL + 1
        Bshift = CL + sqd

        with ExitStack() as es:
            wqkv = self.sbuf(f"wqkv_l{l}", (128, NCH, 3 * INNER), BF16, es)
            wo = self.sbuf(f"wo_l{l}", (128, 4, HID), BF16, es)
            qs_rep = self.sbuf(f"qsr{l}", (128, INNER), BF16, es)
            ks_rep = self.sbuf(f"ksr{l}", (128, INNER), BF16, es)
            qf = [self.sbuf(f"qf{l}_{i}", (128, 4, 512), BF16, es) for i in range(3)]
            kf_e = [self.sbuf(f"kfe{l}_{i}", (128, 4, 512), BF16, es) for i in range(3)]
            kf_o = [self.sbuf(f"kfo{l}_{i}", (128, 4, 512), BF16, es) for i in range(3)]
            vt = [self.sbuf(f"vt{l}_{i}", (128, 4, 520), BF16, es) for i in range(3)]
            tabs = self.sbuf(f"tab{l}", (128, 4, 4, DF), BF16, es)
            pt = [self.sbuf(f"pt{l}_{i}", (128, 640), BF16, es) for i in range(3)]
            m8 = self.sbuf(f"m8_{l}", (128, 1), F32, es)
            maskrowH = self.sbuf(f"mrowH{l}", (128, NT_EXT), F32, es)
            O_acc = self.sbuf(f"Oacc{l}", (65, HEADS_F, 512), BF16, es)
            o_sb = [self.sbuf(f"osb{l}_0", (128, 4, 512), BF16, es)]
            tsum = self.sbuf(f"tsum{l}", (65, 512), F32, es)
            rrowb = self.sbuf(f"rrowb{l}", (1, 512), BF16, es)
            rrowf = self.sbuf(f"rrowf{l}", (1, 512), F32, es)
            r4sb = self.sbuf(f"r4sb{l}", (128, 4), F32, es)
            ubuf = self.sbuf(f"ubuf{l}", (128, 512), BF16, es)
            qhat = self.sbuf(f"qhat{l}", (128, 512), BF16, es)
            qrot = self.sbuf(f"qrot{l}", (128, 512), F32, es)
            krot = self.sbuf(f"krot{l}", (128, 512), F32, es)
            xqk = self.sbuf(f"xqk{l}", (128, 512), BF16, es)
            qraw = self.sbuf(f"qraw{l}", (128, 512), BF16, es)
            bnf = self.sbuf(f"bnf{l}", (128, 32), F32, es)
            P = [self.psum(f"L{l}b{i}", (128, 512), F32, es) for i in range(8)]

            dW = s.new_dma_sem(f"L{l}w", group=True)
            win = inp[f"wqkv{l}"].rearrange("(c p) o -> p c o", p=128)
            tw = None
            for i in range(4):
                tw = s.dma("gpsimd", wqkv[:, 2 * i:2 * i + 2, :],
                           win[:, 2 * i:2 * i + 2, :], dW, deps=gate if i == 0 else [])
            woin = inp[f"wo{l}"].rearrange("(c p) o -> p c o", p=128)
            tw = s.dma("gpsimd", wo[:, :, :], woin[:, :, :], dW)
            tw = s.dma("gpsimd", qs_rep[:, :], inp[f"qs_rep{l}"][:, :], dW)
            tw = s.dma("gpsimd", ks_rep[:, :], inp[f"ks_rep{l}"][:, :], dW)
            t_w = tw
            t_w8 = s.add("vector", lambda v: v.tensor_scalar(
                qs_rep[:, :], qs_rep[:, :], 8.0, None, ALU.mult), [t_w])
            t_mr = s.add("vector", lambda v: v.tensor_mul(
                ubuf[:, :], qs_rep[:, :], ks_rep[:, :]), [t_w8])
            t_mr = s.add("vector", lambda v: v.tensor_reduce(
                m8[:, 0:1], ubuf[:, :], AX.X, ALU.max), [t_mr])
            t_mrow = s.add("vector", lambda v: v.tensor_scalar(
                maskrowH[:, :], sb["maskrow"][:, :], m8[:, 0:1], None,
                ALU.subtract), [t_mr, self.t_consts])
            dtab = s.new_dma_sem(f"L{l}tab")

            mz = []
            mz.append(s.add("vector", lambda v: v.memset(kf_e[2][:, :, :], 0.0), gate))
            mz.append(s.add("vector", lambda v: v.memset(kf_o[2][:, :, :], 0.0), []))
            for i in range(2):
                mz.append(s.add("vector", lambda v, i=i: v.memset(
                    kf_e[i][64:128, :, :], 0.0), []))
                mz.append(s.add("vector", lambda v, i=i: v.memset(
                    kf_o[i][0:64, :, :], 0.0), []))
            for i in range(3):
                mz.append(s.add("vector", lambda v, i=i: v.memset(
                    bass.AP(vt[i], 64, [[4 * 520, 128], [520, 4], [65, 8]]), 1.0), []))
            mz.append(s.add("vector", lambda v: v.memset(O_acc[:, :, :], 0.0), []))
            t_init = mz[-1]

            t_tab = None
            t_tabuse = None
            t_vevac = {}
            t_qkevac = {}
            t_qke = {0: [None] * 4, 1: [None] * 4, 2: [None] * 4}
            t_expL = None
            t_expP = [None, None]
            t_ksq_last = None
            self.t_ttq = None
            self.t_ttk = None
            self.t_qe_prev = None
            self.t_ke_prev = None
            self.t_cpr = None
            t_pv = None
            t_oacc = None
            t_tt1 = None
            t_wo = None
            t_wo2 = None
            t_wres = None
            t_uq = None
            t_uk = None
            t_qsqr = None
            t_qrd = None
            t_rep = None
            t_tke_last = None
            t_projmm = None
            t_stmm = None

            def piece_tab(pw):
                nonlocal t_tab
                t_tab = s.dma("gpsimd", tabs[:, :, :, :],
                              inp["tab"][pcol0 + 512 * pw:pcol0 + 512 * pw + 512, :, :]
                              .rearrange("(j p) a d -> p j a d", p=128),
                              dtab, deps=[t_tabuse])

            def piece_v(pw):
                slot = pw % 3
                for j in range(4):
                    tc = pcol0 + 512 * pw + 128 * j
                    mm = None
                    for c in range(NCH):
                        off = -1 if c >= 4 else 0
                        deps = []
                        if c == 0:
                            deps = [t_vevac.get(slot), t_tt1, t_w,
                                    self.t_p2 if l == 0 else
                                    getattr(self, "t_halofix", None),
                                    getattr(self, "_dbg_tok", None)]
                            if pw == 0 and j == 0:
                                deps = deps + gate
                        mm = s.add("tensor", lambda t, c=c, tc=tc, off=off, st=(c == 0): t.matmul(
                            P[6][:, :], sb["X"][:, c, tc + off:tc + off + 128],
                            wqkv[:, c, 1024:1536],
                            start=st, stop=(c == NCH - 1)), deps)
                    t_vevac[slot] = s.add("vector", lambda v, slot=slot, j=j: v.tensor_copy(
                        bass.AP(vt[slot], 520 * j, [[4 * 520, 128], [65, 8], [1, 64]]),
                        P[6][:, :].rearrange("p (h m) -> p h m", m=64)),
                        [mm, t_init, t_pv, getattr(self, "_dbg_tok", None)])

            def piece_q(pw, j):
                nonlocal t_projmm, t_qsqr, t_qrd, t_uq
                tc = pcol0 + 512 * pw + 128 * j
                deps0 = [t_w, self.t_p2 if l == 0 else
                         getattr(self, "t_halofix", None)]
                mm = None
                for c in range(NCH):
                    off = -1 if c >= 4 else 0
                    deps = []
                    if c == 0:
                        deps = deps0 + [t_uq, t_tt1]
                    mm = s.add("tensor", lambda t, c=c, tc=tc, off=off, st=(c == 0): t.matmul(
                        P[2][:, :], sb["X"][:, c, tc + off:tc + off + 128],
                        wqkv[:, c, 0:512],
                        start=st, stop=(c == NCH - 1)), deps)
                t_projmm = mm
                t_qsqr = s.add("scalar", lambda sc: sc.activation(
                    xqk[:, :], P[2][:, :], AF.Square), [mm, t_qrd])
                rd = s.add("vector", lambda v: v.tensor_reduce(
                    bnf[:, 0:8], xqk[:, :].rearrange("p (h d) -> p h d", d=64),
                    AX.X, ALU.add), [t_qsqr])
                t_qrd = rd
                tqr = s.add("vector", lambda v: v.tensor_copy(
                    qraw[:, :], P[2][:, :]), [rd])
                t_uq = tqr

            def piece_k(pw, j):
                nonlocal t_projmm, t_ksq_last, t_qrd, t_uk, t_tabuse
                tc = pcol0 + 512 * pw + 128 * j
                mm = None
                for c in range(NCH):
                    off = -1 if c >= 4 else 0
                    deps = []
                    if c == 0:
                        deps = [t_uk, t_uq, t_tt1]
                    mm = s.add("tensor", lambda t, c=c, tc=tc, off=off, st=(c == 0): t.matmul(
                        P[2][:, :], sb["X"][:, c, tc + off:tc + off + 128],
                        wqkv[:, c, 512:1024],
                        start=st, stop=(c == NCH - 1)), deps)
                t_projmm = mm
                tks = s.add("scalar", lambda sc: sc.activation(
                    xqk[:, :], P[2][:, :], AF.Square), [mm, t_qrd])
                t_ksq_last = tks
                rd = s.add("vector", lambda v: v.tensor_reduce(
                    bnf[:, 8:16], xqk[:, :].rearrange("p (h d) -> p h d", d=64),
                    AX.X, ALU.add), [tks])
                t_qrd = rd
                sq = s.add("scalar", lambda sc: sc.activation(
                    bnf[:, 16:32], bnf[:, 0:16], AF.Sqrt), [rd])
                mx = s.add("vector", lambda v: v.tensor_scalar(
                    bnf[:, 16:32], bnf[:, 16:32], 1e-12, None, ALU.max), [sq])
                rc = s.add("vector", lambda v: v.reciprocal(
                    bnf[:, 16:32], bnf[:, 16:32]), [mx])
                t1 = s.add("vector", lambda v: v.tensor_tensor(
                    ubuf[:, :], qraw[:, :],
                    bass.AP(bnf, 16, [[32, 128], [1, 8], [0, 64]]),
                    ALU.mult), [rc])
                t2 = s.add("vector", lambda v: v.tensor_mul(
                    qhat[:, :], ubuf[:, :], qs_rep[:, :]), [t1, t_w8])
                ta = s.add("vector", lambda v, j=j: v.tensor_tensor(
                    ubuf[:, :], qhat[:, :],
                    bass.AP(tabs, 256 * j + 0 * DF, [[4 * 4 * DF, 128], [0, 8], [1, DF]]),
                    ALU.mult), [t2, t_tab])
                tb = s.add("vector", lambda v, j=j: v.tensor_tensor(
                    qrot[:, :].rearrange("p (h t x) -> p h t x", h=8, t=2),
                    bass.AP(qhat, 32, [[512, 128], [64, 8], [-32, 2], [1, 32]]),
                    bass.AP(tabs, 256 * j + 1 * DF, [[4 * 4 * DF, 128], [0, 8], [32, 2], [1, 32]]),
                    ALU.mult), [ta, self.t_ttq])
                tq = s.add("vector", lambda v: v.tensor_add(
                    qrot[:, :], qrot[:, :], ubuf[:, :]), [tb])
                t1 = s.add("vector", lambda v: v.tensor_tensor(
                    ubuf[:, :], P[2][:, :],
                    bass.AP(bnf, 24, [[32, 128], [1, 8], [0, 64]]),
                    ALU.mult), [tq])
                t_uk = t1
                t2 = s.add("vector", lambda v: v.tensor_mul(
                    qhat[:, :], ubuf[:, :], ks_rep[:, :]), [t1])
                ta = s.add("vector", lambda v, j=j: v.tensor_tensor(
                    ubuf[:, :], qhat[:, :],
                    bass.AP(tabs, 256 * j + 2 * DF, [[4 * 4 * DF, 128], [0, 8], [1, DF]]),
                    ALU.mult), [t2])
                tb = s.add("vector", lambda v, j=j: v.tensor_tensor(
                    krot[:, :].rearrange("p (h t x) -> p h t x", h=8, t=2),
                    bass.AP(qhat, 32, [[512, 128], [64, 8], [-32, 2], [1, 32]]),
                    bass.AP(tabs, 256 * j + 3 * DF, [[4 * 4 * DF, 128], [0, 8], [32, 2], [1, 32]]),
                    ALU.mult), [ta, self.t_ttk])
                tk = s.add("vector", lambda v: v.tensor_add(
                    krot[:, :], krot[:, :], ubuf[:, :]), [tb])
                t_tabuse = tk
                return tq, tk

            def piece_T(pw, j, tq_tok, tk_tok):
                nonlocal t_qkevac, t_tke_last
                slot = pw % 3
                tt = None
                for cc in range(4):
                    deps = [tq_tok, t_w]
                    if cc == 0:
                        deps = deps + [t_qkevac.get(slot), self.t_qe_prev]
                    tt = s.add("tensor", lambda t, cc=cc: t.transpose(
                        P[4][:, 128 * cc:128 * (cc + 1)],
                        qrot[:, 128 * cc:128 * (cc + 1)], sb["identF"][:, :]), deps)
                self.t_ttq = tt
                tqe = s.add("scalar", lambda sc, slot=slot, j=j: sc.copy(
                    qf[slot][:, :, 128 * j:128 * (j + 1)],
                    P[4][:, :].rearrange("p (c x) -> p c x", x=128)),
                    [tt, getattr(self, "_dbg_tok", None)])
                self.t_qe_prev = tqe
                tt = None
                for cc in range(4):
                    deps = [tk_tok]
                    if cc == 0:
                        deps = deps + [tqe, self.t_ke_prev]
                    tt = s.add("tensor", lambda t, cc=cc: t.transpose(
                        P[4][:, 128 * cc:128 * (cc + 1)],
                        krot[:, 128 * cc:128 * (cc + 1)], sb["identF"][:, :]), deps)
                self.t_ttk = tt
                tke = s.add("scalar", lambda sc, slot=slot, j=j: sc.copy(
                    kf_e[slot][0:64, :, 128 * j:128 * (j + 1)],
                    P[4][0:64, :].rearrange("p (c x) -> p c x", x=128)),
                    [tt, getattr(self, "_dbg_tok", None)])
                tke = s.add("scalar", lambda sc, slot=slot, j=j: sc.copy(
                    kf_o[slot][64:128, :, 128 * j:128 * (j + 1)],
                    P[4][64:128, :].rearrange("p (c x) -> p c x", x=128)), [tke])
                self.t_ke_prev = tke
                t_qkevac[slot] = tke
                t_qke[slot][j] = tke
                t_tke_last = tke

            for w in range(n_iter):
                slot = w % 3
                if w < n_proj:
                    piece_tab(w)
                    piece_v(w)
                    pend = None
                    for j in range(4):
                        if pend is not None:
                            piece_T(w, pend[0], pend[1], pend[2])
                        piece_q(w, j)
                        tqt, tkt = piece_k(w, j)
                        pend = (j, tqt, tkt)
                    piece_T(w, pend[0], pend[1], pend[2])

                if l == 0 and w == 2 and DBG_TAP:
                    dsemd = s.new_dma_sem(f"dbg{w}", group=True)
                    s.dma("gpsimd", self.dbg_qf[:, :, :], qf[w % 3][:, :, :], dsemd,
                          deps=[t_tke_last])
                    s.dma("gpsimd", self.dbg_kf[:, :, :], kf_e[w % 3][:, :, :], dsemd)
                    dtv = s.dma("gpsimd", self.dbg_vt[:, :, :], vt[w % 3][:, :, :], dsemd)
                    s.dma("gpsimd", self.dbg_kf2[:, :, :], kf_o[w % 3][:, :, :], dsemd)
                    dtv = s.dma("gpsimd", self.dbg_vt2[:, :, :], vt[(w - 1) % 3][:, :, :], dsemd)
                    self._dbg_tok = dtv
                B_wo = w - 1 - Bshift
                if 0 <= B_wo < n_O:
                    B = B_wo
                    bcol = qcol0 + 512 * B
                    mm = None
                    for c in range(NCH):
                        wbank = P[6] if c % 2 == 0 else P[7]
                        for fin in range(4):
                            deps = []
                            if c == 0 and fin == 0:
                                deps = [t_tt1, t_w, t_oacc, t_vevac.get(w % 3)]
                            if fin == 0:
                                deps = deps + [t_wres]
                            mm = s.add("tensor", lambda t, c=c, fin=fin, wbank=wbank, B=B, st=(fin == 0): t.matmul(
                                wbank[:, :], wo[:, fin, 128 * c:128 * (c + 1)],
                                o_sb[0][:, fin, :],
                                start=st, stop=(fin == 3)), deps)
                        t_wres = s.add("vector", lambda v, c=c, wbank=wbank, bcol=bcol: v.tensor_add(
                            sb["X"][:, c, bcol:bcol + 512],
                            sb["X"][:, c, bcol:bcol + 512], wbank[:, :]), [mm])
                    if bcol <= HALO - 1 < bcol + 512:
                        t_wres = s.add("vector", lambda v: v.tensor_mul(
                            sb["X"][:, :, HALO - 1:HALO],
                            sb["X"][:, :, HALO - 1:HALO],
                            bass.AP(sb["halo01"], 0, [[1, 128], [0, NCH], [0, 1]])),
                            [t_wres])
                        self.t_halofix = t_wres
                    t_wo2 = t_wres
                    t_wo = t_wres
                    if l == 0 and B == 2 and DBG_TAP:
                        dsemo = s.new_dma_sem("dbgo", group=True)
                        t_wo2 = s.dma("gpsimd", self.dbg_o[:, :, :], o_sb[0][:, :, :], dsemo,
                                      deps=[t_tt1])

                pb = w - CL
                B_left = w - Bshift
                B_right = B_left + 1
                do_left = 0 <= B_left < n_O
                do_right = 0 <= B_right < n_O
                if 0 <= pb < n_proj and (do_left or do_right):
                    kslot = pb % 3
                    jlast_r = -1
                    for j in range(4):
                        kt0_ = pcol0 + 512 * pb + 128 * j
                        qn_ = min(kt0_ + 640, TEXT) - kt0_
                        if qn_ > 512 - 128 * j:
                            jlast_r = j

                    for h in range(HEADS_F):
                        kfp = kf_e if h % 2 == 0 else kf_o
                        pends = []
                        rstarted = [False]

                        def emit_pv(j, e1):
                            split = 512 - 128 * j
                            qn_j = min(pcol0 + 512 * pb + 128 * j + 640, TEXT) - (
                                pcol0 + 512 * pb + 128 * j)
                            if do_left:
                                n_l = min(split, qn_j)
                                st = (j == 0)
                                deps = [e1] + ([t_tt1, t_vevac.get(w % 3), t_wo]
                                               if st else [])
                                s.add("tensor", lambda t, j=j, h=h, kslot=kslot, n_l=n_l, st=st: t.matmul(
                                    P[6][0:65, 128 * j:128 * j + n_l],
                                    vt[kslot][:, j, 65 * h:65 * h + 65],
                                    pt[j % 3][:, 0:n_l],
                                    start=st, stop=(j == 3)), deps)
                            if do_right and qn_j > split:
                                n_r = qn_j - split
                                s.add("tensor", lambda t, j=j, h=h, kslot=kslot, n_r=n_r, split=split, jl=jlast_r: t.matmul(
                                    P[7][0:65, 0:n_r],
                                    vt[kslot][:, j, 65 * h:65 * h + 65],
                                    pt[j % 3][:, split:split + n_r],
                                    start=False, stop=(j == jl)), deps=[e1])
                            return s.ops["tensor"][-1]

                        for j in range(4):
                            kt0 = pcol0 + 512 * pb + 128 * j
                            kti = kt0 // 128
                            qe = min(kt0 + 640, TEXT)
                            qn = qe - kt0
                            par = j % 2
                            bankF = P[5] if par == 0 else P[0]
                            bankT = P[3] if par == 0 else P[1]
                            segs = [(pb % 3, 128 * j, min(512 - 128 * j, qn), 0)]
                            rem = qn - segs[0][2]
                            if rem > 0:
                                segs.append((((pb + 1) % 3), 0, rem, segs[0][2]))
                            first = True
                            bstart = {}
                            for (qs_, qoff, qlen, pso) in segs:
                                a = 0
                                while a < qlen:
                                    p = pso + a
                                    if p < 512:
                                        take = min(qlen - a, 512 - p)
                                        bank, boff = bankF, p
                                    else:
                                        take = qlen - a
                                        bank, boff = bankT, p - 512
                                    deps = []
                                    if first:
                                        deps = [t_qkevac.get(pb % 3),
                                                t_qke[(pb + 1) % 3][j],
                                                t_expP[par]]
                                    st = not bstart.get(id(bank), False)
                                    bstart[id(bank)] = True
                                    s.add("tensor", lambda t, qs_=qs_, qoff=qoff, a=a, take=take, bank=bank, boff=boff, h=h, kslot=kslot, j=j, st=st, kfp=kfp: t.matmul(
                                        bank[:, boff:boff + take],
                                        kfp[kslot][:, h // 2, 128 * j:128 * (j + 1)],
                                        qf[qs_][:, h // 2, qoff + a:qoff + a + take],
                                        start=st, stop=False), deps)
                                    first = False
                                    a += take
                            t_stmm = s.add("tensor", lambda t, bankF=bankF: t.matmul(
                                bankF[:, 0:128], sb["negI"][:, :], sb["mk_hi"][:, :],
                                start=False, stop=True), [])
                            if qn > 512:
                                t_stmm = s.add("tensor", lambda t, qn=qn, bankT=bankT: t.matmul(
                                    bankT[:, 0:qn - 512], sb["negI"][:, :],
                                    sb["mk_lo"][:, 0:qn - 512], start=False, stop=True), [])
                            e1 = s.add("scalar", lambda sc, j=j, kti=kti, qn=qn, bankF=bankF: sc.activation(
                                pt[j % 3][:, 0:min(qn, 512)], bankF[:, 0:min(qn, 512)], AF.Exp,
                                bias=maskrowH[:, kti:kti + 1]), [t_stmm, t_pv, t_mrow])
                            if qn > 512:
                                e1 = s.add("scalar", lambda sc, j=j, kti=kti, qn=qn, bankT=bankT: sc.activation(
                                    pt[j % 3][:, 512:qn], bankT[:, 0:qn - 512], AF.Exp,
                                    bias=maskrowH[:, kti:kti + 1]), [e1])
                            t_expL = e1
                            t_expP[par] = e1
                            if j == 0 and do_right:
                                s.add("tensor", lambda t: t.matmul(
                                    P[7][0:65, :], sb["ones_row"][0:1, 0:65],
                                    sb["zrow"][0:1, 0:512], start=True, stop=False),
                                    [t_oacc, t_wo, self.t_consts])
                            if len(pends) == 2:
                                t_pv = emit_pv(*pends.pop(0))
                            pends.append((j, e1))
                        for pp in pends:
                            t_pv = emit_pv(*pp)
                        pends = []
                        if do_left:
                            tt1 = s.add("vector", lambda v, h=h: v.tensor_add(
                                tsum[:, :], P[6][0:65, :], O_acc[:, h, :]), [t_pv, t_oacc])
                            tcp = s.add("vector", lambda v: v.tensor_copy(
                                rrowf[0:1, :], tsum[64:65, :]), [tt1])
                            fm = None
                            for c4 in range(4):
                                fm = s.add("tensor", lambda t, c4=c4, st=(c4 == 0): t.matmul(
                                    P[4][:, 0:4],
                                    rrowf[0:1, 128 * c4:128 * (c4 + 1)],
                                    sb["e4"][0:1, 4 * c4:4 * c4 + 4],
                                    start=st, stop=(c4 == 3)),
                                    [tcp, self.t_cpr, t_tke_last] if c4 == 0 else [])
                            flr = s.add("vector", lambda v: v.tensor_scalar(
                                P[4][:, 0:4], P[4][:, 0:4], 1e-30, None, ALU.max), [fm])
                            rcp = s.add("vector", lambda v: v.reciprocal(
                                r4sb[:, :], P[4][:, 0:4]), [flr])
                            bt = None
                            for c4 in range(4):
                                bt = s.add("tensor", lambda t, c4=c4: t.transpose(
                                    P[4][0:1, 128 * c4:128 * (c4 + 1)],
                                    r4sb[:, c4:c4 + 1], sb["identF"][:, :]),
                                    [rcp] if c4 == 0 else [])
                            cpr = s.add("vector", lambda v: v.tensor_copy(
                                rrowb[0:1, :], P[4][0:1, 0:512]), [bt])
                            self.t_cpr = cpr
                            trp = s.add("tensor", lambda t: t.matmul(
                                P[2][0:64, 0:512], sb["ones_row"][0:1, 0:64], rrowb[0:1, :],
                                start=True, stop=True), [cpr, t_tt1, t_uq, t_uk])
                            t_rep = trp
                            to2 = s.add("vector", lambda v, h=h, B=B_left: v.tensor_tensor(
                                o_sb[0][64 * (h % 2):64 * (h % 2) + 64, h // 2, :],
                                tsum[0:64, :], P[2][0:64, :], ALU.mult), [trp, t_wo2])
                            t_tt1 = to2
                        if do_right:
                            t_oacc = s.add("vector", lambda v, h=h: v.tensor_copy(
                                O_acc[:, h, :], P[7][0:65, :]),
                                [t_pv] + ([t_tt1] if do_left else []))
                        else:
                            t_oacc = s.add("vector", lambda v, h=h: v.memset(
                                O_acc[:, h, :], 0.0), [t_tt1])
                    if False:
                        B = B_left
                        bcol = qcol0 + 512 * B
                        mm = None
                        for c in range(NCH):
                            wbank = P[6] if c % 2 == 0 else P[7]
                            for fin in range(4):
                                deps = []
                                if c == 0 and fin == 0:
                                    deps = [t_tt1, t_w, t_oacc]
                                if fin == 0:
                                    deps = deps + [t_wres]
                                mm = s.add("tensor", lambda t, c=c, fin=fin, wbank=wbank, B=B, st=(fin == 0): t.matmul(
                                    wbank[:, :], wo[:, fin, 128 * c:128 * (c + 1)],
                                    o_sb[0][:, fin, :],
                                    start=st, stop=(fin == 3)), deps)
                            t_wres = s.add("vector", lambda v, c=c, wbank=wbank, bcol=bcol: v.tensor_add(
                                sb["X"][:, c, bcol:bcol + 512],
                                sb["X"][:, c, bcol:bcol + 512], wbank[:, :]), [mm])
                        if bcol <= HALO - 1 < bcol + 512:
                            t_wres = s.add("vector", lambda v: v.tensor_mul(
                                sb["X"][:, :, HALO - 1:HALO],
                                sb["X"][:, :, HALO - 1:HALO],
                                bass.AP(sb["halo01"], 0, [[1, 128], [0, NCH], [0, 1]])),
                                [t_wres])
                            self.t_halofix = t_wres
                        t_wo2 = t_wres
                        t_wo = t_wres
                        if l == 0 and B == 2 and DBG_TAP:
                            dsemo = s.new_dma_sem("dbgo", group=True)
                            t_wo2 = s.dma("gpsimd", self.dbg_o[:, :, :], o_sb[0][:, :, :], dsemo,
                                          deps=[t_tt1])

        if do_mlp:
            self._mlp(l)

    def _mlp(self, l):
        s, nc, sb, inp = self.s, self.nc, self.sb, self.inp
        gate = self.gate()
        if l == 0:
            mc0, ntok = 512, 4736
        else:
            mc0, ntok = 1152, 4096
        megas = []
        a = 0
        while a < ntok:
            megas.append((mc0 + a, min(MEGA, ntok - a)))
            a += min(MEGA, ntok - a)

        with ExitStack() as es:
            h_buf = self.sbuf(f"hbuf{l}", (128, 22, MEGA), BF16, es)
            xt = self.sbuf(f"xt{l}", (128, NCH, MEGA), BF16, es)
            w1a = [self.sbuf(f"w1a{l}_{i}", (128, NCH, 128), BF16, es) for i in range(2)]
            w1g = [self.sbuf(f"w1g{l}_{i}", (128, NCH, 128), BF16, es) for i in range(2)]
            w2b = [self.sbuf(f"w2b{l}_{i}", (128, HID), BF16, es) for i in range(3)]
            gout = self.sbuf(f"gout{l}", (128, MEGA), BF16, es)
            c12 = self.sbuf(f"c12_{l}", (2, H2), BF16, es)
            c2row = self.sbuf(f"c2row{l}", (1, H2), BF16, es)
            misc2 = self.sbuf(f"misc2_{l}", (2, MEGA), BF16, es)
            xb = [self.sbuf(f"xb{l}_{i}", (128, NCH, 1), BF16, es) for i in range(2)]
            mrow_ = self.sbuf(f"mrow{l}", (1, MEGA), F32, es)
            srow = self.sbuf(f"srow{l}", (1, MEGA), F32, es)
            srowb = self.sbuf(f"srowb{l}", (1, MEGA), BF16, es)
            P = [self.psum(f"M{l}b{i}", (128, 512), F32, es) for i in range(8)]

            dw1a = [s.new_dma_sem(f"m{l}w1a{i}") for i in range(2)]
            dw1g = [s.new_dma_sem(f"m{l}w1g{i}") for i in range(2)]
            dw2 = [s.new_dma_sem(f"m{l}w2{i}") for i in range(3)]
            dmisc = s.new_dma_sem(f"mm{l}", group=True)
            t_b1 = s.dma("gpsimd", c2row[0:1, :], inp[f"b1row{l}"][:, :], dmisc, deps=gate)
            t_m2 = s.dma("gpsimd", misc2[:, :], inp["ones2row"][:, :], dmisc)
            w1hb = self.w1bf[l].rearrange("(c p) o -> p c o", p=128)

            t_xb = s.add("vector", lambda v: v.tensor_copy(
                xb[0][:, :, :], sb["X"][:, :, mc0 - 1:mc0]),
                gate + [getattr(self, "t_halofix", None)])
            t_ev = None
            t_h = None
            t_hP = [None, None]
            t_gel = None
            t_gelP = [None, None]
            t_stg2 = None
            t_c12done = None
            t_w1free = [None, None]
            t_w2free = [None, None, None]
            t_evc = [None] * NCH
            t_repread = None
            t_sqw = None
            t_smm = None
            t_c2dma = None

            for mi, (m0, Nm) in enumerate(megas):
                nmb = (Nm + 511) // 512
                if mi + 1 < len(megas):
                    t_xb2 = s.add("vector", lambda v, m0=m0, Nm=Nm, mi=mi: v.tensor_copy(
                        xb[(mi + 1) % 2][:, :, :], sb["X"][:, :, m0 + Nm - 1:m0 + Nm]),
                        [t_ev])
                t_sq1 = s.add("scalar", lambda sc, m0=m0, Nm=Nm: sc.activation(
                    xt[:, 0:4, 0:Nm], sb["X"][:, 0:4, m0:m0 + Nm], AF.Square),
                    [t_smm, t_ev, t_repread, getattr(self, "t_halofix", None)])
                t_sq2 = s.add("scalar", lambda sc, m0=m0, Nm=Nm: sc.activation(
                    xt[:, 4:8, 1:Nm], sb["X"][:, 4:8, m0:m0 + Nm - 1], AF.Square), [t_sq1])
                t_sqw = s.add("scalar", lambda sc, mi=mi: sc.activation(
                    xt[:, 4:8, 0:1], xb[mi % 2][:, 4:8, :], AF.Square), [t_sq2, t_xb])
                mm = None
                for c in range(NCH):
                    last_c = (c == NCH - 1)
                    for hb in range(nmb):
                        lo = 512 * hb
                        hi = min(Nm, lo + 512)
                        if c < 4:
                            mm = s.add("tensor", lambda t, c=c, hb=hb, lo=lo, hi=hi, m0=m0, st=(c == 0): t.matmul(
                                P[4 + hb][0:1, 0:hi - lo], sb["ones_col"][:, :],
                                sb["X"][:, c, m0 + lo:m0 + hi],
                                start=st, stop=False),
                                ([t_stg2, t_ev, getattr(self, "t_halofix", None)]
                                 + (gate if mi == 0 else [])) if (c == 0 and hb == 0) else [])
                        else:
                            if hb == 0:
                                mm = s.add("tensor", lambda t, c=c, mi=mi: t.matmul(
                                    P[4][0:1, 0:1], sb["ones_col"][:, :],
                                    xb[mi % 2][:, c, :],
                                    start=False, stop=False), [t_xb])
                                mm = s.add("tensor", lambda t, c=c, hi=hi, m0=m0, sp=(c == NCH - 1): t.matmul(
                                    P[4][0:1, 1:hi], sb["ones_col"][:, :],
                                    sb["X"][:, c, m0:m0 + hi - 1],
                                    start=False, stop=sp), [])
                            else:
                                mm = s.add("tensor", lambda t, c=c, lo=lo, hi=hi, m0=m0, sp=(c == NCH - 1): t.matmul(
                                    P[4 + hb][0:1, 0:hi - lo], sb["ones_col"][:, :],
                                    sb["X"][:, c, m0 + lo - 1:m0 + hi - 1],
                                    start=False, stop=sp), [])
                for c in range(NCH):
                    for hb in range(nmb):
                        lo = 512 * hb
                        hi = min(Nm, lo + 512)
                        mm = s.add("tensor", lambda t, c=c, hb=hb, lo=lo, hi=hi, st=(c == 0), sp=(c == NCH - 1): t.matmul(
                            P[6 + hb][0:1, 0:hi - lo], sb["ones_col"][:, :],
                            xt[:, c, lo:hi], start=st, stop=sp),
                            [t_sqw] if c == 0 and hb == 0 else [])
                t_smm = mm
                r1 = s.add("vector", lambda v, Nm=Nm: v.tensor_scalar(
                    mrow_[0:1, 0:min(Nm, 512)], P[4][0:1, 0:min(Nm, 512)],
                    1.0 / HID, None, ALU.mult), [t_smm])
                if Nm > 512:
                    r1 = s.add("vector", lambda v, Nm=Nm: v.tensor_scalar(
                        mrow_[0:1, 512:Nm], P[5][0:1, 0:Nm - 512],
                        1.0 / HID, None, ALU.mult), [r1])
                r2 = s.add("vector", lambda v, Nm=Nm: v.tensor_scalar(
                    srow[0:1, 0:min(Nm, 512)], P[6][0:1, 0:min(Nm, 512)],
                    1.0 / HID, None, ALU.mult), [r1])
                if Nm > 512:
                    r2 = s.add("vector", lambda v, Nm=Nm: v.tensor_scalar(
                        srow[0:1, 512:Nm], P[7][0:1, 0:Nm - 512],
                        1.0 / HID, None, ALU.mult), [r2])
                r3 = r2
                for hb in range(nmb):
                    lo = 512 * hb
                    hi = min(Nm, lo + 512)
                    r3 = s.add("vector", lambda v, hb=hb, lo=lo, hi=hi: v.tensor_mul(
                        P[4 + hb][0:1, 0:hi - lo], mrow_[0:1, lo:hi], mrow_[0:1, lo:hi]), [r3])
                    r3 = s.add("vector", lambda v, hb=hb, lo=lo, hi=hi: v.tensor_sub(
                        srow[0:1, lo:hi], srow[0:1, lo:hi], P[4 + hb][0:1, 0:hi - lo]), [r3])
                r5 = s.add("scalar", lambda sc, Nm=Nm: sc.activation(
                    srow[0:1, 0:Nm], srow[0:1, 0:Nm], AF.Sqrt,
                    bias=sb["eps"][0:1, 0:1]), [r3])
                r6 = s.add("vector", lambda v, Nm=Nm: v.reciprocal(
                    srow[0:1, 0:Nm], srow[0:1, 0:Nm]), [r5])
                r7 = r6
                for hb in range(nmb):
                    lo = 512 * hb
                    hi = min(Nm, lo + 512)
                    r7 = s.add("vector", lambda v, hb=hb, lo=lo, hi=hi: v.tensor_mul(
                        P[4 + hb][0:1, 0:hi - lo], mrow_[0:1, lo:hi], srow[0:1, lo:hi]), [r7])
                c1_ = s.add("vector", lambda v, Nm=Nm: v.tensor_copy(
                    srowb[0:1, 0:Nm], srow[0:1, 0:Nm]), [r7])
                c2_ = c1_
                for hb in range(nmb):
                    lo = 512 * hb
                    hi = min(Nm, lo + 512)
                    c2_ = s.add("vector", lambda v, hb=hb, lo=lo, hi=hi: v.tensor_scalar(
                        misc2[0:1, lo:hi], P[4 + hb][0:1, 0:hi - lo], -1.0, None,
                        ALU.mult), [c2_, t_m2])
                rp1 = s.add("tensor", lambda t, Nm=Nm: t.matmul(
                    P[0][:, 0:min(Nm, 512)], sb["ones_row"][0:1, 0:128],
                    srowb[0:1, 0:min(Nm, 512)], start=True, stop=True),
                    [c1_, t_h, t_smm])
                rp2 = None
                if Nm > 512:
                    rp2 = s.add("tensor", lambda t, Nm=Nm: t.matmul(
                        P[1][:, 0:Nm - 512], sb["ones_row"][0:1, 0:128],
                        srowb[0:1, 512:Nm], start=True, stop=True), [c1_, t_h])
                x1_ = s.add("vector", lambda v, m0=m0, Nm=Nm: v.tensor_tensor(
                    xt[:, 0:4, 0:min(Nm, 512)], sb["X"][:, 0:4, m0:m0 + min(Nm, 512)],
                    bass.AP(P[0], 0, [[512, 128], [0, 4], [1, min(Nm, 512)]]),
                    ALU.mult), [rp1, t_sqw, t_smm])
                if Nm > 512:
                    x1_ = s.add("vector", lambda v, m0=m0, Nm=Nm: v.tensor_tensor(
                        xt[:, 0:4, 512:Nm], sb["X"][:, 0:4, m0 + 512:m0 + Nm],
                        bass.AP(P[1], 0, [[512, 128], [0, 4], [1, Nm - 512]]),
                        ALU.mult), [rp2, x1_])
                x2_ = s.add("vector", lambda v, m0=m0, Nm=Nm: v.tensor_tensor(
                    xt[:, 4:8, 1:min(Nm, 512)], sb["X"][:, 4:8, m0:m0 + min(Nm, 512) - 1],
                    bass.AP(P[0], 1, [[512, 128], [0, 4], [1, min(Nm, 512) - 1]]),
                    ALU.mult), [x1_])
                if Nm > 512:
                    x2_ = s.add("vector", lambda v, m0=m0, Nm=Nm: v.tensor_tensor(
                        xt[:, 4:8, 512:Nm], sb["X"][:, 4:8, m0 + 511:m0 + Nm - 1],
                        bass.AP(P[1], 0, [[512, 128], [0, 4], [1, Nm - 512]]),
                        ALU.mult), [x2_])
                x3_ = s.add("vector", lambda v, mi=mi: v.tensor_tensor(
                    xt[:, 4:8, 0:1], xb[mi % 2][:, 4:8, :],
                    bass.AP(P[0], 0, [[512, 128], [0, 4], [1, 1]]),
                    ALU.mult), [x2_, t_xb])
                t_xt = s.add("vector", lambda v, Nm=Nm: v.tensor_tensor(
                    xt[:, :, 0:Nm], xt[:, :, 0:Nm],
                    bass.AP(sb[f"lg_col{l}"], 0, [[NCH, 128], [1, NCH], [0, Nm]]),
                    ALU.mult), [x3_])
                if mi + 1 < len(megas):
                    t_xb = t_xb2

                for j in range(22):
                    sl = j % 2
                    la = s.dma("sync", w1a[sl][:, :, :], w1hb[:, :, 128 * j:128 * (j + 1)],
                               dw1a[sl], deps=[t_w1free[sl], self.t_wcast]
                               + (gate if (mi == 0 and j < 2) else []))
                    lg = s.dma("scalar", w1g[sl][:, :, :],
                               w1hb[:, :, HPAD + 128 * j:HPAD + 128 * (j + 1)],
                               dw1g[sl], deps=[t_w1free[sl], self.t_wcast])
                    if mi == 0:
                        for (wt, coff) in ((w1a[sl], 128 * j), (w1g[sl], HPAD + 128 * j)):
                            for fin in range(NCH):
                                mm = s.add("tensor", lambda t, wt=wt, fin=fin, st=(fin == 0), sp=(fin == NCH - 1): t.matmul(
                                    P[6][0:1, 0:128], sb[f"lg_col{l}"][:, fin:fin + 1],
                                    wt[:, fin, :], start=st, stop=sp),
                                    [la, lg, t_gel, t_xt, t_c12done] if fin == 0 else [])
                            for fin in range(NCH):
                                mm = s.add("tensor", lambda t, wt=wt, fin=fin, st=(fin == 0), sp=(fin == NCH - 1): t.matmul(
                                    P[7][0:1, 0:128], sb[f"lb_col{l}"][:, fin:fin + 1],
                                    wt[:, fin, :], start=st, stop=sp), [])
                            ec1 = s.add("vector", lambda v, coff=coff: v.tensor_copy(
                                c12[0:1, coff:coff + 128], P[6][0:1, 0:128]), [mm])
                            ec2 = s.add("vector", lambda v, coff=coff: v.tensor_add(
                                c2row[0:1, coff:coff + 128], c2row[0:1, coff:coff + 128],
                                P[7][0:1, 128:256] if False else P[7][0:1, 0:128]), [ec1, t_b1])
                            t_c12done = ec2
                    jb = 4 * (j % 2)
                    for (banks, wt, dep0) in (((P[jb + 0], P[jb + 1]), w1a[sl],
                                               [la, t_xt, t_hP[j % 2], t_stg2]),
                                              ((P[jb + 2], P[jb + 3]), w1g[sl],
                                               [lg, t_gelP[j % 2]])):
                        for half in range(2 if Nm > 512 else 1):
                            nn = min(Nm - 512 * half, 512)
                            for fin in range(NCH):
                                deps = list(dep0) + ([t_c12done] if mi == 0 else []) if (half == 0 and fin == 0) else []
                                mm = s.add("tensor", lambda t, banks=banks, wt=wt, half=half, fin=fin, nn=nn, st=(fin == 0): t.matmul(
                                    banks[half][:, 0:nn], wt[:, fin, :],
                                    xt[:, fin, 512 * half:512 * half + nn],
                                    start=st, stop=False), deps)
                            if mi == 0:
                                coff = (128 * j) if wt is w1a[sl] else (HPAD + 128 * j)
                                mm = s.add("tensor", lambda t, banks=banks, half=half, nn=nn, coff=coff: t.matmul(
                                    banks[half][:, 0:nn], c12[0:1, coff:coff + 128],
                                    misc2[0:1, 512 * half:512 * half + nn],
                                    start=False, stop=False), [t_c12done, c2_])
                                mm = s.add("tensor", lambda t, banks=banks, half=half, nn=nn, coff=coff: t.matmul(
                                    banks[half][:, 0:nn], c2row[0:1, coff:coff + 128],
                                    sb["ones_row"][0:1, 0:nn],
                                    start=False, stop=True), [t_c12done])
                            else:
                                coff = (128 * j) if wt is w1a[sl] else (HPAD + 128 * j)
                                mm = s.add("tensor", lambda t, banks=banks, half=half, nn=nn, coff=coff: t.matmul(
                                    banks[half][:, 0:nn], c12[0:2, coff:coff + 128],
                                    misc2[0:2, 512 * half:512 * half + nn],
                                    start=False, stop=True), [t_c2dma, c2_])
                    t_ag = mm
                    t_w1free[sl] = t_ag
                    tg = None
                    for half in range(2 if Nm > 512 else 1):
                        nn = min(Nm - 512 * half, 512)
                        tg = s.add("scalar", lambda sc, jb=jb, half=half, nn=nn: sc.activation(
                            gout[:, 512 * half:512 * half + nn],
                            P[jb + 2 + half][:, 0:nn], AF.Gelu), [t_ag, t_h] if half == 0 else [tg])
                    t_gel = tg
                    t_gelP[j % 2] = tg
                    th = None
                    for half in range(2 if Nm > 512 else 1):
                        nn = min(Nm - 512 * half, 512)
                        th = s.add("vector", lambda v, j=j, jb=jb, half=half, nn=nn: v.scalar_tensor_tensor(
                            h_buf[:, j, 512 * half:512 * half + nn],
                            P[jb + half][:, 0:nn], 0.0, gout[:, 512 * half:512 * half + nn],
                            ALU.bypass, ALU.mult), [t_gel] if half == 0 else [th])
                    t_h = th
                    t_hP[j % 2] = th
                if mi == 0:
                    t_c2dma = s.dma("sync", c12[1:2, :], c2row[0:1, :],
                                    s.new_dma_sem(f"c2d{l}", group=True), deps=[t_c12done])

                for half in range(2 if Nm > 512 else 1):
                    nn = min(Nm - 512 * half, 512)
                    for j in range(22):
                        sl = j % 3
                        l2 = s.dma(("sync", "scalar", "gpsimd")[sl],
                                   w2b[sl][:, :],
                                   self.w2bf[l][128 * j:128 * (j + 1), :], dw2[sl],
                                   deps=[t_w2free[sl], self.t_wcast]
                                   + (gate if (mi == 0 and half == 0 and j < 3) else []))
                        for c in range(NCH):
                            deps = []
                            if c == 0 and j == 0:
                                deps = [l2, t_h]
                            elif c == 0:
                                deps = [l2]
                            if j == 0:
                                deps = deps + [t_evc[c]]
                            mm = s.add("tensor", lambda t, c=c, j=j, sl=sl, half=half, nn=nn, st=(j == 0), sp=(j == 21): t.matmul(
                                P[c][:, 0:nn], w2b[sl][:, 128 * c:128 * (c + 1)],
                                h_buf[:, j, 512 * half:512 * half + nn],
                                start=st, stop=sp), deps)
                        t_w2free[sl] = mm
                    t_stg2 = mm
                    ev = None
                    for c in range(NCH):
                        ev = s.add("vector", lambda v, c=c, half=half, nn=nn, m0=m0: v.scalar_tensor_tensor(
                            sb["X"][:, c, m0 + 512 * half:m0 + 512 * half + nn],
                            P[c][:, 0:nn], sb[f"b2_col{l}"][:, c:c + 1],
                            sb["X"][:, c, m0 + 512 * half:m0 + 512 * half + nn],
                            ALU.add, ALU.add), [t_stg2] if c == 0 else [ev])
                        t_evc[c] = ev
                    if m0 + 512 * half <= HALO - 1 < m0 + 512 * half + nn:
                        ev = s.add("vector", lambda v: v.tensor_mul(
                            sb["X"][:, :, HALO - 1:HALO], sb["X"][:, :, HALO - 1:HALO],
                            bass.AP(sb["halo01"], 0, [[1, 128], [0, NCH], [0, 1]])), [ev])
                        self.t_halofix = ev
                    t_ev = ev
                t_repread = t_ev

    def build_final(self):
        s, nc, sb, inp = self.s, self.nc, self.sb, self.inp
        gate = self.gate()
        dout = [s.new_dma_sem(f"out{i}") for i in range(2)]
        dx1r = [s.new_dma_sem(f"x1r{i}") for i in range(2)]
        with ExitStack() as es:
            x1t = [self.sbuf(f"x1t{i}", (128, NCH, 128), BF16, es) for i in range(2)]
            tsb = self.sbuf("fin_t", (128, NCH, 128), F32, es)
            tsbb = self.sbuf("fin_tb", (128, NCH, 128), BF16, es)
            tq = self.sbuf("fin_q", (128, NCH, 128), BF16, es)
            of = [self.sbuf(f"fin_o{i}", (128, NCH, 128), F32, es) for i in range(2)]
            frow = [self.sbuf(f"frow{i}", (1, 128), F32, es) for i in range(4)]
            onesf = self.sbuf("onesf_s", (1, 128), F32, es)
            P = [self.psum(f"F{i}", (128, 512), F32, es) for i in range(4)]

            t_of = s.add("gpsimd", lambda gg: gg.memset(onesf[:, :], 1.0), gate)
            t_add = None
            t_sq = None
            t_norm = [None, None]
            t_store = [None, None]
            t_rows = None
            t_rep = None
            for ti in range(32):
                col = HALO + 128 * ti
                sl = ti % 2
                ld = s.dma("sync", x1t[sl][:, :, :], self.x1b[:, :, 128 * ti:128 * (ti + 1)],
                           dx1r[sl], deps=[t_norm[sl]] + (gate if ti < 2 else []))
                ta = s.add("vector", lambda v, col=col, sl=sl: v.tensor_add(
                    tsb[:, :, :], sb["X"][:, :, col:col + 128], x1t[sl][:, :, :]),
                    [ld, t_sq] + (gate if ti == 0 else []))
                tcb = s.add("vector", lambda v: v.tensor_copy(
                    tsbb[:, :, :], tsb[:, :, :]), [ta])
                t_sq = s.add("scalar", lambda sc: sc.activation(
                    tq[:, :, :], tsb[:, :, :], AF.Square), [ta, t_rows])
                mm = None
                for c in range(NCH):
                    mm = s.add("tensor", lambda t, c=c, st=(c == 0), sp=(c == NCH - 1): t.matmul(
                        P[0][0:1, 0:128], sb["ones_col"][:, :], tsbb[:, c, :],
                        start=st, stop=sp), [tcb, t_rows] if c == 0 else [])
                for c in range(NCH):
                    mm = s.add("tensor", lambda t, c=c, st=(c == 0), sp=(c == NCH - 1): t.matmul(
                        P[1][0:1, 0:128], sb["ones_col"][:, :], tq[:, c, :],
                        start=st, stop=sp), [t_sq] if c == 0 else [])
                r1 = s.add("vector", lambda v: v.tensor_scalar(
                    frow[0][0:1, :], P[0][0:1, 0:128], 1.0 / HID, None, ALU.mult), [mm])
                r2 = s.add("vector", lambda v: v.tensor_scalar(
                    frow[1][0:1, :], P[1][0:1, 0:128], 1.0 / HID, None, ALU.mult), [r1])
                r3 = s.add("vector", lambda v: v.tensor_mul(
                    frow[2][0:1, :], frow[0][0:1, :], frow[0][0:1, :]), [r2])
                r4 = s.add("vector", lambda v: v.tensor_sub(
                    frow[1][0:1, :], frow[1][0:1, :], frow[2][0:1, :]), [r3])
                r5 = s.add("scalar", lambda sc: sc.activation(
                    frow[1][0:1, :], frow[1][0:1, :], AF.Sqrt, bias=sb["eps"][0:1, 0:1]), [r4])
                r6 = s.add("vector", lambda v: v.reciprocal(
                    frow[1][0:1, :], frow[1][0:1, :]), [r5])
                r7 = s.add("vector", lambda v: v.tensor_mul(
                    frow[2][0:1, :], frow[0][0:1, :], frow[1][0:1, :]), [r6])
                t_rows = r7
                m1 = s.add("tensor", lambda t: t.matmul(
                    P[2][:, 0:128], onesf[0:1, 0:128], frow[1][0:1, :],
                    start=True, stop=True), [r6, t_rep, t_of])
                m2 = s.add("tensor", lambda t: t.matmul(
                    P[3][:, 0:128], onesf[0:1, 0:128], frow[2][0:1, :],
                    start=True, stop=True), [r7])
                t_rep = m2
                n1 = s.add("vector", lambda v, sl=sl: v.tensor_tensor(
                    of[sl][:, :, :], tsb[:, :, :],
                    bass.AP(P[2], 0, [[512, 128], [0, NCH], [1, 128]]), ALU.mult),
                    [m1, t_store[sl]])
                t_norm[sl] = s.add("vector", lambda v, sl=sl: v.tensor_sub(
                    of[sl][:, :, :],
                    of[sl][:, :, :],
                    bass.AP(P[3], 0, [[512, 128], [0, NCH], [1, 128]])), [n1, m2])
                t_rep = m2
                t_store[sl] = s.dma(
                    "sync", self.outT.rearrange("(c p) t -> p c t", p=128)[:, :, 128 * ti:128 * (ti + 1)],
                    of[sl][:, :, :], dout[sl], deps=[t_norm[sl]])
            def fin(sy):
                sy.wait_ge(dout[0].sem, dout[0].count)
                sy.wait_ge(dout[1].sem, dout[1].count)
                return None
            s.add("sync", fin)


_LAST_SCHED = None


def build_program(upto="all", mlp0=True, mlp1=True):
    global _LAST_SCHED
    nc = bass.Bass(trn_type="TRN2", target_bir_lowering=False)
    ctx = ExitStack()
    kb = KB(nc, ctx)
    _LAST_SCHED = kb.s
    with ctx:
        kb.build_io()
        kb.build_consts()
        with ExitStack() as es_kv:
            kb.es_kv = es_kv
            if upto != "consts":
                kb.build_p1()
            if upto not in ("consts", "p1"):
                kb.build_p2()
        if upto in ("L0", "L1", "all"):
            kb.build_layer(0, do_mlp=mlp0)
        if upto in ("L1", "all"):
            kb.build_layer(1, do_mlp=mlp1)
        if upto == "all":
            kb.build_final()
        else:
            gate = kb.gate()
            dsem = kb.s.new_dma_sem("dbgout", group=True)
            t = kb.s.dma("gpsimd", kb.outT.rearrange("(c p) t -> p c t", p=128)[:, :, 0:128],
                         kb.sb["X"][:, :, HALO:HALO + 128], dsem, deps=gate)
            def fin(sy):
                sy.wait_ge(dsem.sem, dsem.count)
                return None
            kb.s.add("sync", fin)
        kb.s.setup_sems(ctx)
        kb.s.emit()
    return nc


def kernel(**inputs):
    nc = build_program()
    in_maps = [make_core_inputs(inputs, c) for c in range(8)]
    res = run_bass_kernel_spmd(nc, in_maps, core_ids=list(range(8)))
    out = np.empty((4, 8192, 1024), np.float32)
    for c in range(8):
        b, h = c // 2, c % 2
        out[b, 4096 * h:4096 * (h + 1), :] = res.results[c]["outT"].T
    return out


# version 17
# speedup vs baseline: 1.0431x; 1.0048x over previous
import numpy as np
import ml_dtypes
import concourse.bass as bass
import concourse.mybir as mybir
from concourse.bass_utils import run_bass_kernel_spmd

F32 = mybir.dt.float32
BF16 = mybir.dt.bfloat16
AF = mybir.ActivationFunctionType
ALU = mybir.AluOpType
AX = mybir.AxisListType
BF = ml_dtypes.bfloat16

ENGINES = ("tensor", "vector", "scalar", "gpsimd", "sync")


class _Op:
    __slots__ = ("engine", "fn", "deps", "idx", "dma_sem", "value", "marked", "label")

    def __init__(self, engine, fn, deps, idx, dma_sem=None):
        self.engine = engine
        self.fn = fn
        self.deps = deps
        self.idx = idx
        self.dma_sem = dma_sem
        self.value = None
        self.marked = dma_sem is not None


class DmaSem:
    __slots__ = ("name", "sem", "count", "group")

    def __init__(self, name, group=False):
        self.name = name
        self.sem = None
        self.count = 0
        self.group = group


class Sched:
    def __init__(self, nc):
        self.nc = nc
        self.ops = {e: [] for e in ENGINES}
        self.all_ops = []
        self.sems = {}
        self.dma_sems = []

    def new_dma_sem(self, name, group=False):
        d = DmaSem(name, group)
        self.dma_sems.append(d)
        return d

    def setup_sems(self, ctx):
        for e in ENGINES:
            self.sems[e] = ctx.enter_context(self.nc.semaphore(f"chain_{e}"))
        for d in self.dma_sems:
            d.sem = ctx.enter_context(self.nc.semaphore(f"dma_{d.name}"))

    def add(self, engine, fn, deps=()):
        deps = tuple(d for d in deps if d is not None)
        op = _Op(engine, fn, deps, len(self.ops[engine]))
        self.ops[engine].append(op)
        self.all_ops.append(op)
        return op

    def dma(self, engine, out, in_, dsem, deps=()):
        deps = tuple(d for d in deps if d is not None)

        def fn(eng):
            return eng.dma_start(out=out, in_=in_)

        op = _Op(engine, fn, deps, len(self.ops[engine]), dma_sem=dsem)
        dsem.count += 16
        op.value = dsem.count
        self.ops[engine].append(op)
        self.all_ops.append(op)
        return op

    def barrier_token(self, engine):
        ops = self.ops[engine]
        return ops[-1] if ops else None

    def all_tokens(self):
        return [self.barrier_token(e) for e in ENGINES if self.ops[e]]

    def emit(self):
        for op in self.all_ops:
            for d in op.deps:
                d.marked = True
        counts = {e: 0 for e in ENGINES}
        for e in ENGINES:
            for op in self.ops[e]:
                if op.marked and op.dma_sem is None:
                    counts[e] += 1
                    op.value = counts[e]
        self.final_counts = dict(counts)
        self.inst_map = {}
        nc, sems, ops = self.nc, self.sems, self.ops
        with nc.Block() as block:
            def mk(e):
                def body(eng):
                    seen = {}
                    for op in ops[e]:
                        for d in op.deps:
                            if d.value is None:
                                raise RuntimeError(f"unmarked dep on {e} #{op.idx}")
                            if d.dma_sem is not None:
                                key, sem = d.dma_sem, d.dma_sem.sem
                                val = d.dma_sem.count if d.dma_sem.group else d.value
                            else:
                                key, sem, val = d.engine, sems[d.engine], d.value
                            if val > seen.get(key, 0):
                                eng.wait_ge(sem, val)
                                seen[key] = val
                        inst = op.fn(eng)
                        try:
                            self.inst_map[inst.ins.name] = (e, op.idx, getattr(op, 'label', None))
                        except Exception:
                            pass
                        if op.dma_sem is not None:
                            inst.then_inc(op.dma_sem.sem, 16)
                        elif op.marked:
                            inst.then_inc(sems[e], 1)
                    return None
                return body
            for e in ENGINES:
                getattr(block, e)(mk(e))


HID = 1024
NCH = 8
HALO = 1152
OW = 4096
TEXT = HALO + OW
NT_EXT = TEXT // 128
NT_REST = 4096 // 128
HEADS_A = 16
DA = 64
HEADS_F = 8
DF = 64
WIN = 512
INNER = 512
HPAD = 2816
H2 = 2 * HPAD
MEGA = 1024

L_CFG = [
    dict(blk0_col=128, nblk=10, mlp_col0=512, mlp_ntok=4736),
    dict(blk0_col=1152, nblk=8, mlp_col0=1152, mlp_ntok=4096),
]

NEG = -1.0e30
DBG_TAP = False
P2STOP = 99


def _perm_head_major():
    return np.array([d * 16 + h for h in range(16) for d in range(64)], dtype=np.int64)


def _rotary_tables():
    D = 64
    half = np.arange(0, D, 2, dtype=np.float64)
    inv_freq = 1.0 / 10000.0 ** (half / D)
    t = np.arange(8192, dtype=np.float64)
    freqs = t[:, None] * inv_freq[None, :]
    freqs = np.concatenate([freqs, freqs], axis=1)
    base = (half + 0.4 * D) / (1.4 * D)
    power = (t - 4096.0) / float(WIN // 2)
    scale = base[None, :] ** power[:, None]
    scale = np.concatenate([scale, scale], axis=1)
    cos = np.cos(freqs)
    sin = np.sin(freqs)
    cs = cos * scale
    csk = cos / scale
    sgn = np.concatenate([-np.ones(32), np.ones(32)])[None, :]
    ssn = sin * scale * sgn
    sskn = sin / scale * sgn
    return (cs.astype(np.float32), ssn.astype(np.float32),
            csk.astype(np.float32), sskn.astype(np.float32))


def _local_masks():
    qq = np.arange(128)[:, None]
    kk = np.arange(128)[None, :]
    mq_lo = (kk < qq).astype(np.float32)
    mq_hi = (kk > qq).astype(np.float32)
    return mq_lo, mq_hi, mq_lo.T.copy(), mq_hi.T.copy()


def make_core_inputs(inputs, core):
    b, h = core // 2, core % 2
    s = 4096 * h
    perm = _perm_head_major()
    x = np.asarray(inputs["L0_attn_x"])
    xT = np.ascontiguousarray(x[b].T)

    x_extT = np.zeros((HID, TEXT), np.float32)
    lo = s - HALO
    src_lo = max(lo, 0)
    x_extT[:, src_lo - lo:] = xT[:, src_lo:s + OW]

    x_restT = np.zeros((HID, 4096), np.float32)
    if h == 0:
        x_restT[:, :] = xT[:, 4096:8192]
        rest_valid = 4096
    else:
        x_restT[:, :2944] = xT[:, 0:2944]
        rest_valid = 2944

    ext_valid = np.zeros(TEXT, np.float32)
    ext_valid[HALO - min(HALO, s):] = 1.0
    if h == 1:
        ext_valid[:] = 1.0
    mask_ext_col = ext_valid.reshape(NT_EXT, 128).T.copy()
    rv = np.zeros(4096, np.float32)
    rv[:rest_valid] = 1.0
    mask_rest_col = rv.reshape(NT_REST, 128).T.copy()

    mrow = np.where(ext_valid > 0, 0.0, NEG).astype(np.float32)
    maskrow_col = mrow.reshape(NT_EXT, 128).T.copy()

    d = {
        "x_extT": x_extT, "x_restT": x_restT,
        "mask_ext_col": mask_ext_col, "mask_rest_col": mask_rest_col,
        "maskrow_col": maskrow_col,
    }

    d["Wh"] = np.ascontiguousarray(np.asarray(inputs["L0_attn_Wh"])[:, perm])
    d["Wq"] = np.ascontiguousarray(np.asarray(inputs["L0_attn_Wq"])[:, perm])
    d["Wk"] = np.ascontiguousarray(np.asarray(inputs["L0_attn_Wk"])[:, perm])
    d["Wv"] = np.ascontiguousarray(np.asarray(inputs["L0_attn_Wv"])[:, perm])
    d["Wo"] = np.ascontiguousarray(np.asarray(inputs["L0_attn_Wo"])[perm, :])
    d["bh_row"] = np.asarray(inputs["L0_attn_bh"])[perm][None, :].copy()
    d["beta_row"] = np.asarray(inputs["L0_attn_beta"])[perm][None, :].copy()
    d["bk_row"] = np.asarray(inputs["L0_attn_bk"])[perm][None, :].copy()
    d["bv_row"] = np.asarray(inputs["L0_attn_bv"])[perm][None, :].copy()
    d["bq_col"] = np.ascontiguousarray(
        np.asarray(inputs["L0_attn_bq"])[perm].reshape(NCH, 128).T)
    d["bo_col"] = np.ascontiguousarray(
        np.asarray(inputs["L0_attn_bo"]).reshape(NCH, 128).T)
    d["lnw_rep"] = np.ascontiguousarray(
        np.broadcast_to(np.asarray(inputs["L0_attn_ln_w"])[perm], (128, HID)))
    d["lnb_rep"] = np.ascontiguousarray(
        np.broadcast_to(np.asarray(inputs["L0_attn_ln_b"])[perm], (128, HID)))

    w1 = np.asarray(inputs["L0_ffn_w1"])
    b1 = np.asarray(inputs["L0_ffn_b1"])
    w2 = np.asarray(inputs["L0_ffn_w2"])
    for l in range(2):
        d[f"wqkv{l}"] = np.ascontiguousarray(np.asarray(inputs["L0_ffn_wqkv"])[l])
        d[f"wo{l}"] = np.ascontiguousarray(np.asarray(inputs["L0_ffn_wo"])[l])
        w1p = np.zeros((HID, H2), np.float32)
        w1p[:, 0:2730] = w1[l][:, 0:2730]
        w1p[:, HPAD:HPAD + 2730] = w1[l][:, 2730:5460]
        d[f"w1{l}"] = w1p
        b1p = np.zeros((1, H2), np.float32)
        b1p[0, 0:2730] = b1[l][0:2730]
        b1p[0, HPAD:HPAD + 2730] = b1[l][2730:5460]
        d[f"b1row{l}"] = b1p
        w2p = np.zeros((HPAD, HID), np.float32)
        w2p[0:2730, :] = w2[l]
        d[f"w2{l}"] = w2p
        d[f"lg_col{l}"] = np.ascontiguousarray(
            np.asarray(inputs["L0_ffn_ln_g"])[l].reshape(NCH, 128).T)
        d[f"lb_col{l}"] = np.ascontiguousarray(
            np.asarray(inputs["L0_ffn_ln_b"])[l].reshape(NCH, 128).T)
        d[f"b2_col{l}"] = np.ascontiguousarray(
            np.asarray(inputs["L0_ffn_b2"])[l].reshape(NCH, 128).T)
        d[f"qs_rep{l}"] = np.ascontiguousarray(np.broadcast_to(
            np.tile(np.asarray(inputs["L0_ffn_q_scale"])[l], HEADS_F), (128, INNER)))
        d[f"ks_rep{l}"] = np.ascontiguousarray(np.broadcast_to(
            np.tile(np.asarray(inputs["L0_ffn_k_scale"])[l], HEADS_F), (128, INNER)))

    cs, ssn, csk, sskn = _rotary_tables()
    pos = np.clip(np.arange(s - HALO, s + OW), 0, 8191)
    d["tab"] = np.ascontiguousarray(
        np.stack([cs[pos], ssn[pos], csk[pos], sskn[pos]], axis=1)
    ).astype(np.float32)
    mq_lo, mq_hi, mk_lo, mk_hi = _local_masks()
    d["mq_lo"], d["mq_hi"], d["mk_lo"], d["mk_hi"] = mq_lo, mq_hi, mk_lo, mk_hi
    d["ident"] = np.eye(128, dtype=np.float32)
    o2 = np.zeros((2, MEGA), np.float32)
    o2[1, :] = 1.0
    d["ones2row"] = o2
    d["negI"] = (NEG * np.eye(128)).astype(np.float32)
    hs = np.zeros((8, 8, 128), np.float32)
    for hh in range(8):
        hs[hh, hh, :] = 1.0
    d["hsel"] = hs
    e4 = np.zeros((1, 16), np.float32)
    for c in range(4):
        e4[0, 4 * c + c] = 1.0
    d["e4"] = e4
    d["halo01"] = np.full((128, 1), 1.0 if s > 0 else 0.0, np.float32)
    return d



from contextlib import ExitStack


class KB:

    def __init__(self, nc, ctx):
        self.nc = nc
        self.ctx = ctx
        self.s = Sched(nc)
        self.inp = {}
        self.sb = {}

    def din(self, name, shape):
        ap = self.nc.dram_tensor(name, list(shape), F32, kind="ExternalInput").ap()
        self.inp[name] = ap
        return ap

    def sbuf(self, name, shape, dtype=BF16, ctx=None):
        return (ctx or self.ctx).enter_context(
            self.nc.sbuf_tensor(name, list(shape), dtype))

    def psum(self, name, shape, dtype=F32, ctx=None):
        return (ctx or self.ctx).enter_context(
            self.nc.psum_tensor(name, list(shape), dtype))

    def gate(self):
        return [t for t in (self.s.barrier_token(e) for e in ENGINES) if t is not None]

    def build_io(self):
        d = self.din
        d("x_extT", (HID, TEXT))
        d("x_restT", (HID, 4096))
        d("mask_ext_col", (128, NT_EXT))
        d("mask_rest_col", (128, NT_REST))
        d("maskrow_col", (128, NT_EXT))
        for w in ("Wh", "Wq", "Wk", "Wv", "Wo"):
            d(w, (HID, HID))
        for r in ("bh_row", "beta_row", "bk_row", "bv_row"):
            d(r, (1, HID))
        d("bq_col", (128, NCH))
        d("bo_col", (128, NCH))
        d("lnw_rep", (128, HID))
        d("lnb_rep", (128, HID))
        for l in range(2):
            d(f"wqkv{l}", (HID, 3 * INNER))
            d(f"wo{l}", (INNER, HID))
            d(f"w1{l}", (HID, H2))
            d(f"b1row{l}", (1, H2))
            d(f"w2{l}", (HPAD, HID))
            d(f"lg_col{l}", (128, NCH))
            d(f"lb_col{l}", (128, NCH))
            d(f"b2_col{l}", (128, NCH))
            d(f"qs_rep{l}", (128, INNER))
            d(f"ks_rep{l}", (128, INNER))
        d("tab", (TEXT, 4, DF))
        d("ones2row", (2, MEGA))
        d("halo01", (128, 1))
        d("e4", (1, 16))
        d("hsel", (8, 8, 128))
        for m in ("mq_lo", "mq_hi", "mk_lo", "mk_hi", "ident", "negI"):
            d(m, (128, 128))
        self.outT = self.nc.dram_tensor("outT", [HID, OW], F32, kind="ExternalOutput").ap()
        self.dbg_qf = self.nc.dram_tensor("dbg_qf", [128, 4, 512], F32, kind="ExternalOutput").ap()
        self.dbg_kf = self.nc.dram_tensor("dbg_kf", [128, 4, 512], F32, kind="ExternalOutput").ap()
        self.dbg_vt = self.nc.dram_tensor("dbg_vt", [128, 4, 520], F32, kind="ExternalOutput").ap()
        self.dbg_m = self.nc.dram_tensor("dbg_m", [8, TEXT], F32, kind="ExternalOutput").ap()
        self.dbg_o = self.nc.dram_tensor("dbg_o", [128, 4, 512], F32, kind="ExternalOutput").ap()
        self.dbg_kf2 = self.nc.dram_tensor("dbg_kf2", [128, 4, 512], F32, kind="ExternalOutput").ap()
        self.dbg_vt2 = self.nc.dram_tensor("dbg_vt2", [128, 4, 520], F32, kind="ExternalOutput").ap()
        self.x1b = self.nc.dram_tensor("x1b", [128, NCH, OW], BF16, kind="Internal").ap()
        self.w1bf = [self.nc.dram_tensor(f"w1bf{l}", [HID, H2], BF16, kind="Internal").ap()
                     for l in range(2)]
        self.w2bf = [self.nc.dram_tensor(f"w2bf{l}", [HPAD, HID], BF16, kind="Internal").ap()
                     for l in range(2)]

    def build_consts(self):
        s, sb, inp = self.s, self.sb, self.inp
        for n, shp, dt in [
            ("ones_row", (1, 640), BF16), ("ones_col", (128, 1), BF16),
            ("identF", (128, 128), F32),
            ("negI", (128, 128), BF16),
            ("mk_lo", (128, 128), BF16), ("mk_hi", (128, 128), BF16),
            ("eps", (128, 1), F32), ("maskrow", (128, NT_EXT), F32),
            ("zrow", (1, 512), BF16),
            ("X", (128, NCH, TEXT), BF16),
            ("lg_col0", (128, NCH), BF16), ("lb_col0", (128, NCH), BF16),
            ("lg_col1", (128, NCH), BF16), ("lb_col1", (128, NCH), BF16),
            ("b2_col0", (128, NCH), F32), ("b2_col1", (128, NCH), F32),
            ("halo01", (128, 1), F32), ("e4", (1, 16), F32),
        ]:
            sb[n] = self.sbuf(n + "_g", shp, dt)

        dco = s.new_dma_sem("consts", group=True)
        g = "gpsimd"
        for m in ("negI", "mk_lo", "mk_hi"):
            s.dma(g, sb[m][:, :], inp[m][:, :], dco)
        s.dma(g, sb["identF"][:, :], inp["ident"][:, :], dco)
        s.dma(g, sb["maskrow"][:, :], inp["maskrow_col"][:, :], dco)
        s.dma(g, sb["halo01"][:, :], inp["halo01"][:, :], dco)
        s.dma(g, sb["e4"][:, :], inp["e4"][:, :], dco)
        for l in range(2):
            s.dma(g, sb[f"lg_col{l}"][:, :], inp[f"lg_col{l}"][:, :], dco)
            s.dma(g, sb[f"lb_col{l}"][:, :], inp[f"lb_col{l}"][:, :], dco)
            s.dma(g, sb[f"b2_col{l}"][:, :], inp[f"b2_col{l}"][:, :], dco)
        t_cd = s.ops[g][-1]
        s.add(g, lambda gg: gg.memset(sb["ones_row"][0:1, 0:640], 1.0), [t_cd])
        s.add(g, lambda gg: gg.memset(sb["ones_col"][:, :], 1.0))
        s.add(g, lambda gg: gg.memset(sb["zrow"][:, :], 0.0))
        t_ms = s.add(g, lambda gg: gg.memset(sb["eps"][:, :], 1e-5))
        self.d_consts = dco
        self.t_consts = t_ms

        dX = s.new_dma_sem("xload", group=True)
        xin = inp["x_extT"].rearrange("(c p) t -> p c t", p=128)
        tx = None
        for i in range(4):
            tx = s.dma(g, sb["X"][:, 2 * i:2 * i + 2, :], xin[:, 2 * i:2 * i + 2, :], dX)
        self.t_X = tx

        dwc = s.new_dma_sem("wcast", group=True)
        tw = None
        for l in range(2):
            for i in range(4):
                tw = s.dma(g, self.w1bf[l][:, i * (H2 // 4):(i + 1) * (H2 // 4)],
                           inp[f"w1{l}"][:, i * (H2 // 4):(i + 1) * (H2 // 4)], dwc)
            for i in range(4):
                tw = s.dma(g, self.w2bf[l][i * (HPAD // 4):(i + 1) * (HPAD // 4), :],
                           inp[f"w2{l}"][i * (HPAD // 4):(i + 1) * (HPAD // 4), :], dwc)
        self.t_wcast = tw

    def build_p1(self):
        s, nc, sb, inp = self.s, self.nc, self.sb, self.inp
        sb["kv_sb"] = self.sbuf("kv_sb", (128, 1040), BF16, self.es_kv)
        sb["kv2"] = self.sbuf("kv2", (128, 1040), BF16, self.es_kv)
        with ExitStack() as es:
            Wk = self.sbuf("WkSb", (128, NCH, HID), BF16, es)
            Wv = self.sbuf("WvSb", (128, NCH, HID), BF16, es)
            xr = [self.sbuf(f"xr{i}", (128, NCH, 128), BF16, es) for i in range(2)]
            ktile = self.sbuf("ktile", (128, HID), BF16, es)
            brow1 = self.sbuf("brow1", (1, 2 * HID), BF16, es)
            vtile = self.sbuf("vtile", (128, 16 * 65), BF16, es)
            sb["mask_ext"] = self.sbuf("mask_ext_s", (128, NT_EXT), F32, es)
            sb["mask_rest"] = self.sbuf("mask_rest_s", (128, NT_REST), F32, es)
            kps = [self.psum(f"kps{i}", (128, 512), F32, es) for i in range(2)]
            vps = [self.psum(f"vps{i}", (128, 512), F32, es) for i in range(2)]
            kvp = [self.psum(f"kvp{i}", (64, 512), F32, es) for i in range(3)]

            dW = s.new_dma_sem("p1w", group=True)
            s.dma("gpsimd", sb["mask_ext"][:, :], inp["mask_ext_col"][:, :], dW)
            s.dma("gpsimd", sb["mask_rest"][:, :], inp["mask_rest_col"][:, :], dW)
            s.dma("gpsimd", brow1[0:1, 0:HID], inp["bk_row"][:, :], dW)
            s.dma("gpsimd", brow1[0:1, HID:2 * HID], inp["bv_row"][:, :], dW)
            wk_in = inp["Wk"].rearrange("(c p) o -> p c o", p=128)
            wv_in = inp["Wv"].rearrange("(c p) o -> p c o", p=128)
            tw = None
            for i in range(4):
                s.dma("gpsimd", Wk[:, 2 * i:2 * i + 2, :], wk_in[:, 2 * i:2 * i + 2, :], dW)
                tw = s.dma("gpsimd", Wv[:, 2 * i:2 * i + 2, :], wv_in[:, 2 * i:2 * i + 2, :], dW)
            vones = s.add("vector", lambda v: v.memset(
                bass.AP(vtile, 64, [[16 * 65, 128], [65, 16]]), 1.0), [])

            xrest_in = inp["x_restT"].rearrange("(c p) t -> p c t", p=128)
            dxr = [s.new_dma_sem(f"xr{i}") for i in range(2)]

            t_kevac = t_vevac = t_kmask = t_vcopy = t_kvmm = None
            t_xr_free = [None, None]
            started = {}
            n_total = NT_EXT + NT_REST

            for it in range(n_total):
                is_ext = it < NT_EXT
                ti = it if is_ext else it - NT_EXT
                if is_ext:
                    def xsl(c, ti=ti):
                        return sb["X"][:, c, 128 * ti:128 * (ti + 1)]
                    xdep = self.t_X
                    mcol = sb["mask_ext"][:, ti:ti + 1]
                else:
                    slot = ti % 2
                    ld = s.dma("gpsimd", xr[slot][:, :, :],
                               xrest_in[:, :, 128 * ti:128 * (ti + 1)], dxr[slot],
                               deps=[t_xr_free[slot]])

                    def xsl(c, slot=slot):
                        return xr[slot][:, c, :]
                    xdep = ld
                    mcol = sb["mask_rest"][:, ti:ti + 1]

                mm = None
                for half in range(2):
                    for c in range(NCH):
                        deps = [xdep, tw, t_kevac, t_kmask, self.t_consts] if (half == 0 and c == 0) else []
                        mm = s.add("tensor", lambda t, c=c, half=half, xs=xsl, st=(c == 0): t.matmul(
                            kps[half][:, :], xs(c), Wk[:, c, 512 * half:512 * (half + 1)],
                            start=st, stop=False), deps)
                    mm = s.add("tensor", lambda t, half=half: t.matmul(
                        kps[half][:, :], sb["ones_row"][0:1, 0:128],
                        brow1[0:1, 512 * half:512 * (half + 1)],
                        start=False, stop=True), [tw, self.t_consts] if it == 0 else [])
                t_kmm = mm
                for half in range(2):
                    for c in range(NCH):
                        deps = [t_vevac] if (half == 0 and c == 0) else []
                        mm = s.add("tensor", lambda t, c=c, half=half, xs=xsl, st=(c == 0): t.matmul(
                            vps[half][:, :], xs(c), Wv[:, c, 512 * half:512 * (half + 1)],
                            start=st, stop=False), deps)
                    mm = s.add("tensor", lambda t, half=half: t.matmul(
                        vps[half][:, :], sb["ones_row"][0:1, 0:128],
                        brow1[0:1, HID + 512 * half:HID + 512 * (half + 1)],
                        start=False, stop=True), [])
                t_vmm = mm
                if not is_ext:
                    t_xr_free[ti % 2] = t_vmm

                t1 = s.add("scalar", lambda sc: sc.activation(
                    ktile[:, 0:512], kps[0][:, :], AF.Sigmoid), [t_kmm, t_kvmm])
                t_kevac = s.add("scalar", lambda sc: sc.activation(
                    ktile[:, 512:1024], kps[1][:, :], AF.Sigmoid), [t1])
                t_kmask = s.add("vector", lambda v, mcol=mcol: v.tensor_scalar(
                    ktile[:, :], ktile[:, :], mcol, None, ALU.mult),
                    [t_kevac, self.t_consts])
                t1 = s.add("vector", lambda v: v.tensor_copy(
                    bass.AP(vtile, 0, [[16 * 65, 128], [65, 8], [1, 64]]),
                    vps[0][:, :].rearrange("p (h m) -> p h m", m=64)),
                    [t_vmm, t_kvmm, vones])
                t_vcopy = s.add("vector", lambda v: v.tensor_copy(
                    bass.AP(vtile, 8 * 65, [[16 * 65, 128], [65, 8], [1, 64]]),
                    vps[1][:, :].rearrange("p (h m) -> p h m", m=64)), [t1])
                t_vevac = t_vcopy

                for hd in range(16):
                    piece = 0 if hd < 7 else (1 if hd < 14 else 2)
                    cb = hd - (0, 7, 14)[piece]
                    st = not started.get(piece, False)
                    started[piece] = True
                    last = (it == n_total - 1) and (hd in (6, 13, 15))
                    mm = s.add("tensor", lambda t, hd=hd, piece=piece, cb=cb, st=st, last=last: t.matmul(
                        kvp[piece][:, 65 * cb:65 * cb + 65],
                        ktile[:, 64 * hd:64 * (hd + 1)],
                        vtile[:, 65 * hd:65 * (hd + 1)],
                        start=st, stop=last),
                        [t_kmask, t_vcopy] if hd == 0 else [])
                t_kvmm = mm

            mzk1 = s.add("vector", lambda v: v.memset(sb["kv_sb"][64:128, :], 0.0))
            mzk2 = s.add("vector", lambda v: v.memset(sb["kv2"][0:64, :], 0.0), [mzk1])
            e1 = s.add("vector", lambda v: v.tensor_copy(
                sb["kv_sb"][0:64, 0:455], kvp[0][:, 0:455]), [t_kvmm])
            e2 = s.add("vector", lambda v: v.tensor_copy(
                sb["kv_sb"][0:64, 455:910], kvp[1][:, 0:455]), [e1])
            e3 = s.add("vector", lambda v: v.tensor_copy(
                sb["kv_sb"][0:64, 910:1040], kvp[2][:, 0:130]), [e2])
            self.t_kv = s.dma("sync", sb["kv2"][64:128, :], sb["kv_sb"][0:64, :],
                              s.new_dma_sem("kvdup", group=True),
                              deps=[e3, mzk2])

    def build_p2(self):
        s, nc, sb, inp = self.s, self.nc, self.sb, self.inp
        gate = self.gate()
        self.t_ttg2 = None
        with ExitStack() as es:
            Wq = self.sbuf("WqSb", (128, NCH, HID), BF16, es)
            Wh = self.sbuf("WhSb", (128, NCH, HID), BF16, es)
            Wo = self.sbuf("WoSb", (128, NCH, HID), BF16, es)
            lnw_rep = self.sbuf("lnw_rep_s", (128, HID), BF16, es)
            lnb_rep = self.sbuf("lnb_rep_s", (128, HID), BF16, es)
            brow2 = self.sbuf("brow2", (1, 2 * HID), BF16, es)
            qf = self.sbuf("qf", (128, NCH, 512), BF16, es)
            gatedT = self.sbuf("gatedT", (128, NCH, 512), BF16, es)
            ao = self.sbuf("ao", (128, NCH, 512), BF16, es)
            htok = self.sbuf("htok", (128, HID), BF16, es)
            attn = self.sbuf("attn", (128, HID), BF16, es)
            lnout = self.sbuf("lnout", (128, HID), BF16, es)
            gated = self.sbuf("gated", (128, HID), BF16, es)
            bn = self.sbuf("bn", (128, 16), F32, es)
            sb["ident"] = self.sbuf("ident_s", (128, 128), BF16, es)
            sb["bq_col"] = self.sbuf("bq_col_s", (128, NCH), F32, es)
            sb["bo_col"] = self.sbuf("bo_col_s", (128, NCH), F32, es)
            denb = self.sbuf("denb", (128, 16), F32, es)
            xsq = self.sbuf("xsq", (128, NCH, 512), BF16, es)
            rowA = self.sbuf("rowA", (1, 512), F32, es)
            rowB = self.sbuf("rowB", (1, 512), F32, es)
            rowC = self.sbuf("rowC", (1, 512), F32, es)
            rowBb = self.sbuf("rowBb", (1, 512), BF16, es)
            rowCb = self.sbuf("rowCb", (1, 512), BF16, es)
            P = [self.psum(f"p2b{i}", (128, 512), F32, es) for i in range(7)]
            TP = self.psum("p2tp", (128, 1024), BF16, es)

            dW = s.new_dma_sem("p2w", group=True)
            s.dma("gpsimd", sb["ident"][:, :], inp["ident"][:, :], dW, deps=gate)
            s.dma("gpsimd", sb["bq_col"][:, :], inp["bq_col"][:, :], dW)
            s.dma("gpsimd", sb["bo_col"][:, :], inp["bo_col"][:, :], dW)
            s.dma("gpsimd", lnw_rep[:, :], inp["lnw_rep"][:, :], dW)
            s.dma("gpsimd", lnb_rep[:, :], inp["lnb_rep"][:, :], dW)
            s.dma("gpsimd", brow2[0:1, 0:HID], inp["bh_row"][:, :], dW)
            t_bb = s.dma("gpsimd", brow2[0:1, HID:2 * HID], inp["beta_row"][:, :], dW)
            t_brow = s.add("vector", lambda v: v.tensor_add(
                brow2[0:1, 0:HID], brow2[0:1, 0:HID],
                brow2[0:1, HID:2 * HID]), [t_bb])
            tw = None
            for nm, t in (("Wq", Wq), ("Wh", Wh), ("Wo", Wo)):
                win = inp[nm].rearrange("(c p) o -> p c o", p=128)
                for i in range(2):
                    tw = s.dma("gpsimd", t[:, 4 * i:4 * i + 4, :],
                               win[:, 4 * i:4 * i + 4, :], dW, deps=gate if nm == "Wq" and i == 0 else [])
            t_w = tw
            dx1 = s.new_dma_sem("x1store", group=True)
            self.d_x1b = dx1

            HEAD_PIECE = [(0, 0), (0, 1), (0, 2), (0, 3), (0, 4), (0, 5), (0, 6),
                          (1, 0), (1, 1), (1, 2), (1, 3), (1, 4), (1, 5), (1, 6),
                          (2, 0), (2, 1)]
            NP = (7, 7, 2)
            KV = sb["kv_sb"]

            groups = [(512 * g, 512) for g in range(10)] + [(5120, 128)]
            t_qevac = [None, None]
            t_hevac = None
            t_div = None
            t_tevac = None
            t_aoevac = None
            t_resid = None
            t_stats = None
            t_x1 = None
            t_wo = None
            t_num = None
            t_sq = None
            t_g_prev = None

            for (g0, Ng) in groups:
                ntile = Ng // 128
                for c in range(NCH):
                    pb = c % 2
                    for fin in range(NCH):
                        deps = []
                        if c == 0 and fin == 0:
                            deps = [t_w, self.t_X, t_div, t_aoevac, t_stats] + (gate if g0 == 0 else [])
                        if fin == 0:
                            deps = deps + [t_qevac[pb]]
                        mm = s.add("tensor", lambda t, c=c, fin=fin, pb=pb, g0=g0, Ng=Ng, st=(fin == 0): t.matmul(
                            P[pb][:, 0:Ng], Wq[:, fin, 128 * c:128 * (c + 1)],
                            sb["X"][:, fin, g0:g0 + Ng],
                            start=st, stop=(fin == NCH - 1)), deps)
                    t_qevac[pb] = s.add("scalar", lambda sc, c=c, pb=pb, Ng=Ng: sc.activation(
                        qf[:, c, 0:Ng], P[pb][:, 0:Ng], AF.Sigmoid,
                        bias=sb["bq_col"][:, c:c + 1]), [mm])
                t_qf = t_qevac[(NCH - 1) % 2]

                def emit_gT(ti, tg_tok):
                    nonlocal t_tevac
                    tt = None
                    for c in range(NCH):
                        deps = [tg_tok, self.t_consts] if c == 0 else []
                        if c == 0:
                            deps = deps + [t_tevac]
                        tt = s.add("tensor", lambda t, c=c: t.transpose(
                            TP[:, 128 * c:128 * (c + 1)], gated[:, 128 * c:128 * (c + 1)],
                            sb["ident"][:, :]), deps)
                    self.t_ttg2 = tt
                    t_tevac = s.add("scalar", lambda sc, ti=ti: sc.copy(
                        gatedT[:, :, 128 * ti:128 * (ti + 1)],
                        TP[:, :].rearrange("p (c x) -> p c x", x=128)), [tt])

                pend_g = None
                for ti in range(ntile):
                    tcol = g0 + 128 * ti
                    ext_ti = tcol // 128
                    if P2STOP < 2:
                        continue
                    mm = None
                    for half in range(2):
                        pb = 2 + half
                        for c in range(NCH):
                            deps = []
                            if half == 0 and c == 0:
                                deps = [t_hevac, t_w]
                            mm = s.add("tensor", lambda t, c=c, half=half, pb=pb, tcol=tcol, st=(c == 0): t.matmul(
                                P[pb][:, :], sb["X"][:, c, tcol:tcol + 128],
                                Wh[:, c, 512 * half:512 * (half + 1)],
                                start=st, stop=False), deps)
                        mm = s.add("tensor", lambda t, half=half, pb=pb: t.matmul(
                            P[pb][:, :], sb["ones_row"][0:1, 0:128],
                            brow2[0:1, 512 * half:512 * (half + 1)],
                            start=False, stop=True), [t_brow] if g0 == 0 and ti == 0 else [])
                    t_hmm = mm
                    th1 = s.add("vector", lambda v: v.tensor_copy(
                        htok[:, 0:512], P[2][:, :]), [t_hmm, t_tevac, t_g_prev])
                    t_hevac = s.add("vector", lambda v: v.tensor_copy(
                        htok[:, 512:1024], P[3][:, :]), [th1])

                    if P2STOP < 3:
                        continue
                    mm = None
                    for hd in range(16):
                        piece, slot = HEAD_PIECE[hd]
                        deps = []
                        if hd == 0:
                            deps = [t_qf, self.t_kv, t_div, t_stats]
                        KVp = KV if hd % 2 == 0 else sb["kv2"]
                        mm = s.add("tensor", lambda t, hd=hd, piece=piece, slot=slot, ti=ti, KVp=KVp: t.matmul(
                            P[4 + piece][:, 65 * slot:65 * slot + 65],
                            qf[:, hd // 2, 128 * ti:128 * (ti + 1)],
                            KVp[:, 65 * hd:65 * hd + 65],
                            start=(slot == 0), stop=(slot == NP[piece] - 1)), deps)
                    t_num = mm
                    if pend_g is not None:
                        emit_gT(*pend_g)
                        pend_g = None

                    if P2STOP < 4:
                        continue
                    td = None
                    for piece in range(3):
                        nh = NP[piece]
                        base = (0, 7, 14)[piece]
                        td = s.add("scalar", lambda sc, piece=piece, nh=nh, base=base: sc.copy(
                            bass.AP(denb, base, [[16, 128], [1, nh], [1, 1]]),
                            bass.AP(P[4 + piece], 64, [[512, 128], [65, nh], [1, 1]])),
                            [t_num, t_g_prev] if piece == 0 else [])
                        td = s.add("vector", lambda v, nh=nh, base=base: v.reciprocal(
                            denb[:, base:base + nh], denb[:, base:base + nh]), [td])
                        td = s.add("vector", lambda v, piece=piece, nh=nh, base=base: v.tensor_tensor(
                            bass.AP(attn, 64 * base, [[HID, 128], [64, nh], [1, 64]]),
                            bass.AP(P[4 + piece], 0, [[512, 128], [65, nh], [1, 64]]),
                            bass.AP(denb, base, [[16, 128], [1, nh], [0, 64]]),
                            ALU.mult), [td])
                    t_div = td

                    if P2STOP < 5:
                        continue
                    b1 = s.add("vector", lambda v: v.bn_stats(bn[:, 0:6], attn[:, 0:512]), [t_div])
                    b2 = s.add("vector", lambda v: v.bn_stats(bn[:, 6:12], attn[:, 512:1024]), [b1])
                    b3 = s.add("vector", lambda v: v.bn_aggr(bn[:, 12:14], bn[:, 0:12]), [b2])
                    sq = s.add("scalar", lambda sc: sc.activation(
                        bn[:, 14:15], bn[:, 13:14], AF.Sqrt, bias=sb["eps"][:, :]), [b3])
                    t_sq = s.add("vector", lambda v: v.reciprocal(bn[:, 15:16], bn[:, 14:15]), [sq])
                    t_g = s.add("vector", lambda v: v.tensor_scalar(
                        lnout[:, :], attn[:, :], bn[:, 12:13], bn[:, 15:16],
                        ALU.subtract, ALU.mult), [t_sq])
                    t_g = s.add("vector", lambda v: v.scalar_tensor_tensor(
                        lnout[:, :], lnout[:, :], 0.0, lnw_rep[:, :],
                        ALU.bypass, ALU.mult), [t_g])
                    t_g = s.add("vector", lambda v: v.tensor_add(
                        lnout[:, :], lnout[:, :], lnb_rep[:, :]), [t_g])
                    t_g = s.add("vector", lambda v: v.tensor_mul(
                        gated[:, :], lnout[:, :], htok[:, :]),
                        [t_g, t_hevac, self.t_ttg2])
                    t_g_prev = t_g

                    if P2STOP < 6:
                        continue
                    pend_g = (ti, t_g)
                if pend_g is not None:
                    emit_gT(*pend_g)
                    pend_g = None

                if P2STOP < 7:
                    continue
                mm = None
                for c in range(NCH):
                    pb = c % 2
                    for fin in range(NCH):
                        deps = []
                        if c == 0 and fin == 0:
                            deps = [t_tevac, t_qf]
                        if fin == 0:
                            deps = deps + [t_aoevac]
                        mm = s.add("tensor", lambda t, c=c, fin=fin, pb=pb, Ng=Ng, st=(fin == 0): t.matmul(
                            P[pb][:, 0:Ng], Wo[:, fin, 128 * c:128 * (c + 1)],
                            gatedT[:, fin, 0:Ng],
                            start=st, stop=(fin == NCH - 1)), deps)
                    t_aoevac = s.add("scalar", lambda sc, c=c, pb=pb, Ng=Ng: sc.activation(
                        ao[:, c, 0:Ng], P[pb][:, 0:Ng], AF.Relu,
                        bias=sb["bo_col"][:, c:c + 1]), [mm])

                if P2STOP < 8:
                    continue
                t_resid = s.add("vector", lambda v, g0=g0, Ng=Ng: v.tensor_add(
                    sb["X"][:, :, g0:g0 + Ng],
                    sb["X"][:, :, g0:g0 + Ng], ao[:, :, 0:Ng]), [t_aoevac, t_stats])

                if P2STOP < 9:
                    continue
                t_sqr = s.add("scalar", lambda sc, g0=g0, Ng=Ng: sc.activation(
                    xsq[:, :, 0:Ng], sb["X"][:, :, g0:g0 + Ng], AF.Square), [t_resid])
                mm = None
                for c in range(NCH):
                    mm = s.add("tensor", lambda t, c=c, g0=g0, Ng=Ng, st=(c == 0): t.matmul(
                        P[4][0:1, 0:Ng], sb["ones_col"][:, :], sb["X"][:, c, g0:g0 + Ng],
                        start=st, stop=(c == NCH - 1)),
                        [t_resid, t_num, t_div] if c == 0 else [])
                for c in range(NCH):
                    mm = s.add("tensor", lambda t, c=c, Ng=Ng, st=(c == 0): t.matmul(
                        P[5][0:1, 0:Ng], sb["ones_col"][:, :], xsq[:, c, 0:Ng],
                        start=st, stop=(c == NCH - 1)), [t_sqr] if c == 0 else [])
                t_smm = mm
                r1 = s.add("vector", lambda v, Ng=Ng: v.tensor_scalar(
                    rowA[0:1, 0:Ng], P[4][0:1, 0:Ng], 1.0 / HID, None, ALU.mult), [t_smm])
                r2 = s.add("vector", lambda v, Ng=Ng: v.tensor_scalar(
                    rowB[0:1, 0:Ng], P[5][0:1, 0:Ng], 1.0 / HID, None, ALU.mult), [r1])
                r3 = s.add("vector", lambda v, Ng=Ng: v.tensor_mul(
                    rowC[0:1, 0:Ng], rowA[0:1, 0:Ng], rowA[0:1, 0:Ng]), [r2])
                r4 = s.add("vector", lambda v, Ng=Ng: v.tensor_sub(
                    rowB[0:1, 0:Ng], rowB[0:1, 0:Ng], rowC[0:1, 0:Ng]), [r3])
                r5 = s.add("scalar", lambda sc, Ng=Ng: sc.activation(
                    rowB[0:1, 0:Ng], rowB[0:1, 0:Ng], AF.Sqrt, bias=sb["eps"][0:1, 0:1]), [r4])
                r6 = s.add("vector", lambda v, Ng=Ng: v.reciprocal(
                    rowB[0:1, 0:Ng], rowB[0:1, 0:Ng]), [r5])
                r7 = s.add("vector", lambda v, Ng=Ng: v.tensor_mul(
                    rowC[0:1, 0:Ng], rowA[0:1, 0:Ng], rowB[0:1, 0:Ng]), [r6])
                c1 = s.add("vector", lambda v, Ng=Ng: v.tensor_copy(
                    rowBb[0:1, 0:Ng], rowB[0:1, 0:Ng]), [r7])
                c2 = s.add("vector", lambda v, Ng=Ng: v.tensor_copy(
                    rowCb[0:1, 0:Ng], rowC[0:1, 0:Ng]), [c1])
                m1 = s.add("tensor", lambda t, Ng=Ng: t.matmul(
                    P[4][:, 0:Ng], sb["ones_row"][0:1, 0:128], rowBb[0:1, 0:Ng],
                    start=True, stop=True), [c1, r1])
                m2 = s.add("tensor", lambda t, Ng=Ng: t.matmul(
                    P[5][:, 0:Ng], sb["ones_row"][0:1, 0:128], rowCb[0:1, 0:Ng],
                    start=True, stop=True), [c2, r2])
                x1a = s.add("vector", lambda v, g0=g0, Ng=Ng: v.tensor_mul(
                    sb["X"][:, :, g0:g0 + Ng], sb["X"][:, :, g0:g0 + Ng],
                    bass.AP(P[4], 0, [[512, 128], [0, NCH], [1, Ng]])), [m1])
                t_x1 = x1a
                t_stats = s.add("vector", lambda v, g0=g0, Ng=Ng: v.tensor_sub(
                    sb["X"][:, :, g0:g0 + Ng], sb["X"][:, :, g0:g0 + Ng],
                    bass.AP(P[5], 0, [[512, 128], [0, NCH], [1, Ng]])), [t_x1, m2])

                lo = max(g0, HALO)
                hi = g0 + Ng
                if hi > lo:
                    s.dma("sync", self.x1b[:, :, lo - HALO:hi - HALO],
                          sb["X"][:, :, lo:hi], dx1, deps=[t_stats])
            self.t_p2 = s.add("vector", lambda v: v.tensor_mul(
                sb["X"][:, :, HALO - 1:HALO], sb["X"][:, :, HALO - 1:HALO],
                bass.AP(sb["halo01"], 0, [[1, 128], [0, NCH], [0, 1]])),
                [t_stats, self.t_consts])

    def build_layer(self, l, do_mlp=True):
        s, nc, sb, inp = self.s, self.nc, self.sb, self.inp
        gate = self.gate()
        if l == 0:
            pcol0, n_proj, qcol0, CL = 128, 10, 128, 1
        else:
            pcol0, n_proj, qcol0, CL = 128, 10, 640, 1
        sqd = (qcol0 - pcol0) // 512
        n_O = (TEXT - qcol0) // 512
        n_iter = n_proj + CL + 1
        Bshift = CL + sqd

        with ExitStack() as es:
            wqkv = self.sbuf(f"wqkv_l{l}", (128, NCH, 3 * INNER), BF16, es)
            wo = self.sbuf(f"wo_l{l}", (128, 4, HID), BF16, es)
            qs_rep = self.sbuf(f"qsr{l}", (128, INNER), BF16, es)
            ks_rep = self.sbuf(f"ksr{l}", (128, INNER), BF16, es)
            qf = [self.sbuf(f"qf{l}_{i}", (128, 4, 512), BF16, es) for i in range(3)]
            kf_e = [self.sbuf(f"kfe{l}_{i}", (128, 4, 512), BF16, es) for i in range(3)]
            kf_o = [self.sbuf(f"kfo{l}_{i}", (128, 4, 512), BF16, es) for i in range(3)]
            vt = [self.sbuf(f"vt{l}_{i}", (128, 4, 520), BF16, es) for i in range(3)]
            tabs = self.sbuf(f"tab{l}", (128, 4, 4, DF), BF16, es)
            pt = [self.sbuf(f"pt{l}_{i}", (128, 640), BF16, es) for i in range(3)]
            m8 = self.sbuf(f"m8_{l}", (128, 1), F32, es)
            maskrowH = self.sbuf(f"mrowH{l}", (128, NT_EXT), F32, es)
            O_acc = self.sbuf(f"Oacc{l}", (65, HEADS_F, 512), BF16, es)
            o_sb = [self.sbuf(f"osb{l}_0", (128, 4, 512), BF16, es)]
            tsum = self.sbuf(f"tsum{l}", (65, 512), F32, es)
            rrowb = self.sbuf(f"rrowb{l}", (1, 512), BF16, es)
            rrowf = self.sbuf(f"rrowf{l}", (1, 512), F32, es)
            r4sb = self.sbuf(f"r4sb{l}", (128, 4), F32, es)
            ubuf = self.sbuf(f"ubuf{l}", (128, 512), BF16, es)
            qhat = self.sbuf(f"qhat{l}", (128, 512), BF16, es)
            qrot = self.sbuf(f"qrot{l}", (128, 512), F32, es)
            krot = self.sbuf(f"krot{l}", (128, 512), F32, es)
            xqk = self.sbuf(f"xqk{l}", (128, 512), BF16, es)
            qraw = self.sbuf(f"qraw{l}", (128, 512), BF16, es)
            bnf = self.sbuf(f"bnf{l}", (128, 32), F32, es)
            P = [self.psum(f"L{l}b{i}", (128, 512), F32, es) for i in range(8)]

            dW = s.new_dma_sem(f"L{l}w", group=True)
            win = inp[f"wqkv{l}"].rearrange("(c p) o -> p c o", p=128)
            tw = None
            for i in range(4):
                tw = s.dma("gpsimd", wqkv[:, 2 * i:2 * i + 2, :],
                           win[:, 2 * i:2 * i + 2, :], dW, deps=gate if i == 0 else [])
            woin = inp[f"wo{l}"].rearrange("(c p) o -> p c o", p=128)
            tw = s.dma("gpsimd", wo[:, :, :], woin[:, :, :], dW)
            tw = s.dma("gpsimd", qs_rep[:, :], inp[f"qs_rep{l}"][:, :], dW)
            tw = s.dma("gpsimd", ks_rep[:, :], inp[f"ks_rep{l}"][:, :], dW)
            t_w = tw
            t_w8 = s.add("vector", lambda v: v.tensor_scalar(
                qs_rep[:, :], qs_rep[:, :], 8.0, None, ALU.mult), [t_w])
            t_mr = s.add("vector", lambda v: v.tensor_mul(
                ubuf[:, :], qs_rep[:, :], ks_rep[:, :]), [t_w8])
            t_mr = s.add("vector", lambda v: v.tensor_reduce(
                m8[:, 0:1], ubuf[:, :], AX.X, ALU.max), [t_mr])
            t_mrow = s.add("vector", lambda v: v.tensor_scalar(
                maskrowH[:, :], sb["maskrow"][:, :], m8[:, 0:1], None,
                ALU.subtract), [t_mr, self.t_consts])
            dtab = s.new_dma_sem(f"L{l}tab")

            mz = []
            mz.append(s.add("vector", lambda v: v.memset(kf_e[2][:, :, :], 0.0), gate))
            mz.append(s.add("vector", lambda v: v.memset(kf_o[2][:, :, :], 0.0), []))
            for i in range(2):
                mz.append(s.add("vector", lambda v, i=i: v.memset(
                    kf_e[i][64:128, :, :], 0.0), []))
                mz.append(s.add("vector", lambda v, i=i: v.memset(
                    kf_o[i][0:64, :, :], 0.0), []))
            for i in range(3):
                mz.append(s.add("vector", lambda v, i=i: v.memset(
                    bass.AP(vt[i], 64, [[4 * 520, 128], [520, 4], [65, 8]]), 1.0), []))
            mz.append(s.add("vector", lambda v: v.memset(O_acc[:, :, :], 0.0), []))
            t_init = mz[-1]

            t_tab = None
            t_tabuse = None
            t_vevac = {}
            t_qkevac = {}
            t_qke = {0: [None] * 4, 1: [None] * 4, 2: [None] * 4}
            t_expL = None
            t_expP = [None, None]
            t_ksq_last = None
            self.t_ttq = None
            self.t_ttk = None
            self.t_qe_prev = None
            self.t_ke_prev = None
            self.t_cpr = None
            t_pv = None
            t_oacc = None
            t_tt1 = None
            t_wo = None
            t_wo2 = None
            t_wres = None
            t_uq = None
            t_uk = None
            t_qsqr = None
            t_qrd = None
            t_rep = None
            t_tke_last = None
            t_projmm = None
            t_stmm = None

            def piece_tab(pw):
                nonlocal t_tab
                t_tab = s.dma("gpsimd", tabs[:, :, :, :],
                              inp["tab"][pcol0 + 512 * pw:pcol0 + 512 * pw + 512, :, :]
                              .rearrange("(j p) a d -> p j a d", p=128),
                              dtab, deps=[t_tabuse])

            def piece_v(pw):
                slot = pw % 3
                for j in range(4):
                    tc = pcol0 + 512 * pw + 128 * j
                    mm = None
                    for c in range(NCH):
                        off = -1 if c >= 4 else 0
                        deps = []
                        if c == 0:
                            deps = [t_vevac.get(slot), t_tt1, t_w,
                                    self.t_p2 if l == 0 else
                                    getattr(self, "t_halofix", None),
                                    getattr(self, "_dbg_tok", None)]
                            if pw == 0 and j == 0:
                                deps = deps + gate
                        mm = s.add("tensor", lambda t, c=c, tc=tc, off=off, st=(c == 0): t.matmul(
                            P[6][:, :], sb["X"][:, c, tc + off:tc + off + 128],
                            wqkv[:, c, 1024:1536],
                            start=st, stop=(c == NCH - 1)), deps)
                    t_vevac[slot] = s.add("vector", lambda v, slot=slot, j=j: v.tensor_copy(
                        bass.AP(vt[slot], 520 * j, [[4 * 520, 128], [65, 8], [1, 64]]),
                        P[6][:, :].rearrange("p (h m) -> p h m", m=64)),
                        [mm, t_init, t_pv, getattr(self, "_dbg_tok", None)])

            def piece_q(pw, j):
                nonlocal t_projmm, t_qsqr, t_qrd, t_uq
                tc = pcol0 + 512 * pw + 128 * j
                deps0 = [t_w, self.t_p2 if l == 0 else
                         getattr(self, "t_halofix", None)]
                mm = None
                for c in range(NCH):
                    off = -1 if c >= 4 else 0
                    deps = []
                    if c == 0:
                        deps = deps0 + [t_uq, t_tt1]
                    mm = s.add("tensor", lambda t, c=c, tc=tc, off=off, st=(c == 0): t.matmul(
                        P[2][:, :], sb["X"][:, c, tc + off:tc + off + 128],
                        wqkv[:, c, 0:512],
                        start=st, stop=(c == NCH - 1)), deps)
                t_projmm = mm
                t_qsqr = s.add("scalar", lambda sc: sc.activation(
                    xqk[:, :], P[2][:, :], AF.Square), [mm, t_qrd])
                rd = s.add("vector", lambda v: v.tensor_reduce(
                    bnf[:, 0:8], xqk[:, :].rearrange("p (h d) -> p h d", d=64),
                    AX.X, ALU.add), [t_qsqr])
                t_qrd = rd
                tqr = s.add("vector", lambda v: v.tensor_copy(
                    qraw[:, :], P[2][:, :]), [rd])
                t_uq = tqr

            def piece_k(pw, j):
                nonlocal t_projmm, t_ksq_last, t_qrd, t_uk, t_tabuse
                tc = pcol0 + 512 * pw + 128 * j
                mm = None
                for c in range(NCH):
                    off = -1 if c >= 4 else 0
                    deps = []
                    if c == 0:
                        deps = [t_uk, t_uq, t_tt1]
                    mm = s.add("tensor", lambda t, c=c, tc=tc, off=off, st=(c == 0): t.matmul(
                        P[2][:, :], sb["X"][:, c, tc + off:tc + off + 128],
                        wqkv[:, c, 512:1024],
                        start=st, stop=(c == NCH - 1)), deps)
                t_projmm = mm
                tks = s.add("scalar", lambda sc: sc.activation(
                    xqk[:, :], P[2][:, :], AF.Square), [mm, t_qrd])
                t_ksq_last = tks
                rd = s.add("vector", lambda v: v.tensor_reduce(
                    bnf[:, 8:16], xqk[:, :].rearrange("p (h d) -> p h d", d=64),
                    AX.X, ALU.add), [tks])
                t_qrd = rd
                sq = s.add("scalar", lambda sc: sc.activation(
                    bnf[:, 16:32], bnf[:, 0:16], AF.Sqrt), [rd])
                mx = s.add("vector", lambda v: v.tensor_scalar(
                    bnf[:, 16:32], bnf[:, 16:32], 1e-12, None, ALU.max), [sq])
                rc = s.add("vector", lambda v: v.reciprocal(
                    bnf[:, 16:32], bnf[:, 16:32]), [mx])
                t1 = s.add("vector", lambda v: v.tensor_tensor(
                    ubuf[:, :], qraw[:, :],
                    bass.AP(bnf, 16, [[32, 128], [1, 8], [0, 64]]),
                    ALU.mult), [rc])
                t2 = s.add("vector", lambda v: v.tensor_mul(
                    qhat[:, :], ubuf[:, :], qs_rep[:, :]), [t1, t_w8])
                ta = s.add("vector", lambda v, j=j: v.tensor_tensor(
                    ubuf[:, :], qhat[:, :],
                    bass.AP(tabs, 256 * j + 0 * DF, [[4 * 4 * DF, 128], [0, 8], [1, DF]]),
                    ALU.mult), [t2, t_tab])
                tb = s.add("vector", lambda v, j=j: v.tensor_tensor(
                    qrot[:, :].rearrange("p (h t x) -> p h t x", h=8, t=2),
                    bass.AP(qhat, 32, [[512, 128], [64, 8], [-32, 2], [1, 32]]),
                    bass.AP(tabs, 256 * j + 1 * DF, [[4 * 4 * DF, 128], [0, 8], [32, 2], [1, 32]]),
                    ALU.mult), [ta, self.t_ttq])
                tq = s.add("vector", lambda v: v.tensor_add(
                    qrot[:, :], qrot[:, :], ubuf[:, :]), [tb])
                t1 = s.add("vector", lambda v: v.tensor_tensor(
                    ubuf[:, :], P[2][:, :],
                    bass.AP(bnf, 24, [[32, 128], [1, 8], [0, 64]]),
                    ALU.mult), [tq])
                t_uk = t1
                t2 = s.add("vector", lambda v: v.tensor_mul(
                    qhat[:, :], ubuf[:, :], ks_rep[:, :]), [t1])
                ta = s.add("vector", lambda v, j=j: v.tensor_tensor(
                    ubuf[:, :], qhat[:, :],
                    bass.AP(tabs, 256 * j + 2 * DF, [[4 * 4 * DF, 128], [0, 8], [1, DF]]),
                    ALU.mult), [t2])
                tb = s.add("vector", lambda v, j=j: v.tensor_tensor(
                    krot[:, :].rearrange("p (h t x) -> p h t x", h=8, t=2),
                    bass.AP(qhat, 32, [[512, 128], [64, 8], [-32, 2], [1, 32]]),
                    bass.AP(tabs, 256 * j + 3 * DF, [[4 * 4 * DF, 128], [0, 8], [32, 2], [1, 32]]),
                    ALU.mult), [ta, self.t_ttk])
                tk = s.add("vector", lambda v: v.tensor_add(
                    krot[:, :], krot[:, :], ubuf[:, :]), [tb])
                t_tabuse = tk
                return tq, tk

            def piece_T(pw, j, tq_tok, tk_tok):
                nonlocal t_qkevac, t_tke_last
                slot = pw % 3
                tt = None
                for cc in range(4):
                    deps = [tq_tok, t_w]
                    if cc == 0:
                        deps = deps + [t_qkevac.get(slot), self.t_qe_prev]
                    tt = s.add("tensor", lambda t, cc=cc: t.transpose(
                        P[4][:, 128 * cc:128 * (cc + 1)],
                        qrot[:, 128 * cc:128 * (cc + 1)], sb["identF"][:, :]), deps)
                self.t_ttq = tt
                tqe = s.add("scalar", lambda sc, slot=slot, j=j: sc.copy(
                    qf[slot][:, :, 128 * j:128 * (j + 1)],
                    P[4][:, :].rearrange("p (c x) -> p c x", x=128)),
                    [tt, getattr(self, "_dbg_tok", None)])
                self.t_qe_prev = tqe
                tt = None
                for cc in range(4):
                    deps = [tk_tok]
                    if cc == 0:
                        deps = deps + [tqe, self.t_ke_prev]
                    tt = s.add("tensor", lambda t, cc=cc: t.transpose(
                        P[4][:, 128 * cc:128 * (cc + 1)],
                        krot[:, 128 * cc:128 * (cc + 1)], sb["identF"][:, :]), deps)
                self.t_ttk = tt
                tke = s.add("scalar", lambda sc, slot=slot, j=j: sc.copy(
                    kf_e[slot][0:64, :, 128 * j:128 * (j + 1)],
                    P[4][0:64, :].rearrange("p (c x) -> p c x", x=128)),
                    [tt, getattr(self, "_dbg_tok", None)])
                tke = s.add("scalar", lambda sc, slot=slot, j=j: sc.copy(
                    kf_o[slot][64:128, :, 128 * j:128 * (j + 1)],
                    P[4][64:128, :].rearrange("p (c x) -> p c x", x=128)), [tke])
                self.t_ke_prev = tke
                t_qkevac[slot] = tke
                t_qke[slot][j] = tke
                t_tke_last = tke

            for w in range(n_iter):
                slot = w % 3
                if w < n_proj:
                    piece_tab(w)
                    piece_v(w)
                    pend = None
                    for j in range(4):
                        if pend is not None:
                            piece_T(w, pend[0], pend[1], pend[2])
                        piece_q(w, j)
                        tqt, tkt = piece_k(w, j)
                        pend = (j, tqt, tkt)
                    piece_T(w, pend[0], pend[1], pend[2])

                if l == 0 and w == 2 and DBG_TAP:
                    dsemd = s.new_dma_sem(f"dbg{w}", group=True)
                    s.dma("gpsimd", self.dbg_qf[:, :, :], qf[w % 3][:, :, :], dsemd,
                          deps=[t_tke_last])
                    s.dma("gpsimd", self.dbg_kf[:, :, :], kf_e[w % 3][:, :, :], dsemd)
                    dtv = s.dma("gpsimd", self.dbg_vt[:, :, :], vt[w % 3][:, :, :], dsemd)
                    s.dma("gpsimd", self.dbg_kf2[:, :, :], kf_o[w % 3][:, :, :], dsemd)
                    dtv = s.dma("gpsimd", self.dbg_vt2[:, :, :], vt[(w - 1) % 3][:, :, :], dsemd)
                    self._dbg_tok = dtv
                B_wo = w - 1 - Bshift
                if 0 <= B_wo < n_O:
                    B = B_wo
                    bcol = qcol0 + 512 * B
                    mm = None
                    for c in range(NCH):
                        wbank = P[6] if c % 2 == 0 else P[7]
                        for fin in range(4):
                            deps = []
                            if c == 0 and fin == 0:
                                deps = [t_tt1, t_w, t_oacc, t_vevac.get(w % 3)]
                            if fin == 0:
                                deps = deps + [t_wres]
                            mm = s.add("tensor", lambda t, c=c, fin=fin, wbank=wbank, B=B, st=(fin == 0): t.matmul(
                                wbank[:, :], wo[:, fin, 128 * c:128 * (c + 1)],
                                o_sb[0][:, fin, :],
                                start=st, stop=(fin == 3)), deps)
                        t_wres = s.add("vector", lambda v, c=c, wbank=wbank, bcol=bcol: v.tensor_add(
                            sb["X"][:, c, bcol:bcol + 512],
                            sb["X"][:, c, bcol:bcol + 512], wbank[:, :]), [mm])
                    if bcol <= HALO - 1 < bcol + 512:
                        t_wres = s.add("vector", lambda v: v.tensor_mul(
                            sb["X"][:, :, HALO - 1:HALO],
                            sb["X"][:, :, HALO - 1:HALO],
                            bass.AP(sb["halo01"], 0, [[1, 128], [0, NCH], [0, 1]])),
                            [t_wres])
                        self.t_halofix = t_wres
                    t_wo2 = t_wres
                    t_wo = t_wres
                    if l == 0 and B == 2 and DBG_TAP:
                        dsemo = s.new_dma_sem("dbgo", group=True)
                        t_wo2 = s.dma("gpsimd", self.dbg_o[:, :, :], o_sb[0][:, :, :], dsemo,
                                      deps=[t_tt1])

                pb = w - CL
                B_left = w - Bshift
                B_right = B_left + 1
                do_left = 0 <= B_left < n_O
                do_right = 0 <= B_right < n_O
                if 0 <= pb < n_proj and (do_left or do_right):
                    kslot = pb % 3
                    jlast_r = -1
                    for j in range(4):
                        kt0_ = pcol0 + 512 * pb + 128 * j
                        qn_ = min(kt0_ + 640, TEXT) - kt0_
                        if qn_ > 512 - 128 * j:
                            jlast_r = j

                    for h in range(HEADS_F):
                        kfp = kf_e if h % 2 == 0 else kf_o
                        pends = []
                        rstarted = [False]

                        def emit_pv(j, e1):
                            split = 512 - 128 * j
                            qn_j = min(pcol0 + 512 * pb + 128 * j + 640, TEXT) - (
                                pcol0 + 512 * pb + 128 * j)
                            if do_left:
                                n_l = min(split, qn_j)
                                st = (j == 0)
                                deps = [e1] + ([t_tt1, t_vevac.get(w % 3), t_wo]
                                               if st else [])
                                s.add("tensor", lambda t, j=j, h=h, kslot=kslot, n_l=n_l, st=st: t.matmul(
                                    P[6][0:65, 128 * j:128 * j + n_l],
                                    vt[kslot][:, j, 65 * h:65 * h + 65],
                                    pt[j % 3][:, 0:n_l],
                                    start=st, stop=(j == 3)), deps)
                            if do_right and qn_j > split:
                                n_r = qn_j - split
                                s.add("tensor", lambda t, j=j, h=h, kslot=kslot, n_r=n_r, split=split, jl=jlast_r: t.matmul(
                                    P[7][0:65, 0:n_r],
                                    vt[kslot][:, j, 65 * h:65 * h + 65],
                                    pt[j % 3][:, split:split + n_r],
                                    start=False, stop=(j == jl)), deps=[e1])
                            return s.ops["tensor"][-1]

                        for j in range(4):
                            kt0 = pcol0 + 512 * pb + 128 * j
                            kti = kt0 // 128
                            qe = min(kt0 + 640, TEXT)
                            qn = qe - kt0
                            par = j % 2
                            bankF = P[5] if par == 0 else P[0]
                            bankT = P[3] if par == 0 else P[1]
                            segs = [(pb % 3, 128 * j, min(512 - 128 * j, qn), 0)]
                            rem = qn - segs[0][2]
                            if rem > 0:
                                segs.append((((pb + 1) % 3), 0, rem, segs[0][2]))
                            first = True
                            bstart = {}
                            for (qs_, qoff, qlen, pso) in segs:
                                a = 0
                                while a < qlen:
                                    p = pso + a
                                    if p < 512:
                                        take = min(qlen - a, 512 - p)
                                        bank, boff = bankF, p
                                    else:
                                        take = qlen - a
                                        bank, boff = bankT, p - 512
                                    deps = []
                                    if first:
                                        deps = [t_qkevac.get(pb % 3),
                                                t_qke[(pb + 1) % 3][j],
                                                t_expP[par]]
                                    st = not bstart.get(id(bank), False)
                                    bstart[id(bank)] = True
                                    s.add("tensor", lambda t, qs_=qs_, qoff=qoff, a=a, take=take, bank=bank, boff=boff, h=h, kslot=kslot, j=j, st=st, kfp=kfp: t.matmul(
                                        bank[:, boff:boff + take],
                                        kfp[kslot][:, h // 2, 128 * j:128 * (j + 1)],
                                        qf[qs_][:, h // 2, qoff + a:qoff + a + take],
                                        start=st, stop=False), deps)
                                    first = False
                                    a += take
                            t_stmm = s.add("tensor", lambda t, bankF=bankF: t.matmul(
                                bankF[:, 0:128], sb["negI"][:, :], sb["mk_hi"][:, :],
                                start=False, stop=True), [])
                            if qn > 512:
                                t_stmm = s.add("tensor", lambda t, qn=qn, bankT=bankT: t.matmul(
                                    bankT[:, 0:qn - 512], sb["negI"][:, :],
                                    sb["mk_lo"][:, 0:qn - 512], start=False, stop=True), [])
                            e1 = s.add("scalar", lambda sc, j=j, kti=kti, qn=qn, bankF=bankF: sc.activation(
                                pt[j % 3][:, 0:min(qn, 512)], bankF[:, 0:min(qn, 512)], AF.Exp,
                                bias=maskrowH[:, kti:kti + 1]), [t_stmm, t_pv, t_mrow])
                            if qn > 512:
                                e1 = s.add("scalar", lambda sc, j=j, kti=kti, qn=qn, bankT=bankT: sc.activation(
                                    pt[j % 3][:, 512:qn], bankT[:, 0:qn - 512], AF.Exp,
                                    bias=maskrowH[:, kti:kti + 1]), [e1])
                            t_expL = e1
                            t_expP[par] = e1
                            if j == 0 and do_right:
                                s.add("tensor", lambda t: t.matmul(
                                    P[7][0:65, :], sb["ones_row"][0:1, 0:65],
                                    sb["zrow"][0:1, 0:512], start=True, stop=False),
                                    [t_oacc, t_wo, self.t_consts])
                            if len(pends) == 2:
                                t_pv = emit_pv(*pends.pop(0))
                            pends.append((j, e1))
                        for pp in pends:
                            t_pv = emit_pv(*pp)
                        pends = []
                        if do_left:
                            tt1 = s.add("vector", lambda v, h=h: v.tensor_add(
                                tsum[:, :], P[6][0:65, :], O_acc[:, h, :]), [t_pv, t_oacc])
                            tcp = s.add("vector", lambda v: v.tensor_copy(
                                rrowf[0:1, :], tsum[64:65, :]), [tt1])
                            fm = None
                            for c4 in range(4):
                                fm = s.add("tensor", lambda t, c4=c4, st=(c4 == 0): t.matmul(
                                    P[4][:, 0:4],
                                    rrowf[0:1, 128 * c4:128 * (c4 + 1)],
                                    sb["e4"][0:1, 4 * c4:4 * c4 + 4],
                                    start=st, stop=(c4 == 3)),
                                    [tcp, self.t_cpr, t_tke_last] if c4 == 0 else [])
                            flr = s.add("vector", lambda v: v.tensor_scalar(
                                P[4][:, 0:4], P[4][:, 0:4], 1e-30, None, ALU.max), [fm])
                            rcp = s.add("vector", lambda v: v.reciprocal(
                                r4sb[:, :], P[4][:, 0:4]), [flr])
                            bt = None
                            for c4 in range(4):
                                bt = s.add("tensor", lambda t, c4=c4: t.transpose(
                                    P[4][0:1, 128 * c4:128 * (c4 + 1)],
                                    r4sb[:, c4:c4 + 1], sb["identF"][:, :]),
                                    [rcp] if c4 == 0 else [])
                            cpr = s.add("vector", lambda v: v.tensor_copy(
                                rrowb[0:1, :], P[4][0:1, 0:512]), [bt])
                            self.t_cpr = cpr
                            trp = s.add("tensor", lambda t: t.matmul(
                                P[2][0:64, 0:512], sb["ones_row"][0:1, 0:64], rrowb[0:1, :],
                                start=True, stop=True), [cpr, t_tt1, t_uq, t_uk])
                            t_rep = trp
                            to2 = s.add("vector", lambda v, h=h, B=B_left: v.tensor_tensor(
                                o_sb[0][64 * (h % 2):64 * (h % 2) + 64, h // 2, :],
                                tsum[0:64, :], P[2][0:64, :], ALU.mult), [trp, t_wo2])
                            t_tt1 = to2
                        if do_right:
                            t_oacc = s.add("vector", lambda v, h=h: v.tensor_copy(
                                O_acc[:, h, :], P[7][0:65, :]),
                                [t_pv] + ([t_tt1] if do_left else []))
                        else:
                            t_oacc = s.add("vector", lambda v, h=h: v.memset(
                                O_acc[:, h, :], 0.0), [t_tt1])
                    if False:
                        B = B_left
                        bcol = qcol0 + 512 * B
                        mm = None
                        for c in range(NCH):
                            wbank = P[6] if c % 2 == 0 else P[7]
                            for fin in range(4):
                                deps = []
                                if c == 0 and fin == 0:
                                    deps = [t_tt1, t_w, t_oacc]
                                if fin == 0:
                                    deps = deps + [t_wres]
                                mm = s.add("tensor", lambda t, c=c, fin=fin, wbank=wbank, B=B, st=(fin == 0): t.matmul(
                                    wbank[:, :], wo[:, fin, 128 * c:128 * (c + 1)],
                                    o_sb[0][:, fin, :],
                                    start=st, stop=(fin == 3)), deps)
                            t_wres = s.add("vector", lambda v, c=c, wbank=wbank, bcol=bcol: v.tensor_add(
                                sb["X"][:, c, bcol:bcol + 512],
                                sb["X"][:, c, bcol:bcol + 512], wbank[:, :]), [mm])
                        if bcol <= HALO - 1 < bcol + 512:
                            t_wres = s.add("vector", lambda v: v.tensor_mul(
                                sb["X"][:, :, HALO - 1:HALO],
                                sb["X"][:, :, HALO - 1:HALO],
                                bass.AP(sb["halo01"], 0, [[1, 128], [0, NCH], [0, 1]])),
                                [t_wres])
                            self.t_halofix = t_wres
                        t_wo2 = t_wres
                        t_wo = t_wres
                        if l == 0 and B == 2 and DBG_TAP:
                            dsemo = s.new_dma_sem("dbgo", group=True)
                            t_wo2 = s.dma("gpsimd", self.dbg_o[:, :, :], o_sb[0][:, :, :], dsemo,
                                          deps=[t_tt1])

        if do_mlp:
            self._mlp(l)

    def _mlp(self, l):
        s, nc, sb, inp = self.s, self.nc, self.sb, self.inp
        gate = self.gate()
        if l == 0:
            mc0, ntok = 512, 4736
        else:
            mc0, ntok = 1152, 4096
        megas = []
        a = 0
        while a < ntok:
            megas.append((mc0 + a, min(MEGA, ntok - a)))
            a += min(MEGA, ntok - a)

        with ExitStack() as es:
            h_buf = self.sbuf(f"hbuf{l}", (128, 22, MEGA), BF16, es)
            xt = self.sbuf(f"xt{l}", (128, NCH, MEGA), BF16, es)
            w1a = [self.sbuf(f"w1a{l}_{i}", (128, NCH, 128), BF16, es) for i in range(2)]
            w1g = [self.sbuf(f"w1g{l}_{i}", (128, NCH, 128), BF16, es) for i in range(2)]
            w2b = [self.sbuf(f"w2b{l}_{i}", (128, HID), BF16, es) for i in range(3)]
            gout = self.sbuf(f"gout{l}", (128, MEGA), BF16, es)
            c12 = self.sbuf(f"c12_{l}", (2, H2), BF16, es)
            c2row = self.sbuf(f"c2row{l}", (1, H2), BF16, es)
            misc2 = self.sbuf(f"misc2_{l}", (2, MEGA), BF16, es)
            xb = [self.sbuf(f"xb{l}_{i}", (128, NCH, 1), BF16, es) for i in range(2)]
            mrow_ = self.sbuf(f"mrow{l}", (1, MEGA), F32, es)
            srow = self.sbuf(f"srow{l}", (1, MEGA), F32, es)
            srowb = self.sbuf(f"srowb{l}", (1, MEGA), BF16, es)
            P = [self.psum(f"M{l}b{i}", (128, 512), F32, es) for i in range(8)]

            dw1a = [s.new_dma_sem(f"m{l}w1a{i}") for i in range(2)]
            dw1g = [s.new_dma_sem(f"m{l}w1g{i}") for i in range(2)]
            dw2 = [s.new_dma_sem(f"m{l}w2{i}") for i in range(3)]
            dmisc = s.new_dma_sem(f"mm{l}", group=True)
            t_b1 = s.dma("gpsimd", c2row[0:1, :], inp[f"b1row{l}"][:, :], dmisc, deps=gate)
            t_m2 = s.dma("gpsimd", misc2[:, :], inp["ones2row"][:, :], dmisc)
            w1hb = self.w1bf[l].rearrange("(c p) o -> p c o", p=128)

            t_xb = s.add("vector", lambda v: v.tensor_copy(
                xb[0][:, :, :], sb["X"][:, :, mc0 - 1:mc0]),
                gate + [getattr(self, "t_halofix", None)])
            t_ev = None
            t_h = None
            t_hP = [None, None]
            t_gel = None
            t_gelP = [None, None]
            t_stg2 = None
            t_c12done = None
            t_w1free = [None, None]
            t_w2free = [None, None, None]
            t_evc = [None] * NCH
            t_repread = None
            t_sqw = None
            t_smm = None
            t_c2dma = None

            for mi, (m0, Nm) in enumerate(megas):
                nmb = (Nm + 511) // 512
                if mi + 1 < len(megas):
                    t_xb2 = s.add("vector", lambda v, m0=m0, Nm=Nm, mi=mi: v.tensor_copy(
                        xb[(mi + 1) % 2][:, :, :], sb["X"][:, :, m0 + Nm - 1:m0 + Nm]),
                        [t_ev])
                t_sq1 = s.add("scalar", lambda sc, m0=m0, Nm=Nm: sc.activation(
                    xt[:, 0:4, 0:Nm], sb["X"][:, 0:4, m0:m0 + Nm], AF.Square),
                    [t_smm, t_ev, t_repread, getattr(self, "t_halofix", None)])
                t_sq2 = s.add("scalar", lambda sc, m0=m0, Nm=Nm: sc.activation(
                    xt[:, 4:8, 1:Nm], sb["X"][:, 4:8, m0:m0 + Nm - 1], AF.Square), [t_sq1])
                t_sqw = s.add("scalar", lambda sc, mi=mi: sc.activation(
                    xt[:, 4:8, 0:1], xb[mi % 2][:, 4:8, :], AF.Square), [t_sq2, t_xb])
                mm = None
                for c in range(NCH):
                    last_c = (c == NCH - 1)
                    for hb in range(nmb):
                        lo = 512 * hb
                        hi = min(Nm, lo + 512)
                        if c < 4:
                            mm = s.add("tensor", lambda t, c=c, hb=hb, lo=lo, hi=hi, m0=m0, st=(c == 0): t.matmul(
                                P[4 + hb][0:1, 0:hi - lo], sb["ones_col"][:, :],
                                sb["X"][:, c, m0 + lo:m0 + hi],
                                start=st, stop=False),
                                ([t_stg2, t_ev, getattr(self, "t_halofix", None)]
                                 + (gate if mi == 0 else [])) if (c == 0 and hb == 0) else [])
                        else:
                            if hb == 0:
                                mm = s.add("tensor", lambda t, c=c, mi=mi: t.matmul(
                                    P[4][0:1, 0:1], sb["ones_col"][:, :],
                                    xb[mi % 2][:, c, :],
                                    start=False, stop=False), [t_xb])
                                mm = s.add("tensor", lambda t, c=c, hi=hi, m0=m0, sp=(c == NCH - 1): t.matmul(
                                    P[4][0:1, 1:hi], sb["ones_col"][:, :],
                                    sb["X"][:, c, m0:m0 + hi - 1],
                                    start=False, stop=sp), [])
                            else:
                                mm = s.add("tensor", lambda t, c=c, lo=lo, hi=hi, m0=m0, sp=(c == NCH - 1): t.matmul(
                                    P[4 + hb][0:1, 0:hi - lo], sb["ones_col"][:, :],
                                    sb["X"][:, c, m0 + lo - 1:m0 + hi - 1],
                                    start=False, stop=sp), [])
                for c in range(NCH):
                    for hb in range(nmb):
                        lo = 512 * hb
                        hi = min(Nm, lo + 512)
                        mm = s.add("tensor", lambda t, c=c, hb=hb, lo=lo, hi=hi, st=(c == 0), sp=(c == NCH - 1): t.matmul(
                            P[6 + hb][0:1, 0:hi - lo], sb["ones_col"][:, :],
                            xt[:, c, lo:hi], start=st, stop=sp),
                            [t_sqw] if c == 0 and hb == 0 else [])
                t_smm = mm
                r1 = s.add("vector", lambda v, Nm=Nm: v.tensor_scalar(
                    mrow_[0:1, 0:min(Nm, 512)], P[4][0:1, 0:min(Nm, 512)],
                    1.0 / HID, None, ALU.mult), [t_smm])
                if Nm > 512:
                    r1 = s.add("vector", lambda v, Nm=Nm: v.tensor_scalar(
                        mrow_[0:1, 512:Nm], P[5][0:1, 0:Nm - 512],
                        1.0 / HID, None, ALU.mult), [r1])
                r2 = s.add("vector", lambda v, Nm=Nm: v.tensor_scalar(
                    srow[0:1, 0:min(Nm, 512)], P[6][0:1, 0:min(Nm, 512)],
                    1.0 / HID, None, ALU.mult), [r1])
                if Nm > 512:
                    r2 = s.add("vector", lambda v, Nm=Nm: v.tensor_scalar(
                        srow[0:1, 512:Nm], P[7][0:1, 0:Nm - 512],
                        1.0 / HID, None, ALU.mult), [r2])
                r3 = r2
                for hb in range(nmb):
                    lo = 512 * hb
                    hi = min(Nm, lo + 512)
                    r3 = s.add("vector", lambda v, hb=hb, lo=lo, hi=hi: v.tensor_mul(
                        P[4 + hb][0:1, 0:hi - lo], mrow_[0:1, lo:hi], mrow_[0:1, lo:hi]), [r3])
                    r3 = s.add("vector", lambda v, hb=hb, lo=lo, hi=hi: v.tensor_sub(
                        srow[0:1, lo:hi], srow[0:1, lo:hi], P[4 + hb][0:1, 0:hi - lo]), [r3])
                r5 = s.add("scalar", lambda sc, Nm=Nm: sc.activation(
                    srow[0:1, 0:Nm], srow[0:1, 0:Nm], AF.Sqrt,
                    bias=sb["eps"][0:1, 0:1]), [r3])
                r6 = s.add("vector", lambda v, Nm=Nm: v.reciprocal(
                    srow[0:1, 0:Nm], srow[0:1, 0:Nm]), [r5])
                r7 = r6
                for hb in range(nmb):
                    lo = 512 * hb
                    hi = min(Nm, lo + 512)
                    r7 = s.add("vector", lambda v, hb=hb, lo=lo, hi=hi: v.tensor_mul(
                        P[4 + hb][0:1, 0:hi - lo], mrow_[0:1, lo:hi], srow[0:1, lo:hi]), [r7])
                c1_ = s.add("vector", lambda v, Nm=Nm: v.tensor_copy(
                    srowb[0:1, 0:Nm], srow[0:1, 0:Nm]), [r7])
                c2_ = c1_
                for hb in range(nmb):
                    lo = 512 * hb
                    hi = min(Nm, lo + 512)
                    c2_ = s.add("vector", lambda v, hb=hb, lo=lo, hi=hi: v.tensor_scalar(
                        misc2[0:1, lo:hi], P[4 + hb][0:1, 0:hi - lo], -1.0, None,
                        ALU.mult), [c2_, t_m2])
                rp1 = s.add("tensor", lambda t, Nm=Nm: t.matmul(
                    P[0][:, 0:min(Nm, 512)], sb["ones_row"][0:1, 0:128],
                    srowb[0:1, 0:min(Nm, 512)], start=True, stop=True),
                    [c1_, t_h, t_smm])
                rp2 = None
                if Nm > 512:
                    rp2 = s.add("tensor", lambda t, Nm=Nm: t.matmul(
                        P[1][:, 0:Nm - 512], sb["ones_row"][0:1, 0:128],
                        srowb[0:1, 512:Nm], start=True, stop=True), [c1_, t_h])
                x1_ = s.add("vector", lambda v, m0=m0, Nm=Nm: v.tensor_tensor(
                    xt[:, 0:4, 0:min(Nm, 512)], sb["X"][:, 0:4, m0:m0 + min(Nm, 512)],
                    bass.AP(P[0], 0, [[512, 128], [0, 4], [1, min(Nm, 512)]]),
                    ALU.mult), [rp1, t_sqw, t_smm])
                if Nm > 512:
                    x1_ = s.add("vector", lambda v, m0=m0, Nm=Nm: v.tensor_tensor(
                        xt[:, 0:4, 512:Nm], sb["X"][:, 0:4, m0 + 512:m0 + Nm],
                        bass.AP(P[1], 0, [[512, 128], [0, 4], [1, Nm - 512]]),
                        ALU.mult), [rp2, x1_])
                x2_ = s.add("vector", lambda v, m0=m0, Nm=Nm: v.tensor_tensor(
                    xt[:, 4:8, 1:min(Nm, 512)], sb["X"][:, 4:8, m0:m0 + min(Nm, 512) - 1],
                    bass.AP(P[0], 1, [[512, 128], [0, 4], [1, min(Nm, 512) - 1]]),
                    ALU.mult), [x1_])
                if Nm > 512:
                    x2_ = s.add("vector", lambda v, m0=m0, Nm=Nm: v.tensor_tensor(
                        xt[:, 4:8, 512:Nm], sb["X"][:, 4:8, m0 + 511:m0 + Nm - 1],
                        bass.AP(P[1], 0, [[512, 128], [0, 4], [1, Nm - 512]]),
                        ALU.mult), [x2_])
                x3_ = s.add("vector", lambda v, mi=mi: v.tensor_tensor(
                    xt[:, 4:8, 0:1], xb[mi % 2][:, 4:8, :],
                    bass.AP(P[0], 0, [[512, 128], [0, 4], [1, 1]]),
                    ALU.mult), [x2_, t_xb])
                t_xt = s.add("vector", lambda v, Nm=Nm: v.tensor_tensor(
                    xt[:, :, 0:Nm], xt[:, :, 0:Nm],
                    bass.AP(sb[f"lg_col{l}"], 0, [[NCH, 128], [1, NCH], [0, Nm]]),
                    ALU.mult), [x3_])
                if mi + 1 < len(megas):
                    t_xb = t_xb2

                for j in range(22):
                    sl = j % 2
                    la = s.dma("sync", w1a[sl][:, :, :], w1hb[:, :, 128 * j:128 * (j + 1)],
                               dw1a[sl], deps=[t_w1free[sl], self.t_wcast]
                               + (gate if (mi == 0 and j < 2) else []))
                    lg = s.dma("scalar", w1g[sl][:, :, :],
                               w1hb[:, :, HPAD + 128 * j:HPAD + 128 * (j + 1)],
                               dw1g[sl], deps=[t_w1free[sl], self.t_wcast])
                    if mi == 0:
                        for (wt, coff) in ((w1a[sl], 128 * j), (w1g[sl], HPAD + 128 * j)):
                            for fin in range(NCH):
                                mm = s.add("tensor", lambda t, wt=wt, fin=fin, st=(fin == 0), sp=(fin == NCH - 1): t.matmul(
                                    P[6][0:1, 0:128], sb[f"lg_col{l}"][:, fin:fin + 1],
                                    wt[:, fin, :], start=st, stop=sp),
                                    [la, lg, t_gel, t_xt, t_c12done] if fin == 0 else [])
                            for fin in range(NCH):
                                mm = s.add("tensor", lambda t, wt=wt, fin=fin, st=(fin == 0), sp=(fin == NCH - 1): t.matmul(
                                    P[7][0:1, 0:128], sb[f"lb_col{l}"][:, fin:fin + 1],
                                    wt[:, fin, :], start=st, stop=sp), [])
                            ec1 = s.add("vector", lambda v, coff=coff: v.tensor_copy(
                                c12[0:1, coff:coff + 128], P[6][0:1, 0:128]), [mm])
                            ec2 = s.add("vector", lambda v, coff=coff: v.tensor_add(
                                c2row[0:1, coff:coff + 128], c2row[0:1, coff:coff + 128],
                                P[7][0:1, 128:256] if False else P[7][0:1, 0:128]), [ec1, t_b1])
                            t_c12done = ec2
                    jb = 4 * (j % 2)
                    for (banks, wt, dep0) in (((P[jb + 0], P[jb + 1]), w1a[sl],
                                               [la, t_xt, t_hP[j % 2], t_stg2]),
                                              ((P[jb + 2], P[jb + 3]), w1g[sl],
                                               [lg, t_gelP[j % 2]])):
                        for half in range(2 if Nm > 512 else 1):
                            nn = min(Nm - 512 * half, 512)
                            for fin in range(NCH):
                                deps = list(dep0) + ([t_c12done] if mi == 0 else []) if (half == 0 and fin == 0) else []
                                mm = s.add("tensor", lambda t, banks=banks, wt=wt, half=half, fin=fin, nn=nn, st=(fin == 0): t.matmul(
                                    banks[half][:, 0:nn], wt[:, fin, :],
                                    xt[:, fin, 512 * half:512 * half + nn],
                                    start=st, stop=False), deps)
                            if mi == 0:
                                coff = (128 * j) if wt is w1a[sl] else (HPAD + 128 * j)
                                mm = s.add("tensor", lambda t, banks=banks, half=half, nn=nn, coff=coff: t.matmul(
                                    banks[half][:, 0:nn], c12[0:1, coff:coff + 128],
                                    misc2[0:1, 512 * half:512 * half + nn],
                                    start=False, stop=False), [t_c12done, c2_])
                                mm = s.add("tensor", lambda t, banks=banks, half=half, nn=nn, coff=coff: t.matmul(
                                    banks[half][:, 0:nn], c2row[0:1, coff:coff + 128],
                                    sb["ones_row"][0:1, 0:nn],
                                    start=False, stop=True), [t_c12done])
                            else:
                                coff = (128 * j) if wt is w1a[sl] else (HPAD + 128 * j)
                                mm = s.add("tensor", lambda t, banks=banks, half=half, nn=nn, coff=coff: t.matmul(
                                    banks[half][:, 0:nn], c12[0:2, coff:coff + 128],
                                    misc2[0:2, 512 * half:512 * half + nn],
                                    start=False, stop=True), [t_c2dma, c2_])
                    t_ag = mm
                    t_w1free[sl] = t_ag
                    tg = None
                    for half in range(2 if Nm > 512 else 1):
                        nn = min(Nm - 512 * half, 512)
                        tg = s.add("scalar", lambda sc, jb=jb, half=half, nn=nn: sc.activation(
                            gout[:, 512 * half:512 * half + nn],
                            P[jb + 2 + half][:, 0:nn], AF.Gelu), [t_ag, t_h] if half == 0 else [tg])
                    t_gel = tg
                    t_gelP[j % 2] = tg
                    th = None
                    for half in range(2 if Nm > 512 else 1):
                        nn = min(Nm - 512 * half, 512)
                        th = s.add("vector", lambda v, j=j, jb=jb, half=half, nn=nn: v.scalar_tensor_tensor(
                            h_buf[:, j, 512 * half:512 * half + nn],
                            P[jb + half][:, 0:nn], 0.0, gout[:, 512 * half:512 * half + nn],
                            ALU.bypass, ALU.mult), [t_gel] if half == 0 else [th])
                    t_h = th
                    t_hP[j % 2] = th
                if mi == 0:
                    t_c2dma = s.dma("sync", c12[1:2, :], c2row[0:1, :],
                                    s.new_dma_sem(f"c2d{l}", group=True), deps=[t_c12done])

                for half in range(2 if Nm > 512 else 1):
                    nn = min(Nm - 512 * half, 512)
                    for j in range(22):
                        sl = j % 3
                        l2 = s.dma(("sync", "scalar", "gpsimd")[sl],
                                   w2b[sl][:, :],
                                   self.w2bf[l][128 * j:128 * (j + 1), :], dw2[sl],
                                   deps=[t_w2free[sl], self.t_wcast]
                                   + (gate if (mi == 0 and half == 0 and j < 3) else []))
                        for c in range(NCH):
                            deps = []
                            if c == 0 and j == 0:
                                deps = [l2, t_h]
                            elif c == 0:
                                deps = [l2]
                            if j == 0:
                                deps = deps + [t_evc[c]]
                            mm = s.add("tensor", lambda t, c=c, j=j, sl=sl, half=half, nn=nn, st=(j == 0), sp=(j == 21): t.matmul(
                                P[c][:, 0:nn], w2b[sl][:, 128 * c:128 * (c + 1)],
                                h_buf[:, j, 512 * half:512 * half + nn],
                                start=st, stop=sp), deps)
                        t_w2free[sl] = mm
                    t_stg2 = mm
                    ev = None
                    for c in range(NCH):
                        ev = s.add("vector", lambda v, c=c, half=half, nn=nn, m0=m0: v.scalar_tensor_tensor(
                            sb["X"][:, c, m0 + 512 * half:m0 + 512 * half + nn],
                            P[c][:, 0:nn], sb[f"b2_col{l}"][:, c:c + 1],
                            sb["X"][:, c, m0 + 512 * half:m0 + 512 * half + nn],
                            ALU.add, ALU.add), [t_stg2] if c == 0 else [ev])
                        t_evc[c] = ev
                    if m0 + 512 * half <= HALO - 1 < m0 + 512 * half + nn:
                        ev = s.add("vector", lambda v: v.tensor_mul(
                            sb["X"][:, :, HALO - 1:HALO], sb["X"][:, :, HALO - 1:HALO],
                            bass.AP(sb["halo01"], 0, [[1, 128], [0, NCH], [0, 1]])), [ev])
                        self.t_halofix = ev
                    t_ev = ev
                t_repread = t_ev

    def build_final(self):
        s, nc, sb, inp = self.s, self.nc, self.sb, self.inp
        gate = self.gate()
        dout = [s.new_dma_sem(f"out{i}") for i in range(2)]
        dx1r = [s.new_dma_sem(f"x1r{i}") for i in range(2)]
        with ExitStack() as es:
            x1t = [self.sbuf(f"x1t{i}", (128, NCH, 128), BF16, es) for i in range(2)]
            tsb = self.sbuf("fin_t", (128, NCH, 128), F32, es)
            tsbb = self.sbuf("fin_tb", (128, NCH, 128), BF16, es)
            tq = self.sbuf("fin_q", (128, NCH, 128), BF16, es)
            of = [self.sbuf(f"fin_o{i}", (128, NCH, 128), F32, es) for i in range(2)]
            frow = [self.sbuf(f"frow{i}", (1, 128), F32, es) for i in range(4)]
            onesf = self.sbuf("onesf_s", (1, 128), F32, es)
            P = [self.psum(f"F{i}", (128, 512), F32, es) for i in range(4)]

            t_of = s.add("gpsimd", lambda gg: gg.memset(onesf[:, :], 1.0), gate)
            t_add = None
            t_sq = None
            t_norm = [None, None]
            t_store = [None, None]
            t_rows = None
            t_rep = None
            for ti in range(32):
                col = HALO + 128 * ti
                sl = ti % 2
                ld = s.dma("sync", x1t[sl][:, :, :], self.x1b[:, :, 128 * ti:128 * (ti + 1)],
                           dx1r[sl], deps=[t_norm[sl]] + (gate if ti < 2 else []))
                ta = s.add("vector", lambda v, col=col, sl=sl: v.tensor_add(
                    tsb[:, :, :], sb["X"][:, :, col:col + 128], x1t[sl][:, :, :]),
                    [ld, t_sq] + (gate if ti == 0 else []))
                tcb = s.add("vector", lambda v: v.tensor_copy(
                    tsbb[:, :, :], tsb[:, :, :]), [ta])
                t_sq = s.add("scalar", lambda sc: sc.activation(
                    tq[:, :, :], tsb[:, :, :], AF.Square), [ta, t_rows])
                mm = None
                for c in range(NCH):
                    mm = s.add("tensor", lambda t, c=c, st=(c == 0), sp=(c == NCH - 1): t.matmul(
                        P[0][0:1, 0:128], sb["ones_col"][:, :], tsbb[:, c, :],
                        start=st, stop=sp), [tcb, t_rows] if c == 0 else [])
                for c in range(NCH):
                    mm = s.add("tensor", lambda t, c=c, st=(c == 0), sp=(c == NCH - 1): t.matmul(
                        P[1][0:1, 0:128], sb["ones_col"][:, :], tq[:, c, :],
                        start=st, stop=sp), [t_sq] if c == 0 else [])
                r1 = s.add("vector", lambda v: v.tensor_scalar(
                    frow[0][0:1, :], P[0][0:1, 0:128], 1.0 / HID, None, ALU.mult), [mm])
                r2 = s.add("vector", lambda v: v.tensor_scalar(
                    frow[1][0:1, :], P[1][0:1, 0:128], 1.0 / HID, None, ALU.mult), [r1])
                r3 = s.add("vector", lambda v: v.tensor_mul(
                    frow[2][0:1, :], frow[0][0:1, :], frow[0][0:1, :]), [r2])
                r4 = s.add("vector", lambda v: v.tensor_sub(
                    frow[1][0:1, :], frow[1][0:1, :], frow[2][0:1, :]), [r3])
                r5 = s.add("scalar", lambda sc: sc.activation(
                    frow[1][0:1, :], frow[1][0:1, :], AF.Sqrt, bias=sb["eps"][0:1, 0:1]), [r4])
                r6 = s.add("vector", lambda v: v.reciprocal(
                    frow[1][0:1, :], frow[1][0:1, :]), [r5])
                r7 = s.add("vector", lambda v: v.tensor_mul(
                    frow[2][0:1, :], frow[0][0:1, :], frow[1][0:1, :]), [r6])
                t_rows = r7
                m1 = s.add("tensor", lambda t: t.matmul(
                    P[2][:, 0:128], onesf[0:1, 0:128], frow[1][0:1, :],
                    start=True, stop=True), [r6, t_rep, t_of])
                m2 = s.add("tensor", lambda t: t.matmul(
                    P[3][:, 0:128], onesf[0:1, 0:128], frow[2][0:1, :],
                    start=True, stop=True), [r7])
                t_rep = m2
                n1 = s.add("vector", lambda v, sl=sl: v.tensor_tensor(
                    of[sl][:, :, :], tsb[:, :, :],
                    bass.AP(P[2], 0, [[512, 128], [0, NCH], [1, 128]]), ALU.mult),
                    [m1, t_store[sl]])
                t_norm[sl] = s.add("vector", lambda v, sl=sl: v.tensor_sub(
                    of[sl][:, :, :],
                    of[sl][:, :, :],
                    bass.AP(P[3], 0, [[512, 128], [0, NCH], [1, 128]])), [n1, m2])
                t_rep = m2
                t_store[sl] = s.dma(
                    "sync", self.outT.rearrange("(c p) t -> p c t", p=128)[:, :, 128 * ti:128 * (ti + 1)],
                    of[sl][:, :, :], dout[sl], deps=[t_norm[sl]])
            def fin(sy):
                sy.wait_ge(dout[0].sem, dout[0].count)
                sy.wait_ge(dout[1].sem, dout[1].count)
                return None
            s.add("sync", fin)


_LAST_SCHED = None


def build_program(upto="all", mlp0=True, mlp1=True):
    global _LAST_SCHED
    nc = bass.Bass(trn_type="TRN2", target_bir_lowering=False)
    ctx = ExitStack()
    kb = KB(nc, ctx)
    _LAST_SCHED = kb.s
    with ctx:
        kb.build_io()
        kb.build_consts()
        with ExitStack() as es_kv:
            kb.es_kv = es_kv
            if upto != "consts":
                kb.build_p1()
            if upto not in ("consts", "p1"):
                kb.build_p2()
        if upto in ("L0", "L1", "all"):
            kb.build_layer(0, do_mlp=mlp0)
        if upto in ("L1", "all"):
            kb.build_layer(1, do_mlp=mlp1)
        if upto == "all":
            kb.build_final()
        else:
            gate = kb.gate()
            dsem = kb.s.new_dma_sem("dbgout", group=True)
            t = kb.s.dma("gpsimd", kb.outT.rearrange("(c p) t -> p c t", p=128)[:, :, 0:128],
                         kb.sb["X"][:, :, HALO:HALO + 128], dsem, deps=gate)
            def fin(sy):
                sy.wait_ge(dsem.sem, dsem.count)
                return None
            kb.s.add("sync", fin)
        kb.s.setup_sems(ctx)
        kb.s.emit()
    return nc


def kernel(**inputs):
    nc = build_program()
    in_maps = [make_core_inputs(inputs, c) for c in range(8)]
    res = run_bass_kernel_spmd(nc, in_maps, core_ids=list(range(8)))
    out = np.empty((4, 8192, 1024), np.float32)
    for c in range(8):
        b, h = c // 2, c % 2
        out[b, 4096 * h:4096 * (h + 1), :] = res.results[c]["outT"].T
    return out
